# version 07
# baseline (speedup 1.0000x reference)
import contextlib
import numpy as np
import concourse.bass as bass
import concourse.mybir as mybir
from concourse.bass_utils import run_bass_kernel_spmd

F32 = mybir.dt.float32
BF16 = mybir.dt.bfloat16
AF = mybir.ActivationFunctionType
ALU = mybir.AluOpType
AX = mybir.AxisListType

H = 1024
T = 4096
HP = H // 128
NEG_BIG = -1.0e30

_UID = [0]


def uname(s):
    _UID[0] += 1
    return f"{s}{_UID[0]}"


class Ctr:
    def __init__(self, h):
        self.h = h
        self.v = 0


class Pipe:

    def __init__(self, ctrs):
        self.c = ctrs
        self.k = len(ctrs)

    def issue(self, eng, dst, src, group, accum=False):
        c = self.c[group % self.k]
        if accum:
            eng.dma_start(dst, src, accum_op=ALU.add).then_inc(c.h, 16)
        else:
            eng.dma_start(dst, src).then_inc(c.h, 16)
        c.v += 16
        return (group % self.k, c.v)

    def wait(self, eng, tok):
        p, v = tok
        eng.wait_ge(self.c[p].h, v)

    def wait_all(self, eng):
        for c in self.c:
            eng.wait_ge(c.h, c.v)

    def bases(self):
        return [c.v for c in self.c]


class CacheJobs:

    def __init__(self, st):
        self.base = st["cs"].bases()
        self.items = []


def _wload(st, gp, pipe, dst_ap, key, f32_ap, bf_ap, group, jobs,
           ready=None):
    cs = st["cs"]
    if key in st["cached"]:
        t0 = st["cstok"].get(key)
        if t0 is not None:
            gp.wait_ge(cs.c[t0[0]].h, t0[1])
        return pipe.issue(gp, dst_ap, bf_ap, group), None
    st["cached"].add(key)
    tok = pipe.issue(gp, dst_ap, f32_ap, group)
    j = len(jobs.items)
    ptok = (j % 2, jobs.base[j % 2] + (j // 2 + 1) * 16)
    jobs.items.append((tok, pipe, dst_ap, bf_ap, ptok, ready))
    st["cstok"][key] = ptok
    return tok, ptok


def _emit_cache_stores(st, sy, jobs):
    cs = st["cs"]
    for j, (tok, pipe, sb_ap, bf_ap, ptok, ready) in enumerate(jobs.items):
        if j >= 2:
            sy.wait_ge(cs.c[j % 2].h, jobs.items[j - 2][4][1])
        if ready is not None:
            sy.wait_ge(ready[0], ready[1])
        else:
            pipe.wait(sy, tok)
        t = cs.issue(sy, bf_ap, sb_ap, j)
        assert t == ptok, (t, ptok)
    if jobs.items:
        cs.wait_all(sy)
    jobs.items = []


def _chain(ctr, eng, instr):
    instr.then_inc(ctr.h, 1)
    ctr.v += 1


class ChainCtx:

    def __init__(self, ctr):
        self.ctr = ctr

    def w(self, eng):
        if self.ctr.v:
            eng.wait_ge(self.ctr.h, self.ctr.v)

    def i(self, instr):
        instr.then_inc(self.ctr.h, 1)
        self.ctr.v += 1




def build_nc(parts, n_tok=T):
    _UID[0] = 0
    nc = bass.Bass("TRN2", target_bir_lowering=False)

    x_in = nc.dram_tensor("x", [n_tok, H], F32, kind="ExternalInput")
    w = {}
    for name, shape in [
        ("L0_attn_Wq", [H, H]), ("L0_attn_bq", [H]),
        ("L0_attn_Wk", [H, H]), ("L0_attn_bk", [H]),
        ("L0_attn_Wv", [H, H]), ("L0_attn_bv", [H]),
        ("L0_attn_Wo", [H, H]), ("L0_attn_bo", [H]),
        ("L0_ffn_gate_w", [H, 8]),
        ("L0_ffn_w1", [8, H, 4096]),
        ("L0_ffn_w2", [8, 4096, H]),
        ("L1_attn_Wq", [H, H]), ("L1_attn_bq", [H]),
        ("L1_attn_Wk", [H, H]), ("L1_attn_bk", [H]),
        ("L1_attn_Wv", [H, H]), ("L1_attn_bv", [H]),
        ("L1_attn_Wo", [H, H]), ("L1_attn_bo", [H]),
        ("L1_ffn_Wq", [H, H]),
        ("L1_ffn_Wk_c", [H, 256]), ("L1_ffn_Wv_c", [H, 256]),
        ("L1_ffn_Wk", [256, H]), ("L1_ffn_Wv", [256, H]),
        ("L1_ffn_Wo", [H, H]),
        ("L1_ffn_W_shared_in", [H, 4096]),
        ("L1_ffn_W_shared_out", [2048, H]),
        ("L1_ffn_W1_shared", [H, 4096]),
        ("L1_ffn_W2_experts", [8, 2048, H]),
        ("L1_ffn_Wg_gate", [H, 2]), ("L1_ffn_We_gate", [H, 8]),
        ("L1_ffn_group_bias", [2]), ("L1_ffn_expert_bias", [8]),
        ("L1_ffn_norm1_w", [H]), ("L1_ffn_norm2_w", [H]),
    ]:
        w[name] = nc.dram_tensor(name, shape, F32, kind="ExternalInput")
    wbf = {}
    for name in ("L0_ffn_w1", "L0_ffn_w2", "L1_ffn_Wq", "L1_ffn_Wk_c",
                 "L1_ffn_Wv_c", "L1_ffn_Wk", "L1_ffn_Wv", "L1_ffn_Wo",
                 "L1_ffn_W_shared_in", "L1_ffn_W_shared_out",
                 "L1_ffn_W1_shared", "L1_ffn_W2_experts"):
        wbf[name] = nc.dram_tensor("c_" + name, w[name].shape, BF16)
    eye_bf_in = nc.dram_tensor("eye_bf", [128, 128], BF16, kind="ExternalInput")
    obd_in = nc.dram_tensor("obd_c", [128, 128], F32, kind="ExternalInput")
    eye_f_in = nc.dram_tensor("eye_f", [128, 128], F32, kind="ExternalInput")
    rope_sin_in = nc.dram_tensor("rope_sin", [T, 512], F32, kind="ExternalInput")
    rope_cos_in = nc.dram_tensor("rope_cos", [T, 512], F32, kind="ExternalInput")

    out = nc.dram_tensor("out", [n_tok, H], F32, kind="ExternalOutput")

    xres = nc.dram_tensor("xres", [H, n_tok], F32)
    qt_hbm = nc.dram_tensor("qt_hbm", [H, T], BF16)
    kt_hbm = nc.dram_tensor("kt_hbm", [H, T], BF16)
    v_hbm = nc.dram_tensor("v_hbm", [T, 1088], BF16)
    kt_bd = nc.dram_tensor("kt_bd", [8, n_tok // 64, 128, 128], BF16)
    v_bd = nc.dram_tensor("v_bd", [8, n_tok // 64, 128, 128], BF16)
    x1f_hbm = nc.dram_tensor("x1f_hbm", [H, T], F32)
    x1b_hbm = nc.dram_tensor("x1b_hbm", [H, T], BF16)
    xnf_hbm = nc.dram_tensor("xnf_hbm", [H, T], F32)
    acc_hbm = nc.dram_tensor("acc_hbm", [H, T], F32)

    st = {"nc": nc, "w": w, "xres": xres, "out": out, "x_in": x_in,
          "qt_hbm": qt_hbm, "kt_hbm": kt_hbm, "v_hbm": v_hbm,
          "kt_bd": kt_bd, "v_bd": v_bd,
          "x1f_hbm": x1f_hbm, "x1b_hbm": x1b_hbm, "xnf_hbm": xnf_hbm,
          "acc_hbm": acc_hbm,
          "rope_sin": rope_sin_in, "rope_cos": rope_cos_in,
          "obd_c": obd_in,
          "n_tok": n_tok}

    with (
        contextlib.ExitStack() as _stack,
    ):
        def _cm(x):
            return _stack.enter_context(x)
        eye_bf = _cm(nc.sbuf_tensor("eye_bf_sb", [128, 128], BF16))
        eye_f = _cm(nc.sbuf_tensor("eye_f_sb", [128, 128], F32))
        ones_f = _cm(nc.sbuf_tensor("ones_f_sb", [128, 128], F32))
        ones_bf = _cm(nc.sbuf_tensor("ones_bf_sb", [128, 128], BF16))
        zbuf = _cm(nc.sbuf_tensor("zero_bf_sb", [128, 2048], BF16))
        bmat0 = _cm(nc.sbuf_tensor("bmat0_sb", [128, 128], F32))
        bmat32 = _cm(nc.sbuf_tensor("bmat32_sb", [128, 128], F32))
        s_ld0 = _cm(nc.semaphore("s_ld0"))
        s_ld1 = _cm(nc.semaphore("s_ld1"))
        s_wl0 = _cm(nc.semaphore("s_wl0"))
        s_wl1 = _cm(nc.semaphore("s_wl1"))
        s_wl2 = _cm(nc.semaphore("s_wl2"))
        s_wl3 = _cm(nc.semaphore("s_wl3"))
        s_st0 = _cm(nc.semaphore("s_st0"))
        s_st1 = _cm(nc.semaphore("s_st1"))
        s_cs0 = _cm(nc.semaphore("s_cs0"))
        s_cs1 = _cm(nc.semaphore("s_cs1"))
        s_mm = _cm(nc.semaphore("s_mm"))
        s_sc = _cm(nc.semaphore("s_sc"))
        s_ve = _cm(nc.semaphore("s_ve"))
        s_vch = _cm(nc.semaphore("s_vch"))
        s_sch = _cm(nc.semaphore("s_sch"))
        st.update(eye_bf=eye_bf, eye_f=eye_f, ones_f=ones_f, ones_bf=ones_bf,
                  bmat0=bmat0, bmat32=bmat32,
                  ld=Pipe([Ctr(s_ld0), Ctr(s_ld1)]),
                  wl=Pipe([Ctr(s_wl0), Ctr(s_wl1)]),
                  wl2=Pipe([Ctr(s_wl2), Ctr(s_wl3)]),
                  stp=Pipe([Ctr(s_st0), Ctr(s_st1)]),
                  cs=Pipe([Ctr(s_cs0), Ctr(s_cs1)]),
                  mm=Ctr(s_mm), sc=Ctr(s_sc), ve=Ctr(s_ve),
                  vch=Ctr(s_vch), sch=Ctr(s_sch),
                  wbf=wbf, cached=set(), cstok={})

        with nc.Block() as b:
            @b.vector
            def _(v):
                ve = st["ve"]
                vch = ChainCtx(st["vch"])
                v.memset(ones_f[:, :], 1.0)
                v.memset(ones_bf[:, :], 1.0)
                vch.i(v.memset(bmat0[:, :], 0.0))
                vch.i(v.memset(bmat32[:, :], 0.0))
                vch.w(v)
                v.memset(bmat0[0:1, :], 1.0)
                v.memset(bmat32[32:33, :], 1.0)
                v.memset(zbuf[:, :], 0.0).then_inc(ve.h, 1)
                ve.v += 1

            @b.sync
            def _(sy):
                ld = st["ld"]
                cs = st["cs"]
                ve = st["ve"]
                ld.issue(sy, eye_bf[:, :], eye_bf_in[:, :], 0)
                tok = ld.issue(sy, eye_f[:, :], eye_f_in[:, :], 0)
                ld.wait(sy, tok)
                sy.wait_ge(ve.h, ve.v)
                kb_n = n_tok // 64
                step = 262144 // (128 * 128)
                j = 0
                for tns in (kt_bd, v_bd):
                    for pair in range(8):
                        for k0 in range(0, kb_n, step):
                            kk = min(step, kb_n - k0)
                            cs.issue(sy, tns[pair, k0:k0 + kk, :, :]
                                     .rearrange("c p t -> p c t"),
                                     zbuf[:, 0:kk * 128]
                                     .rearrange("p (c t) -> p c t", t=128), j)
                            j += 1
                st["zf_final"] = list(cs.bases())

        emit_transpose_hbm(st, src=x_in, dst=xres, direction="a2b")

        for p in parts:
            if p in ("l0ffn_a", "l0ffn_b"):
                emit_l0ffn(st)
            elif p == "l0attn":
                emit_attn(st, layer=0)
            elif p == "l1attn":
                emit_attn(st, layer=1)
            elif p in ("l1ffn_a", "l1ffn_b"):
                emit_l1ffn(st)
            else:
                raise ValueError(p)

        emit_transpose_hbm(st, src=xres, dst=out, direction="b2a")
    return nc




def emit_transpose_hbm(st, src, dst, direction):
    nc = st["nc"]
    eye_f = st["eye_f"]
    ld, stp, mm, ve = st["ld"], st["stp"], st["mm"], st["ve"]
    n_tiles = st["n_tok"] // 128
    b_mm, b_ve = mm.v, ve.v
    b_ld = ld.bases()
    b_st = stp.bases()
    n_dma_per = 1 if direction == "a2b" else HP
    n_st_per = HP if direction == "a2b" else 1

    def mm_val(idx):
        return b_mm + idx + 1

    def ve_val(idx):
        return b_ve + idx + 1

    def ld_tok(i):
        return (i % 2, b_ld[i % 2] + (i // 2 + 1) * 16 * n_dma_per)

    def st_tok(i):
        return (i % 2, b_st[i % 2] + (i // 2 + 1) * 16 * n_st_per)

    with (
        nc.sbuf_tensor(uname("tp_xt"), [128, 2 * H], F32) as xt,
        nc.sbuf_tensor(uname("tp_xc"), [128, 2 * H], F32) as xc,
        nc.psum_tensor(uname("tp_ps"), [128, 4096], F32) as ps,
        nc.Block() as b,
    ):
        @b.sync
        def _(sy):
            for i in range(n_tiles):
                par = i % 2
                if i >= 2:
                    sy.wait_ge(mm.h, mm_val((i - 2) * HP + HP - 1))
                if direction == "a2b":
                    ld.issue(sy, xt[:, par * H:(par + 1) * H],
                             src[i * 128:(i + 1) * 128, :], i)
                else:
                    for c in range(HP):
                        ld.issue(sy,
                                 xt[:, par * H + c * 128:par * H + (c + 1) * 128],
                                 src[c * 128:(c + 1) * 128, i * 128:(i + 1) * 128],
                                 i)

        @b.tensor
        def _(tn):
            for i in range(n_tiles):
                par = i % 2
                ld.wait(tn, ld_tok(i))
                if i >= 2:
                    stp.wait(tn, st_tok(i - 2))
                for j in range(HP):
                    idx = i * HP + j
                    bank = idx % 8
                    if idx >= 8:
                        tn.wait_ge(ve.h, ve_val(idx - 8))
                    tn.transpose(ps[:, bank * 512:bank * 512 + 128],
                                 xt[:, par * H + j * 128: par * H + (j + 1) * 128],
                                 eye_f[:, :]).then_inc(mm.h, 1)
                    mm.v += 1

        @b.vector
        def _(v):
            for i in range(n_tiles):
                par = i % 2
                for j in range(HP):
                    idx = i * HP + j
                    bank = idx % 8
                    v.wait_ge(mm.h, mm_val(idx))
                    v.tensor_copy(
                        xc[:, par * H + j * 128: par * H + (j + 1) * 128],
                        ps[:, bank * 512: bank * 512 + 128]).then_inc(ve.h, 1)
                    ve.v += 1

        @b.gpsimd
        def _(gp):
            for i in range(n_tiles):
                par = i % 2
                gp.wait_ge(ve.h, ve_val(i * HP + HP - 1))
                if direction == "a2b":
                    for c in range(HP):
                        stp.issue(gp,
                                  dst[c * 128:(c + 1) * 128, i * 128:(i + 1) * 128],
                                  xc[:, par * H + c * 128:par * H + (c + 1) * 128],
                                  i)
                else:
                    stp.issue(gp, dst[i * 128:(i + 1) * 128, :],
                              xc[:, par * H:(par + 1) * H], i)
            stp.wait_all(gp)

    assert mm.v == b_mm + n_tiles * HP
    assert ve.v == b_ve + n_tiles * HP




def emit_l0ffn(st):
    nc = st["nc"]
    n_tok = st["n_tok"]
    n_half = n_tok // 2
    with (
        nc.sbuf_tensor(uname("ff_xn"), [128, HP * n_half], BF16) as xn,
        nc.sbuf_tensor(uname("ff_acc"), [128, (n_half // 128) * H], F32) as acc,
        nc.sbuf_tensor(uname("ff_h"), [128, 2 * 4 * n_half], BF16) as hbuf,
        nc.sbuf_tensor(uname("ff_w1"), [128, 2 * HP * 512], BF16) as w1b,
        nc.sbuf_tensor(uname("ff_w2"), [128, 2 * 4 * 1024], BF16) as w2b,
        nc.sbuf_tensor(uname("ff_gw"), [128, HP * 8], F32) as gw_sb,
        nc.sbuf_tensor(uname("ff_c"), [128, (n_tok // 128) * 8], F32) as c_sb,
        nc.sbuf_tensor(uname("ff_tmp"), [128, 2 * 1024], F32) as tmpev,
    ):
        with nc.Block() as b:
            @b.sync
            def _(sy):
                ld = st["ld"]
                tok = None
                for c in range(HP):
                    tok = ld.issue(sy, gw_sb[:, c * 8:(c + 1) * 8],
                                   st["w"]["L0_ffn_gate_w"][c * 128:(c + 1) * 128, :],
                                   0)
                ld.wait(sy, tok)

        for half in range(2):
            _l0_ln_gate_phase(st, half, n_half, xn, gw_sb, c_sb)

            with nc.Block() as b:
                @b.vector
                def _(v):
                    v.memset(acc[:, :], 0.0)

            for e in range(8):
                _l0_expert_block(st, e, half, n_half, xn, hbuf, w1b, w2b,
                                 acc, c_sb, tmpev)

            _acc_to_res_phase(st, half, n_half, acc)


def _l0_ln_gate_phase(st, half, n_half, xn, gw_sb, c_sb):
    nc = st["nc"]
    xres = st["xres"]
    ld, mm, sc, ve = st["ld"], st["mm"], st["sc"], st["ve"]
    vch = ChainCtx(st["vch"])
    sch = ChainCtx(st["sch"])
    ones_f = st["ones_f"]
    n_chunk = n_half // 512
    b_mm, b_sc, b_ve = mm.v, sc.v, ve.v
    b_ld = ld.bases()

    def sq_v(c, j):
        return b_sc + c * 10 + j + 1

    def scA_v(c):
        return b_sc + c * 10 + 9

    def gexp_v(c):
        return b_sc + c * 10 + 10

    def s2_v(c, j):
        return b_mm + c * 11 + j + 1

    def st1_v(c):
        return b_mm + c * 11 + 9

    def bc_v(c):
        return b_mm + c * 11 + 10

    def gmm_v(c):
        return b_mm + c * 11 + 11

    def veA_v(c):
        return b_ve + c * 4 + 1

    def nrm_v(c):
        return b_ve + c * 4 + 2

    def gv_v(c):
        return b_ve + c * 4 + 3

    def gfin_v(c):
        return b_ve + c * 4 + 4

    def ld_tok(c):
        return (c % 2, b_ld[c % 2] + (c // 2 + 1) * 16)

    with (
        nc.sbuf_tensor(uname("lg_xf"), [128, HP * 512], F32) as xf,
        nc.sbuf_tensor(uname("lg_xq"), [128, 1024], F32) as xq,
        nc.sbuf_tensor(uname("lg_srow"), [128, 2048], F32) as srow,
        nc.sbuf_tensor(uname("lg_gs"), [128, 128], F32) as gsc,
        nc.psum_tensor(uname("lg_ps_st"), [128, 1024], F32) as ps_st,
        nc.psum_tensor(uname("lg_ps_b"), [128, 1024], F32) as ps_b,
        nc.psum_tensor(uname("lg_ps_g"), [128, 512], F32) as ps_g,
        nc.Block() as b,
    ):
        @b.sync
        def _(sy):
            for c in range(n_chunk):
                t0 = half * n_half + c * 512
                if c >= 1:
                    sy.wait_ge(mm.h, gmm_v(c - 1))
                    sy.wait_ge(ve.h, nrm_v(c - 1))
                ld.issue(sy, xf[:, :].rearrange("p (c t) -> p c t", t=512),
                         xres[:, t0:t0 + 512]
                         .rearrange("(c p) t -> p c t", p=128), c)

        @b.scalar
        def _(s):
            for c in range(n_chunk):
                ld.wait(s, ld_tok(c))
                if c >= 1:
                    s.wait_ge(mm.h, st1_v(c - 1))
                for j in range(HP):
                    par = j % 2
                    if j >= 2:
                        s.wait_ge(mm.h, s2_v(c, j - 2))
                    s.activation(xq[:, par * 512:(par + 1) * 512],
                                 xf[:, j * 512:(j + 1) * 512],
                                 AF.Square).then_inc(sc.h, 1)
                    sc.v += 1
                s.wait_ge(ve.h, veA_v(c))
                sch.i(s.activation(srow[0:1, 1536:2048], srow[0:1, 1024:1536],
                                   AF.Ln))
                sch.w(s)
                s.activation(srow[0:1, 1536:2048], srow[0:1, 1536:2048],
                             AF.Exp, scale=-0.5).then_inc(sc.h, 1)
                sc.v += 1
                s.wait_ge(ve.h, gv_v(c))
                last = None
                for sub in range(4):
                    d = gsc[:, 96 + sub * 4 + 2:96 + sub * 4 + 3]
                    z = gsc[:, 96 + sub * 4 + 3:96 + sub * 4 + 4]
                    last = s.activation(z, d, AF.Exp)
                last.then_inc(sc.h, 1)
                sc.v += 1

        @b.tensor
        def _(tn):
            for c in range(n_chunk):
                if c >= 1:
                    tn.wait_ge(ve.h, veA_v(c - 1))
                for j in range(HP):
                    par = j % 2
                    tn.wait_ge(sc.h, sq_v(c, j))
                    tn.matmul(ps_st[0:1, 512:1024], ones_f[:, 0:1],
                              xq[:, par * 512:(par + 1) * 512],
                              start=(j == 0), stop=(j == HP - 1)).then_inc(mm.h, 1)
                    mm.v += 1
                i = None
                for j in range(HP):
                    i = tn.matmul(ps_st[0:1, 0:512], ones_f[:, 0:1],
                                  xf[:, j * 512:(j + 1) * 512],
                                  start=(j == 0), stop=(j == HP - 1))
                i.then_inc(mm.h, 1)
                mm.v += 1
                tn.wait_ge(sc.h, scA_v(c))
                if c >= 1:
                    tn.wait_ge(ve.h, nrm_v(c - 1))
                tn.matmul(ps_b[:, 0:512], st["bmat0"][:, :], srow[:, 0:512],
                          start=True, stop=True)
                tn.matmul(ps_b[:, 512:1024], st["bmat0"][:, :],
                          srow[:, 1536:2048], start=True,
                          stop=True).then_inc(mm.h, 1)
                mm.v += 1
                tn.wait_ge(ve.h, nrm_v(c))
                i = None
                for sub in range(4):
                    for j in range(HP):
                        i = tn.matmul(
                            ps_g[:, sub * 8:(sub + 1) * 8],
                            xf[:, j * 512 + sub * 128: j * 512 + (sub + 1) * 128],
                            gw_sb[:, j * 8:(j + 1) * 8],
                            start=(j == 0), stop=(j == HP - 1))
                    i = tn.matmul(ps_g[:, 32 + sub:33 + sub],
                                  srow[0:1, 1536 + sub * 128:
                                       1536 + (sub + 1) * 128],
                                  ones_f[0:1, 0:1], start=True, stop=True)
                i.then_inc(mm.h, 1)
                mm.v += 1

        @b.vector
        def _(v):
            vch.i(v.memset(srow[:, :], 0.0))
            for c in range(n_chunk):
                ctile0 = (half * n_half + c * 512) // 128
                v.wait_ge(mm.h, st1_v(c))
                vch.w(v)
                vch.i(v.tensor_scalar(srow[0:1, 0:512], ps_st[0:1, 0:512],
                                      1.0 / H, None, ALU.mult))
                vch.i(v.tensor_scalar(srow[0:1, 512:1024], ps_st[0:1, 512:1024],
                                      1.0 / H, None, ALU.mult))
                vch.w(v)
                vch.i(v.tensor_tensor(srow[0:1, 1024:1536], srow[0:1, 0:512],
                                      srow[0:1, 0:512], ALU.mult))
                vch.w(v)
                vch.i(v.tensor_tensor(srow[0:1, 1024:1536], srow[0:1, 512:1024],
                                      srow[0:1, 1024:1536], ALU.subtract))
                vch.w(v)
                v.tensor_scalar(srow[0:1, 1024:1536], srow[0:1, 1024:1536],
                                1e-5, None, ALU.add).then_inc(ve.h, 1)
                ve.v += 1
                v.wait_ge(mm.h, bc_v(c))
                last = None
                for j in range(HP):
                    vch.i(v.tensor_tensor(xf[:, j * 512:(j + 1) * 512],
                                          xf[:, j * 512:(j + 1) * 512],
                                          ps_b[:, 0:512], ALU.subtract))
                    vch.w(v)
                    last = v.tensor_tensor(
                        xn[:, j * n_half + c * 512: j * n_half + (c + 1) * 512],
                        xf[:, j * 512:(j + 1) * 512],
                        ps_b[:, 512:1024], ALU.mult)
                last.then_inc(ve.h, 1)
                ve.v += 1
                v.wait_ge(mm.h, gmm_v(c))
                last = None
                for sub in range(4):
                    rcol = gsc[:, 120 + sub:121 + sub]
                    lg = gsc[:, sub * 16:sub * 16 + 8]
                    eq1 = gsc[:, sub * 16 + 8:sub * 16 + 16]
                    l2 = gsc[:, 64 + sub * 8:64 + sub * 8 + 8]
                    m1 = gsc[:, 96 + sub * 4:96 + sub * 4 + 1]
                    m2 = gsc[:, 96 + sub * 4 + 1:96 + sub * 4 + 2]
                    d = gsc[:, 96 + sub * 4 + 2:96 + sub * 4 + 3]
                    vch.i(v.tensor_copy(rcol, ps_g[:, 32 + sub:33 + sub]))
                    vch.w(v)
                    vch.i(v.tensor_scalar(lg, ps_g[:, sub * 8:(sub + 1) * 8],
                                          rcol, None, ALU.mult))
                    vch.w(v)
                    vch.i(v.reduce_max(m1, lg, axis=AX.X))
                    vch.w(v)
                    vch.i(v.tensor_scalar(eq1, lg, m1, None, ALU.is_equal))
                    vch.w(v)
                    vch.i(v.scalar_tensor_tensor(l2, eq1, NEG_BIG, lg,
                                                 ALU.mult, ALU.add))
                    vch.w(v)
                    vch.i(v.reduce_max(m2, l2, axis=AX.X))
                    vch.w(v)
                    vch.i(v.tensor_scalar(l2, l2, m2, None, ALU.is_equal))
                    vch.w(v)
                    last = v.tensor_tensor(d, m2, m1, ALU.subtract)
                last.then_inc(ve.h, 1)
                ve.v += 1
                v.wait_ge(sc.h, gexp_v(c))
                for sub in range(4):
                    eq1 = gsc[:, sub * 16 + 8:sub * 16 + 16]
                    eq2 = gsc[:, 64 + sub * 8:64 + sub * 8 + 8]
                    z = gsc[:, 96 + sub * 4 + 3:96 + sub * 4 + 4]
                    zp = gsc[:, 112 + sub * 2:112 + sub * 2 + 1]
                    r = gsc[:, 112 + sub * 2 + 1:112 + sub * 2 + 2]
                    cdst = c_sb[:, (ctile0 + sub) * 8:(ctile0 + sub + 1) * 8]
                    vch.i(v.tensor_scalar(zp, z, 1.0, None, ALU.add))
                    vch.w(v)
                    vch.i(v.reciprocal(r, zp))
                    vch.w(v)
                    vch.i(v.tensor_tensor(zp, z, r, ALU.mult))
                    vch.w(v)
                    vch.i(v.tensor_scalar(cdst, eq2, zp, None, ALU.mult))
                    vch.w(v)
                    vch.i(v.scalar_tensor_tensor(cdst, eq1, r, cdst,
                                                 ALU.mult, ALU.add))
                    vch.w(v)
                    vch.i(v.tensor_scalar(cdst, cdst, 0.5, None, ALU.mult))
                vch.w(v)
                v.engine_nop().then_inc(ve.h, 1)
                ve.v += 1

    assert sc.v == b_sc + n_chunk * 10
    assert mm.v == b_mm + n_chunk * 11
    assert ve.v == b_ve + n_chunk * 4


def _l0_expert_block(st, e, half, n_half, xn, hbuf, w1b, w2b, acc, c_sb, tmpev):
    nc = st["nc"]
    w = st["w"]
    wl, mm, sc, ve = st["wl"], st["mm"], st["sc"], st["ve"]
    G = 8
    HW = 512
    n_t512 = n_half // 512
    n_t128 = n_half // 128
    na = 4 * n_t512

    b_mm, b_sc, b_ve = mm.v, sc.v, ve.v
    b_wl = wl.bases()

    def wl_tok(g):
        return (g % 2, b_wl[g % 2] + (g // 2 + 1) * 32)

    def a_val(g, gi):
        return b_mm + g * (na + n_t128) + gi + 1

    def bmm_val(g, t):
        return b_mm + g * (na + n_t128) + na + t + 1

    def sil_val(gg):
        return b_sc + (gg // na) * (na + n_t128) + (gg % na) + 1

    def bev_val(bt):
        return b_sc + (bt // n_t128) * (na + n_t128) + na + (bt % n_t128) + 1

    def vad_val(bt):
        return b_ve + bt + 1

    with (
        nc.psum_tensor(uname("ff_ps_a"), [128, 2048], F32) as ps_a,
        nc.psum_tensor(uname("ff_ps_o"), [128, 2048], F32) as ps_o,
        nc.Block() as b,
    ):
        jobs = CacheJobs(st)
        slot_ptoks = {}

        @b.gpsimd
        def _(gp):
            wbf = st["wbf"]
            for g in range(G):
                par = g % 2
                if g >= 2:
                    gp.wait_ge(mm.h, bmm_val(g - 2, n_t128 - 1))
                    for pt in slot_ptoks.get(g - 2, ()):
                        gp.wait_ge(st["cs"].c[pt[0]].h, pt[1])
                rdy = (mm.h, a_val(g, 0))
                _, p1 = _wload(
                    st, gp, wl,
                    w1b[:, par * 4096:(par + 1) * 4096]
                    .rearrange("p (c t) -> p c t", t=HW),
                    ("w1", e, g),
                    w["L0_ffn_w1"][e, :, g * HW:(g + 1) * HW]
                    .rearrange("(c p) t -> p c t", p=128),
                    wbf["L0_ffn_w1"][e, :, g * HW:(g + 1) * HW]
                    .rearrange("(c p) t -> p c t", p=128), g, jobs, ready=rdy)
                _, p2 = _wload(
                    st, gp, wl,
                    w2b[:, par * 4096:(par + 1) * 4096]
                    .rearrange("p (c t) -> p c t", t=1024),
                    ("w2", e, g),
                    w["L0_ffn_w2"][e, g * HW:(g + 1) * HW, :]
                    .rearrange("(c p) t -> p c t", p=128),
                    wbf["L0_ffn_w2"][e, g * HW:(g + 1) * HW, :]
                    .rearrange("(c p) t -> p c t", p=128), g, jobs, ready=rdy)
                slot_ptoks[g] = [p for p in (p1, p2) if p is not None]

        if jobs.items:
            @b.sync
            def _(sy):
                _emit_cache_stores(st, sy, jobs)

        @b.tensor
        def _(tn):
            for g in range(G):
                par = g % 2
                wl.wait(tn, wl_tok(g))
                for m in range(4):
                    for n in range(n_t512):
                        gi = m * n_t512 + n
                        gg = g * na + gi
                        bank = gg % 4
                        if gg >= 4:
                            tn.wait_ge(sc.h, sil_val(gg - 4))
                        i = None
                        for k in range(HP):
                            i = tn.matmul(
                                ps_a[:, bank * 512:(bank + 1) * 512],
                                w1b[:, par * 4096 + k * 512 + m * 128:
                                    par * 4096 + k * 512 + (m + 1) * 128],
                                xn[:, k * n_half + n * 512:
                                   k * n_half + (n + 1) * 512],
                                start=(k == 0), stop=(k == HP - 1))
                        i.then_inc(mm.h, 1)
                        mm.v += 1
                for t in range(n_t128):
                    bt = g * n_t128 + t
                    bp = (t % 2) * 2
                    if bt >= 2:
                        tn.wait_ge(sc.h, bev_val(bt - 2))
                    if t == 0:
                        tn.wait_ge(sc.h, sil_val(g * na + na - 1))
                    i = None
                    for nn in range(2):
                        for k in range(4):
                            i = tn.matmul(
                                ps_o[:, (bp + nn) * 512:(bp + nn + 1) * 512],
                                hbuf[:, par * 4 * n_half + k * n_half + t * 128:
                                     par * 4 * n_half + k * n_half + (t + 1) * 128],
                                w2b[:, par * 4096 + k * 1024 + nn * 512:
                                    par * 4096 + k * 1024 + (nn + 1) * 512],
                                start=(k == 0), stop=(k == 3))
                    i.then_inc(mm.h, 1)
                    mm.v += 1

        @b.scalar
        def _(s):
            for g in range(G):
                par = g % 2
                for m in range(4):
                    for n in range(n_t512):
                        gi = m * n_t512 + n
                        gg = g * na + gi
                        bank = gg % 4
                        s.wait_ge(mm.h, a_val(g, gi))
                        s.activation(
                            hbuf[:, par * 4 * n_half + m * n_half + n * 512:
                                 par * 4 * n_half + m * n_half + (n + 1) * 512],
                            ps_a[:, bank * 512:(bank + 1) * 512],
                            AF.Silu).then_inc(sc.h, 1)
                        sc.v += 1
                for t in range(n_t128):
                    bt = g * n_t128 + t
                    bp = (t % 2) * 2
                    tpar = t % 2
                    s.wait_ge(mm.h, bmm_val(g, t))
                    if bt >= 2:
                        s.wait_ge(ve.h, vad_val(bt - 2))
                    ctile = half * n_t128 + t
                    csl = c_sb[:, ctile * 8 + e:ctile * 8 + e + 1]
                    s.activation(tmpev[:, tpar * 1024:tpar * 1024 + 512],
                                 ps_o[:, bp * 512:(bp + 1) * 512],
                                 AF.Copy, scale=csl)
                    s.activation(
                        tmpev[:, tpar * 1024 + 512:tpar * 1024 + 1024],
                        ps_o[:, (bp + 1) * 512:(bp + 2) * 512],
                        AF.Copy, scale=csl).then_inc(sc.h, 1)
                    sc.v += 1

        @b.vector
        def _(v):
            vch = ChainCtx(st["vch"])
            for g in range(G):
                for t in range(n_t128):
                    bt = g * n_t128 + t
                    tpar = t % 2
                    v.wait_ge(sc.h, bev_val(bt))
                    vch.w(v)
                    vch.i(v.tensor_tensor(
                        acc[:, t * H:t * H + 512],
                        acc[:, t * H:t * H + 512],
                        tmpev[:, tpar * 1024:tpar * 1024 + 512],
                        ALU.add))
                    v.tensor_tensor(
                        acc[:, t * H + 512:(t + 1) * H],
                        acc[:, t * H + 512:(t + 1) * H],
                        tmpev[:, tpar * 1024 + 512:tpar * 1024 + 1024],
                        ALU.add).then_inc(ve.h, 1)
                    ve.v += 1

    assert mm.v == b_mm + G * (na + n_t128), (mm.v, b_mm)
    assert sc.v == b_sc + G * (na + n_t128)
    assert ve.v == b_ve + G * n_t128


def _acc_to_res_phase(st, half, n_half, acc):
    nc = st["nc"]
    xres = st["xres"]
    ld, stp, mm, ve = st["ld"], st["stp"], st["mm"], st["ve"]
    eye_f = st["eye_f"]
    n_t512 = n_half // 512
    b_mm, b_ve = mm.v, ve.v
    b_ld = ld.bases()
    b_st = stp.bases()

    def mm_val(idx):
        return b_mm + idx + 1

    def ve_val(idx):
        return b_ve + idx + 1

    def ld_tok(q):
        return (q % 2, b_ld[q % 2] + (q // 2 + 1) * 16)

    def st_tok(q):
        return (q % 2, b_st[q % 2] + (q // 2 + 1) * 16)

    with (
        nc.sbuf_tensor(uname("ar_xr"), [128, 2 * HP * 512], F32) as xrt,
        nc.psum_tensor(uname("ar_ps"), [128, 4096], F32) as ps_t,
        nc.Block() as b,
    ):
        @b.sync
        def _(sy):
            for q in range(n_t512):
                par = q % 2
                if q >= 2:
                    stp.wait(sy, st_tok(q - 2))
                ld.issue(sy,
                         xrt[:, par * 4096:(par + 1) * 4096]
                         .rearrange("p (c t) -> p c t", t=512),
                         xres[:, half * n_half + q * 512:
                              half * n_half + (q + 1) * 512]
                         .rearrange("(c p) t -> p c t", p=128), q)

        @b.tensor
        def _(tn):
            for q in range(n_t512):
                ld.wait(tn, ld_tok(q))
                for tt in range(4):
                    t = q * 4 + tt
                    for j in range(HP):
                        idx = q * 32 + tt * HP + j
                        bank = idx % 8
                        if idx >= 8:
                            tn.wait_ge(ve.h, ve_val(idx - 8))
                        tn.transpose(
                            ps_t[:, bank * 512:bank * 512 + 128],
                            acc[:, t * H + j * 128: t * H + (j + 1) * 128],
                            eye_f[:, :]).then_inc(mm.h, 1)
                        mm.v += 1

        @b.vector
        def _(v):
            for q in range(n_t512):
                par = q % 2
                for tt in range(4):
                    for j in range(HP):
                        idx = q * 32 + tt * HP + j
                        bank = idx % 8
                        v.wait_ge(mm.h, mm_val(idx))
                        dst = xrt[:, par * 4096 + j * 512 + tt * 128:
                                  par * 4096 + j * 512 + (tt + 1) * 128]
                        v.tensor_tensor(
                            dst, dst, ps_t[:, bank * 512:bank * 512 + 128],
                            ALU.add).then_inc(ve.h, 1)
                        ve.v += 1

        @b.gpsimd
        def _(gp):
            for q in range(n_t512):
                par = q % 2
                gp.wait_ge(ve.h, ve_val(q * 32 + 31))
                stp.issue(gp,
                          xres[:, half * n_half + q * 512:
                               half * n_half + (q + 1) * 512]
                          .rearrange("(c p) t -> p c t", p=128),
                          xrt[:, par * 4096:(par + 1) * 4096]
                          .rearrange("p (c t) -> p c t", t=512), q)
            stp.wait_all(gp)

    assert mm.v == b_mm + n_t512 * 32
    assert ve.v == b_ve + n_t512 * 32


def emit_attn(st, layer):
    nc = st["nc"]
    w = st["w"]
    n_tok = st["n_tok"]
    pre = "L0_attn_" if layer == 0 else "L1_attn_"
    scale = (1.0 / np.sqrt(512.0)) if layer == 0 else (1.0 / 8.0)
    with (
        nc.sbuf_tensor(uname("at_wo"), [128, HP * H], BF16) as wo,
        nc.sbuf_tensor(uname("at_bo"), [128, 8], F32) as boP,
        nc.sbuf_tensor(uname("at_bq"), [128, 8], F32) as bqP,
    ):
        _attn_weights_phase(st, pre, layer, wo, boP, bqP)
        _attn_proj_phase(st, pre, layer, wo, bqP)
        _attn_core_phase(st, layer, wo, boP, scale)


def _attn_weights_phase(st, pre, layer, wo, boP, bqP):
    nc = st["nc"]
    w = st["w"]
    wl, ld, mm, sc, ve = st["wl"], st["ld"], st["mm"], st["sc"], st["ve"]
    eye_bf = st["eye_bf"]
    ones_f = st["ones_f"]
    import contextlib
    stk = contextlib.ExitStack()
    wq = stk.enter_context(nc.sbuf_tensor(uname("at_wq"), [128, HP * H], BF16))
    wk = stk.enter_context(nc.sbuf_tensor(uname("at_wk"), [128, HP * H], BF16))
    wv = stk.enter_context(nc.sbuf_tensor(uname("at_wv"), [128, HP * H], BF16))
    st["_attn_stack"] = stk
    st["_attn_wqkv"] = (wq, wk, wv)

    b_wl = wl.bases()
    b_ld = ld.bases()
    b_mm, b_sc, b_ve = mm.v, sc.v, ve.v

    with (
        nc.sbuf_tensor(uname("at_bv"), [128, 8], BF16) as bvB,
        nc.sbuf_tensor(uname("at_bo0"), [128, 8], F32) as bo0,
        nc.sbuf_tensor(uname("at_brow"), [128, 1024], F32) as brow,
        nc.psum_tensor(uname("at_pw"), [128, 4096], F32) as pw,
        nc.Block() as b,
    ):
        @b.gpsimd
        def _(gp):
            for gi, (buf, name) in enumerate([(wq, "Wq"), (wk, "Wk"),
                                              (wv, "Wv"), (wo, "Wo")]):
                wl.issue(gp, buf[:, :].rearrange("p (c t) -> p c t", t=H),
                         w[pre + name][:, :].rearrange("(c p) t -> p c t", p=128),
                         0)

        @b.sync
        def _(sy):
            ld.issue(sy, brow[0:1, 0:1024], w[pre + "bq"][None, :], 0)
            ld.issue(sy, brow[32:33, 0:1024], w[pre + "bv"][None, :], 0)
            ld.issue(sy, brow[64:65, 0:1024], w[pre + "bo"][None, :], 0)

        b_ld0 = b_ld[0]

        @b.tensor
        def _(tn):
            ld.wait(tn, (0, b_ld0 + 48))
            i = None
            for bi, bp in enumerate((0, 32, 64)):
                for c in range(HP):
                    i = tn.matmul(pw[:, 3584 + bi * 8 + c:3584 + bi * 8 + c + 1],
                                  brow[bp:bp + 1, c * 128:(c + 1) * 128],
                                  ones_f[bp:bp + 1, 0:1], start=True, stop=True)
            i.then_inc(mm.h, 1)
            mm.v += 1

        @b.vector
        def _(v):
            v.wait_ge(mm.h, b_mm + 1)
            v.tensor_copy(bqP[:, :], pw[:, 3584:3592]).then_inc(ve.h, 1)
            ve.v += 1
            v.tensor_copy(bvB[:, :], pw[:, 3592:3600])
            v.tensor_copy(bo0[:, :], pw[:, 3600:3608]).then_inc(ve.h, 1)
            ve.v += 1

        b_mm += 1
        b_ve += 2

        n_tr = 0
        if layer == 0:
            entries = []
            for bi in range(4):
                for r in range(HP):
                    for j in range(r, HP):
                        if r == j:
                            entries.append((bi, r, r, len(entries)))
                        else:
                            i0 = len(entries)
                            entries.append((bi, r, j, i0 + 1))
                            entries.append((bi, j, r, i0))
            bufs = (wq, wk, wv, wo)

            def _slotap(k):
                slot = k % 8
                pt = pw[:, slot * 512:slot * 512 + 512]
                return pt.bitcast(BF16)[:, 0:128]

            @b.tensor
            def _(tn):
                tn.wait_ge(wl.c[0].h, b_wl[0] + 4 * 16)
                tn.wait_ge(ve.h, b_ve)
                for k, (bi, a, bb, pk) in enumerate(entries):
                    if k >= 8:
                        tn.wait_ge(ve.h, b_ve + k - 8 + 1)
                    tn.transpose(
                        _slotap(k),
                        bufs[bi][:, a * H + bb * 128: a * H + (bb + 1) * 128],
                        eye_bf[:, :]).then_inc(mm.h, 1)
                    mm.v += 1

            @b.vector
            def _(v):
                for k, (bi, a, bb, pk) in enumerate(entries):
                    v.wait_ge(mm.h, b_mm + max(k, pk) + 1)
                    v.tensor_copy(
                        bufs[bi][:, bb * H + a * 128: bb * H + (a + 1) * 128],
                        _slotap(k)).then_inc(ve.h, 1)
                    ve.v += 1

            n_tr = len(entries)

        b_mm2 = b_mm + n_tr

        @b.tensor
        def _(tn):
            if layer != 0:
                tn.wait_ge(wl.c[0].h, b_wl[0] + 4 * 16)
            tn.wait_ge(ve.h, b_ve + n_tr)
            i = None
            for m in range(HP):
                for k in range(HP):
                    i = tn.matmul(pw[:, 2048 + m:2048 + m + 1],
                                  wo[:, k * H + m * 128: k * H + (m + 1) * 128],
                                  bvB[:, k:k + 1], start=(k == 0), stop=(k == HP - 1))
            i.then_inc(mm.h, 1)
            mm.v += 1

        @b.vector
        def _(v):
            v.wait_ge(mm.h, b_mm2 + 1)
            v.tensor_tensor(boP[:, :], pw[:, 2048:2056], bo0[:, :], ALU.add)\
                .then_inc(ve.h, 1)
            ve.v += 1

    assert mm.v == b_mm + n_tr + 1
    assert ve.v == b_ve + n_tr + 1


def _attn_proj_phase(st, pre, layer, wo, bqP):
    nc = st["nc"]
    xres = st["xres"]
    ld, stp, mm, sc, ve = st["ld"], st["stp"], st["mm"], st["sc"], st["ve"]
    vch = ChainCtx(st["vch"])
    sch = ChainCtx(st["sch"])
    ones_f = st["ones_f"]
    wq, wk, wv = st["_attn_wqkv"]
    n_tok = st["n_tok"]
    n_chunk = n_tok // 512
    b_mm, b_sc, b_ve = mm.v, sc.v, ve.v
    b_ld = ld.bases()
    b_st = stp.bases()

    def s2_v(c, j):
        return b_mm + c * 34 + j + 1

    def st1_v(c):
        return b_mm + c * 34 + 9

    def bc_v(c):
        return b_mm + c * 34 + 10

    def qmm_v(c, m):
        return b_mm + c * 34 + 11 + m

    def kmm_v(c, m):
        return b_mm + c * 34 + 19 + m

    def vmm_v(c, g):
        return b_mm + c * 34 + 27 + g

    def sq_v(c, j):
        return b_sc + c * 17 + j + 1

    def scA_v(c):
        return b_sc + c * 17 + 9

    def qev_v(c, m):
        return b_sc + c * 17 + 10 + m

    def veA_v(c):
        return b_ve + c * 18 + 1

    def nrm_v(c):
        return b_ve + c * 18 + 2

    def kev_v(c, m):
        return b_ve + c * 18 + 3 + m

    def vev_v(c, g):
        return b_ve + c * 18 + 11 + g

    def ld_tok(c):
        return (c % 2, b_ld[c % 2] + (c // 2 + 1) * 16)

    def st_tok(c):
        return (c % 2, b_st[c % 2] + (c // 2 + 1) * 48)

    with (
        nc.sbuf_tensor(uname("ap_xf"), [128, HP * 512], F32) as xf,
        nc.sbuf_tensor(uname("ap_xq"), [128, 1024], F32) as xq,
        nc.sbuf_tensor(uname("ap_srow"), [128, 2048], F32) as srow,
        nc.sbuf_tensor(uname("ap_xn"), [128, 2 * HP * 512], BF16) as xnb,
        nc.sbuf_tensor(uname("ap_qt"), [128, 2 * HP * 512], BF16) as qtb,
        nc.sbuf_tensor(uname("ap_kt"), [128, 2 * HP * 512], BF16) as ktb,
        nc.sbuf_tensor(uname("ap_vt"), [128, 2 * 4 * H], BF16) as vtb,
        nc.psum_tensor(uname("ap_ps_st"), [128, 1024], F32) as ps_st,
        nc.psum_tensor(uname("ap_ps_b"), [128, 1024], F32) as ps_b,
        nc.psum_tensor(uname("ap_ps_p"), [128, 2048], F32) as ps_p,
        nc.Block() as b,
    ):
        @b.sync
        def _(sy):
            for c in range(n_chunk):
                t0 = c * 512
                if c >= 1:
                    sy.wait_ge(ve.h, nrm_v(c - 1))
                ld.issue(sy, xf[:, :].rearrange("p (c t) -> p c t", t=512),
                         xres[:, t0:t0 + 512]
                         .rearrange("(c p) t -> p c t", p=128), c)

        @b.scalar
        def _(s):
            for c in range(n_chunk):
                par = c % 2
                ld.wait(s, ld_tok(c))
                if c >= 1:
                    s.wait_ge(mm.h, st1_v(c - 1))
                for j in range(HP):
                    jp = j % 2
                    if j >= 2:
                        s.wait_ge(mm.h, s2_v(c, j - 2))
                    s.activation(xq[:, jp * 512:(jp + 1) * 512],
                                 xf[:, j * 512:(j + 1) * 512],
                                 AF.Square).then_inc(sc.h, 1)
                    sc.v += 1
                s.wait_ge(ve.h, veA_v(c))
                sch.i(s.activation(srow[0:1, 1536:2048],
                                   srow[0:1, 1024:1536], AF.Ln))
                sch.w(s)
                s.activation(srow[0:1, 1536:2048], srow[0:1, 1536:2048],
                             AF.Exp, scale=-0.5).then_inc(sc.h, 1)
                sc.v += 1
                for m in range(HP):
                    s.wait_ge(mm.h, qmm_v(c, m))
                    if c >= 2:
                        stp.wait(s, st_tok(c - 2))
                    s.activation(qtb[:, par * 4096 + m * 512:
                                     par * 4096 + (m + 1) * 512],
                                 ps_p[:, (m % 2) * 512:(m % 2) * 512 + 512],
                                 AF.Identity, bias=bqP[:, m:m + 1]).then_inc(sc.h, 1)
                    sc.v += 1

        @b.tensor
        def _(tn):
            for c in range(n_chunk):
                par = c % 2
                if c >= 1:
                    tn.wait_ge(ve.h, veA_v(c - 1))
                for j in range(HP):
                    jp = j % 2
                    tn.wait_ge(sc.h, sq_v(c, j))
                    tn.matmul(ps_st[0:1, 512:1024], ones_f[:, 0:1],
                              xq[:, jp * 512:(jp + 1) * 512],
                              start=(j == 0), stop=(j == HP - 1)).then_inc(mm.h, 1)
                    mm.v += 1
                i = None
                for j in range(HP):
                    i = tn.matmul(ps_st[0:1, 0:512], ones_f[:, 0:1],
                                  xf[:, j * 512:(j + 1) * 512],
                                  start=(j == 0), stop=(j == HP - 1))
                i.then_inc(mm.h, 1)
                mm.v += 1
                tn.wait_ge(sc.h, scA_v(c))
                if c >= 1:
                    tn.wait_ge(ve.h, nrm_v(c - 1))
                tn.matmul(ps_b[:, 0:512], st["bmat0"][:, :], srow[:, 0:512],
                          start=True, stop=True)
                tn.matmul(ps_b[:, 512:1024], st["bmat0"][:, :],
                          srow[:, 1536:2048], start=True,
                          stop=True).then_inc(mm.h, 1)
                mm.v += 1
                tn.wait_ge(ve.h, nrm_v(c))
                for m in range(HP):
                    bank = m % 2
                    if c >= 1 and m < 2:
                        tn.wait_ge(ve.h, vev_v(c - 1, 6 + m))
                    if m >= 2:
                        tn.wait_ge(sc.h, qev_v(c, m - 2))
                    i = None
                    for k in range(HP):
                        i = tn.matmul(ps_p[:, bank * 512:(bank + 1) * 512],
                                      wq[:, k * H + m * 128: k * H + (m + 1) * 128],
                                      xnb[:, par * 4096 + k * 512:
                                          par * 4096 + (k + 1) * 512],
                                      start=(k == 0), stop=(k == HP - 1))
                    i.then_inc(mm.h, 1)
                    mm.v += 1
                for m in range(HP):
                    bank = 2 + m % 2
                    if m >= 2:
                        tn.wait_ge(ve.h, kev_v(c, m - 2))
                    i = None
                    for k in range(HP):
                        i = tn.matmul(ps_p[:, bank * 512:(bank + 1) * 512],
                                      wk[:, k * H + m * 128: k * H + (m + 1) * 128],
                                      xnb[:, par * 4096 + k * 512:
                                          par * 4096 + (k + 1) * 512],
                                      start=(k == 0), stop=(k == HP - 1))
                    i.then_inc(mm.h, 1)
                    mm.v += 1
                for g in range(8):
                    tsub, n2 = g // 2, g % 2
                    bank = g % 2
                    if g < 2:
                        tn.wait_ge(sc.h, qev_v(c, 6 + g))
                    else:
                        tn.wait_ge(ve.h, vev_v(c, g - 2))
                    i = None
                    for k in range(HP):
                        i = tn.matmul(
                            ps_p[:, bank * 512:(bank + 1) * 512],
                            xnb[:, par * 4096 + k * 512 + tsub * 128:
                                par * 4096 + k * 512 + (tsub + 1) * 128],
                            wv[:, k * H + n2 * 512: k * H + (n2 + 1) * 512],
                            start=(k == 0), stop=(k == HP - 1))
                    i.then_inc(mm.h, 1)
                    mm.v += 1

        @b.vector
        def _(v):
            vch.i(v.memset(srow[:, :], 0.0))
            for c in range(n_chunk):
                par = c % 2
                v.wait_ge(mm.h, st1_v(c))
                vch.w(v)
                vch.i(v.tensor_scalar(srow[0:1, 0:512], ps_st[0:1, 0:512],
                                      1.0 / H, None, ALU.mult))
                vch.i(v.tensor_scalar(srow[0:1, 512:1024], ps_st[0:1, 512:1024],
                                      1.0 / H, None, ALU.mult))
                vch.w(v)
                vch.i(v.tensor_tensor(srow[0:1, 1024:1536], srow[0:1, 0:512],
                                      srow[0:1, 0:512], ALU.mult))
                vch.w(v)
                vch.i(v.tensor_tensor(srow[0:1, 1024:1536], srow[0:1, 512:1024],
                                      srow[0:1, 1024:1536], ALU.subtract))
                vch.w(v)
                v.tensor_scalar(srow[0:1, 1024:1536], srow[0:1, 1024:1536],
                                1e-5, None, ALU.add).then_inc(ve.h, 1)
                ve.v += 1
                v.wait_ge(mm.h, bc_v(c))
                if c >= 2:
                    pass
                last = None
                for j in range(HP):
                    vch.i(v.tensor_tensor(xf[:, j * 512:(j + 1) * 512],
                                          xf[:, j * 512:(j + 1) * 512],
                                          ps_b[:, 0:512], ALU.subtract))
                    vch.w(v)
                    last = v.tensor_tensor(
                        xnb[:, par * 4096 + j * 512: par * 4096 + (j + 1) * 512],
                        xf[:, j * 512:(j + 1) * 512],
                        ps_b[:, 512:1024], ALU.mult)
                last.then_inc(ve.h, 1)
                ve.v += 1
                for m in range(HP):
                    v.wait_ge(mm.h, kmm_v(c, m))
                    if c >= 2:
                        stp.wait(v, st_tok(c - 2))
                    v.tensor_copy(ktb[:, par * 4096 + m * 512:
                                      par * 4096 + (m + 1) * 512],
                                  ps_p[:, (2 + m % 2) * 512:
                                       (2 + m % 2) * 512 + 512]).then_inc(ve.h, 1)
                    ve.v += 1
                for g in range(8):
                    tsub, n2 = g // 2, g % 2
                    v.wait_ge(mm.h, vmm_v(c, g))
                    v.tensor_copy(vtb[:, par * 4096 + tsub * 1024 + n2 * 512:
                                      par * 4096 + tsub * 1024 + (n2 + 1) * 512],
                                  ps_p[:, (g % 2) * 512:
                                       (g % 2) * 512 + 512]).then_inc(ve.h, 1)
                    ve.v += 1

        @b.gpsimd
        def _(gp):
            qt_hbm, kt_hbm, v_hbm = st["qt_hbm"], st["kt_hbm"], st["v_hbm"]
            for c in range(n_chunk):
                par = c % 2
                t0 = c * 512
                gp.wait_ge(sc.h, qev_v(c, 7))
                stp.issue(gp, qt_hbm[:, t0:t0 + 512]
                          .rearrange("(c p) t -> p c t", p=128),
                          qtb[:, par * 4096:(par + 1) * 4096]
                          .rearrange("p (c t) -> p c t", t=512), c)
                gp.wait_ge(ve.h, kev_v(c, 7))
                stp.issue(gp, kt_hbm[:, t0:t0 + 512]
                          .rearrange("(c p) t -> p c t", p=128),
                          ktb[:, par * 4096:(par + 1) * 4096]
                          .rearrange("p (c t) -> p c t", t=512), c)
                gp.wait_ge(ve.h, vev_v(c, 7))
                stp.issue(gp, v_hbm[t0:t0 + 512, 0:1024]
                          .rearrange("(s p) i -> p s i", p=128),
                          vtb[:, par * 4096:(par + 1) * 4096]
                          .rearrange("p (s i) -> p s i", i=1024), c)
            stp.wait_all(gp)

    assert mm.v == b_mm + n_chunk * 34
    assert sc.v == b_sc + n_chunk * 17
    assert ve.v == b_ve + n_chunk * 18
    st["_attn_stack"].close()
    del st["_attn_stack"], st["_attn_wqkv"]


def _attn_core_phase(st, layer, wo, boP, scale):
    nc = st["nc"]
    ld = st["ld"]
    n_tok = st["n_tok"]
    n_kt = n_tok // 128
    n_grp = n_tok // 512

    with (
        nc.sbuf_tensor(uname("ac_kt"), [128, HP * n_tok], BF16) as kt_sb,
        nc.sbuf_tensor(uname("ac_v"), [128, n_kt * H], BF16) as v_sb,
        nc.sbuf_tensor(uname("ac_qt"), [128, HP * 512], BF16) as qtg,
        nc.sbuf_tensor(uname("ac_pb"), [128, n_tok], BF16) as probs,
        nc.sbuf_tensor(uname("ac_pt"), [128, n_kt * 256], BF16) as probsT,
        nc.sbuf_tensor(uname("ac_cx"), [128, HP * 512], BF16) as ctxb,
        nc.sbuf_tensor(uname("ac_xr"), [128, 2 * 512], F32) as xrg,
        nc.sbuf_tensor(uname("ac_xo"), [128, 2 * 512], F32) as xout,
        nc.sbuf_tensor(uname("ac_rr"), [128, 512], F32) as rrow,
        nc.sbuf_tensor(uname("ac_rsb"), [128, 512], F32) as rsb,
        nc.sbuf_tensor(uname("ac_t1"), [128, 512], F32) as t1b,
        nc.sbuf_tensor(uname("ac_sm"), [128, 32], F32) as sm,
    ):
        with nc.Block() as b:
            @b.sync
            def _(sy):
                ld.issue(sy, kt_sb[:, :].rearrange("p (c t) -> p c t", t=n_tok),
                         st["kt_hbm"][:, 0:n_tok]
                         .rearrange("(c p) t -> p c t", p=128), 0)
                tok = ld.issue(sy, v_sb[:, :].rearrange("p (s i) -> p s i", i=H),
                               st["v_hbm"][0:n_tok, 0:1024]
                               .rearrange("(s p) i -> p s i", p=128), 0)
                ld.wait(sy, tok)

            @b.vector
            def _(v):
                v.memset(rrow[:, :], 0.0)

        for g in range(n_grp):
            _attn_group(st, g, layer, wo, boP, scale, kt_sb, v_sb, qtg,
                        probs, probsT, ctxb, xrg, xout, rrow, rsb, t1b, sm,
                        n_tok)


def _attn_group(st, g, layer, wo, boP, scale, kt_sb, v_sb, qtg, probs,
                probsT, ctxb, xrg, xout, rrow, rsb, t1b, sm, n_tok):
    nc = st["nc"]
    xres = st["xres"]
    ld, stp, mm, sc, ve = st["ld"], st["stp"], st["mm"], st["sc"], st["ve"]
    vch = ChainCtx(st["vch"])
    eye_bf, eye_f, ones_f = st["eye_bf"], st["eye_f"], st["ones_f"]
    n_kt = n_tok // 128
    nb = n_tok // 512
    b_mm, b_sc, b_ve = mm.v, sc.v, ve.v
    b_ld = ld.bases()
    b_st = stp.bases()
    qt_hbm = st["qt_hbm"]

    def mm_sco(sub):
        return b_mm + sub * (1 + n_kt) + 8 * (sub // 2) + 1

    def mm_tr(sub, i):
        return mm_sco(sub) + 1 + i

    def mm_ctx(pair, ic):
        return b_mm + (2 + 2 * pair) * (1 + n_kt) + 8 * pair + ic + 1

    mm_rsi = b_mm + 4 * (1 + n_kt) + 16 + 1
    mm_rsb = mm_rsi + 1

    def mm_wo(m):
        return mm_rsb + m + 1

    def sc_exp(sub):
        return b_sc + sub + 1

    def ve_nmx(sub):
        return b_ve + sub * (n_kt + 2) + 8 * (sub // 2) + 1

    def ve_trev(sub, i):
        return ve_nmx(sub) + 1 + i

    def ve_rcp(sub):
        return ve_nmx(sub) + 1 + n_kt

    def ve_cxe(pair, ic):
        return b_ve + (2 + 2 * pair) * (n_kt + 2) + 8 * pair + ic + 1

    ve_rre = b_ve + 4 * (n_kt + 2) + 16 + 1
    ve_rsbe = ve_rre + 1

    def ve_woe(m):
        return ve_rsbe + m + 1

    with (
        nc.psum_tensor(uname("ag_ps"), [128, 4096], F32) as ps,
        nc.Block() as b,
    ):
        tok_xr = []
        tok_q = [None]

        @b.sync
        def _(sy):
            tok_q[0] = ld.issue(sy, qtg[:, :].rearrange("p (c t) -> p c t", t=512),
                                qt_hbm[:, g * 512:(g + 1) * 512]
                                .rearrange("(c p) t -> p c t", p=128), 2 * g)
            for m in range(HP):
                if m >= 2:
                    sy.wait_ge(ve.h, ve_woe(m - 2))
                tok_xr.append(ld.issue(
                    sy, xrg[:, (m % 2) * 512:(m % 2) * 512 + 512],
                    xres[m * 128:(m + 1) * 128, g * 512:(g + 1) * 512],
                    2 * g + m))

        @b.scalar
        def _(s):
            for sub in range(4):
                s.wait_ge(ve.h, ve_nmx(sub))
                s.activation(probs[:, :], ps[:, 0:n_tok], AF.Exp,
                             bias=sm[:, sub:sub + 1], scale=scale,
                             accum_out=sm[:, 4 + sub:5 + sub]).then_inc(sc.h, 1)
                sc.v += 1

        @b.tensor
        def _(tn):
            ld.wait(tn, tok_xr[0])
            for sub in range(4):
                pair = sub // 2
                if sub >= 1:
                    tn.wait_ge(ve.h, ve_trev(sub - 1, n_kt - 1))
                if sub == 2:
                    tn.wait_ge(ve.h, ve_cxe(0, 7))
                i = None
                for n in range(nb):
                    for k in range(HP):
                        i = tn.matmul(
                            ps[:, n * 512:(n + 1) * 512],
                            qtg[:, k * 512 + sub * 128: k * 512 + (sub + 1) * 128],
                            kt_sb[:, k * n_tok + n * 512: k * n_tok + (n + 1) * 512],
                            start=(k == 0), stop=(k == HP - 1))
                i.then_inc(mm.h, 1)
                mm.v += 1
                tn.wait_ge(sc.h, sc_exp(sub))
                for idx in range(n_kt):
                    bank = idx % nb
                    slot = idx // nb
                    if idx >= nb:
                        tn.wait_ge(ve.h, ve_trev(sub, idx - nb))
                    pt = ps[:, bank * 512:(bank + 1) * 512]
                    pt = pt.bitcast(BF16)[:, slot * 128:(slot + 1) * 128]
                    tn.transpose(pt, probs[:, idx * 128:(idx + 1) * 128],
                                 eye_bf[:, :]).then_inc(mm.h, 1)
                    mm.v += 1
                if sub % 2 == 1:
                    tn.wait_ge(ve.h, ve_trev(sub, n_kt - 1))
                    for ic in range(HP):
                        off = (ic % 2) * 512
                        if ic >= 2:
                            tn.wait_ge(ve.h, ve_cxe(pair, ic - 2))
                        i = None
                        for kt in range(n_kt):
                            i = tn.matmul(
                                ps[:, off:off + 256],
                                v_sb[:, kt * H + ic * 128: kt * H + (ic + 1) * 128],
                                probsT[:, kt * 256:(kt + 1) * 256],
                                start=(kt == 0), stop=(kt == n_kt - 1))
                        i.then_inc(mm.h, 1)
                        mm.v += 1
            tn.wait_ge(ve.h, ve_rcp(3))
            i = None
            for sub in range(4):
                i = tn.matmul(ps[0:1, 1024 + sub * 128:1024 + (sub + 1) * 128],
                              sm[:, 8 + sub:9 + sub], eye_f[:, :],
                              start=True, stop=True)
            i.then_inc(mm.h, 1)
            mm.v += 1
            tn.wait_ge(ve.h, ve_rre)
            tn.matmul(ps[:, 1536:2048], st["bmat0"][:, :], rrow[:, 0:512],
                      start=True, stop=True).then_inc(mm.h, 1)
            mm.v += 1
            tn.wait_ge(ve.h, ve_rsbe)
            for m in range(HP):
                off = (m % 2) * 512
                if m >= 2:
                    tn.wait_ge(ve.h, ve_woe(m - 2))
                i = None
                for k in range(HP):
                    i = tn.matmul(ps[:, off:off + 512],
                                  wo[:, k * H + m * 128: k * H + (m + 1) * 128],
                                  ctxb[:, k * 512:(k + 1) * 512],
                                  start=(k == 0), stop=(k == HP - 1))
                i.then_inc(mm.h, 1)
                mm.v += 1

        @b.vector
        def _(v):
            for sub in range(4):
                pair = sub // 2
                v.wait_ge(mm.h, mm_sco(sub))
                vch.i(v.reduce_max(sm[:, 24 + sub:25 + sub], ps[:, 0:n_tok],
                                   axis=AX.X, negate=True))
                vch.w(v)
                v.tensor_scalar(sm[:, sub:sub + 1], sm[:, 24 + sub:25 + sub],
                                scale, None, ALU.mult).then_inc(ve.h, 1)
                ve.v += 1
                for idx in range(n_kt):
                    bank = idx % nb
                    slot = idx // nb
                    v.wait_ge(mm.h, mm_tr(sub, idx))
                    if sub == 2 and idx == 0:
                        v.wait_ge(mm.h, mm_ctx(0, 7))
                    pt = ps[:, bank * 512:(bank + 1) * 512]
                    pt = pt.bitcast(BF16)[:, slot * 128:(slot + 1) * 128]
                    v.tensor_copy(
                        probsT[:, idx * 256 + (sub % 2) * 128:
                               idx * 256 + (sub % 2) * 128 + 128],
                        pt).then_inc(ve.h, 1)
                    ve.v += 1
                v.wait_ge(sc.h, sc_exp(sub))
                v.reciprocal(sm[:, 8 + sub:9 + sub],
                             sm[:, 4 + sub:5 + sub]).then_inc(ve.h, 1)
                ve.v += 1
                if sub % 2 == 1:
                    for ic in range(HP):
                        off = (ic % 2) * 512
                        v.wait_ge(mm.h, mm_ctx(pair, ic))
                        v.tensor_copy(
                            ctxb[:, ic * 512 + pair * 256:
                                 ic * 512 + pair * 256 + 256],
                            ps[:, off:off + 256]).then_inc(ve.h, 1)
                        ve.v += 1
            v.wait_ge(mm.h, mm_rsi)
            v.tensor_copy(rrow[0:1, 0:512],
                          ps[0:1, 1024:1536]).then_inc(ve.h, 1)
            ve.v += 1
            v.wait_ge(mm.h, mm_rsb)
            vch.i(v.tensor_copy(rsb[:, :], ps[:, 1536:2048]))
            vch.w(v)
            v.engine_nop().then_inc(ve.h, 1)
            ve.v += 1
            for m in range(HP):
                off = (m % 2) * 512
                v.wait_ge(mm.h, mm_wo(m))
                if m >= 2:
                    pm = (g * 8 + m - 2) % 2
                    stp.wait(v, (pm, b_st[pm] + ((m - 2) // 2 + 1) * 16))
                vch.w(v)
                vch.i(v.tensor_tensor(t1b[:, :], ps[:, off:off + 512],
                                      rsb[:, :], ALU.mult))
                vch.w(v)
                ld.wait(v, tok_xr[m])
                i2 = v.scalar_tensor_tensor(
                    xout[:, (m % 2) * 512:(m % 2) * 512 + 512], t1b[:, :],
                    boP[:, m:m + 1], xrg[:, (m % 2) * 512:(m % 2) * 512 + 512],
                    ALU.add, ALU.add)
                vch.i(i2)
                vch.w(v)
                v.engine_nop().then_inc(ve.h, 1)
                ve.v += 1

        @b.gpsimd
        def _(gp):
            for m in range(HP):
                gp.wait_ge(ve.h, ve_woe(m))
                stp.issue(gp, xres[m * 128:(m + 1) * 128,
                                   g * 512:(g + 1) * 512],
                          xout[:, (m % 2) * 512:(m % 2) * 512 + 512],
                          g * 8 + m)
            stp.wait_all(gp)

    assert mm.v == mm_wo(HP - 1), (mm.v, mm_wo(HP - 1))
    assert sc.v == b_sc + 4
    assert ve.v == ve_woe(HP - 1)




def emit_l1ffn(st):
    nc = st["nc"]
    n_tok = st["n_tok"]
    with (
        nc.sbuf_tensor(uname("lf_we"), [128, (n_tok // 128) * 8], F32) as we_sb,
        nc.sbuf_tensor(uname("lf_nw"), [128, 16], F32) as nw_sb,
    ):
        with (
            nc.sbuf_tensor(uname("lf_kc"), [128, 2 * n_tok], BF16) as kc_sb,
            nc.sbuf_tensor(uname("lf_vc"), [128, 2 * n_tok], BF16) as vc_sb,
        ):
            _l1_p0(st, kc_sb, vc_sb, nw_sb)
            _l1_p1(st, kc_sb, vc_sb)
        _l1_p2(st, we_sb, nw_sb)
        _l1_p3(st, we_sb)
        _l1_p4(st, nw_sb)


def _l1_rope(v, vch, ps_bank0, ps_bank1, roped, cos_t, sin_t, tmp):
    for n2, bank in ((0, ps_bank0), (1, ps_bank1)):
        qe = bank.rearrange("p (h two i) -> p h two i", two=2, i=32)[:, :, 0, :]
        qo = bank.rearrange("p (h two i) -> p h two i", two=2, i=32)[:, :, 1, :]
        cs = cos_t[:, n2 * 256:(n2 + 1) * 256].rearrange("p (h i) -> p h i", i=32)
        sn = sin_t[:, n2 * 256:(n2 + 1) * 256].rearrange("p (h i) -> p h i", i=32)
        t1 = tmp[:, 0:256].rearrange("p (h i) -> p h i", i=32)
        t2 = tmp[:, 256:512].rearrange("p (h i) -> p h i", i=32)
        ev = roped[:, n2 * 512:(n2 + 1) * 512] \
            .rearrange("p (h two i) -> p h two i", two=2, i=32)[:, :, 0, :]
        od = roped[:, n2 * 512:(n2 + 1) * 512] \
            .rearrange("p (h two i) -> p h two i", two=2, i=32)[:, :, 1, :]
        vch.w(v)
        vch.i(v.tensor_tensor(t1, qe, cs, ALU.mult))
        vch.i(v.tensor_tensor(t2, qo, sn, ALU.mult))
        vch.w(v)
        vch.i(v.tensor_tensor(ev, t1, t2, ALU.subtract))
        vch.w(v)
        vch.i(v.tensor_tensor(t1, qe, sn, ALU.mult))
        vch.i(v.tensor_tensor(t2, qo, cs, ALU.mult))
        vch.w(v)
        vch.i(v.tensor_tensor(od, t1, t2, ALU.add))


def _l1_p0(st, kc_sb, vc_sb, nw_sb):
    nc = st["nc"]
    w = st["w"]
    xres = st["xres"]
    ld, wl, stp, mm, sc, ve = (st["ld"], st["wl"], st["stp"], st["mm"],
                               st["sc"], st["ve"])
    vch = ChainCtx(st["vch"])
    sch = ChainCtx(st["sch"])
    ones_f = st["ones_f"]
    eye_bf = st["eye_bf"]
    n_tok = st["n_tok"]
    n_chunk = n_tok // 512
    b_mm, b_sc, b_ve = mm.v, sc.v, ve.v
    b_ld = ld.bases()
    b_wl = wl.bases()
    b_st = stp.bases()

    MMC = 10 + 4 * 10 + 4
    SCC = 9
    VEC = 3 + 4 * 9 + 4

    def s2_v(c, j):
        return b_mm + c * MMC + j + 1

    def st1_v(c):
        return b_mm + c * MMC + 9

    def bc_v(c):
        return b_mm + c * MMC + 10

    def qmm_v(c, t, n2):
        return b_mm + c * MMC + 10 + t * 10 + n2 + 1

    def qtr_v(c, t, i):
        return b_mm + c * MMC + 10 + t * 10 + 2 + i + 1

    def kcm_v(c, m):
        return b_mm + c * MMC + 50 + m + 1

    def vcm_v(c, m):
        return b_mm + c * MMC + 52 + m + 1

    def sq_v(c, j):
        return b_sc + c * SCC + j + 1

    def scA_v(c):
        return b_sc + c * SCC + 9

    def veA_v(c):
        return b_ve + c * VEC + 1

    def nrm_v(c):
        return b_ve + c * VEC + 2

    def xnf_v(c):
        return b_ve + c * VEC + 3

    def rope_v(c, t):
        return b_ve + c * VEC + 3 + t * 9 + 1

    def qtev_v(c, t, i):
        return b_ve + c * VEC + 3 + t * 9 + 1 + i + 1

    def kcev_v(c, m):
        return b_ve + c * VEC + 39 + m + 1

    def vcev_v(c, m):
        return b_ve + c * VEC + 41 + m + 1

    def ld_tok(c):
        return (c % 2, b_ld[c % 2] + (c // 2 + 1) * 16 * 9)

    def st_tok(c):
        return (c % 2, b_st[c % 2] + (c // 2 + 1) * 32)

    with (
        nc.sbuf_tensor(uname("p0_wq"), [128, HP * H], BF16) as wq,
        nc.sbuf_tensor(uname("p0_wkc"), [128, HP * 256], BF16) as wkc,
        nc.sbuf_tensor(uname("p0_wvc"), [128, HP * 256], BF16) as wvc,
        nc.sbuf_tensor(uname("p0_xf"), [128, HP * 512], F32) as xf,
        nc.sbuf_tensor(uname("p0_xq"), [128, 1024], F32) as xq,
        nc.sbuf_tensor(uname("p0_srow"), [128, 2048], F32) as srow,
        nc.sbuf_tensor(uname("p0_xn"), [128, HP * 512], BF16) as xnb,
        nc.sbuf_tensor(uname("p0_xnf"), [128, HP * 512], F32) as xnf,
        nc.sbuf_tensor(uname("p0_rope"), [128, 4 * 1024], F32) as rop,
        nc.sbuf_tensor(uname("p0_rop2"), [128, 1024], F32) as rtmp,
        nc.sbuf_tensor(uname("p0_rpd"), [128, 1024], BF16) as roped,
        nc.sbuf_tensor(uname("p0_qt"), [128, 2 * HP * 512], BF16) as qtb,
        nc.psum_tensor(uname("p0_ps_st"), [128, 1024], F32) as ps_st,
        nc.psum_tensor(uname("p0_ps_b"), [128, 1024], F32) as ps_b,
        nc.psum_tensor(uname("p0_ps_p"), [128, 1024], F32) as ps_p,
        nc.psum_tensor(uname("p0_ps_t"), [128, 1024], F32) as ps_t,
        nc.Block() as b,
    ):
        jobs = CacheJobs(st)

        @b.gpsimd
        def _(gp):
            wbf = st["wbf"]
            for nm, buf, tt in (("L1_ffn_Wq", wq, H), ("L1_ffn_Wk_c", wkc, 256),
                                ("L1_ffn_Wv_c", wvc, 256)):
                _wload(st, gp, wl,
                       buf[:, :].rearrange("p (c t) -> p c t", t=tt),
                       (nm,),
                       w[nm][:, :].rearrange("(c p) t -> p c t", p=128),
                       wbf[nm][:, :].rearrange("(c p) t -> p c t", p=128),
                       0, jobs, ready=(mm.h, s2_v(0, 0)))

        @b.sync
        def _(sy):
            for c in range(n_chunk):
                t0 = c * 512
                if c == 1:
                    _emit_cache_stores(st, sy, jobs)
                if c >= 1:
                    sy.wait_ge(ve.h, nrm_v(c - 1))
                    sy.wait_ge(ve.h, xnf_v(c - 1))
                ld.issue(sy, xf[:, :].rearrange("p (c t) -> p c t", t=512),
                         xres[:, t0:t0 + 512]
                         .rearrange("(c p) t -> p c t", p=128), c)
                for t in range(4):
                    if c >= 1:
                        sy.wait_ge(ve.h, rope_v(c - 1, t))
                    ld.issue(sy, rop[:, t * 1024:t * 1024 + 512],
                             st["rope_cos"][t0 + t * 128:t0 + (t + 1) * 128, :], c)
                    ld.issue(sy, rop[:, t * 1024 + 512:(t + 1) * 1024],
                             st["rope_sin"][t0 + t * 128:t0 + (t + 1) * 128, :], c)

        @b.scalar
        def _(s):
            for c in range(n_chunk):
                ld.wait(s, ld_tok(c))
                if c >= 1:
                    s.wait_ge(mm.h, st1_v(c - 1))
                for j in range(HP):
                    jp = j % 2
                    if j >= 2:
                        s.wait_ge(mm.h, s2_v(c, j - 2))
                    s.activation(xq[:, jp * 512:(jp + 1) * 512],
                                 xf[:, j * 512:(j + 1) * 512],
                                 AF.Square).then_inc(sc.h, 1)
                    sc.v += 1
                s.wait_ge(ve.h, veA_v(c))
                sch.i(s.activation(srow[0:1, 1536:2048],
                                   srow[0:1, 1024:1536], AF.Ln))
                sch.w(s)
                s.activation(srow[0:1, 1536:2048], srow[0:1, 1536:2048],
                             AF.Exp, scale=-0.5).then_inc(sc.h, 1)
                sc.v += 1

        @b.tensor
        def _(tn):
            wl.wait(tn, (0, b_wl[0] + 48))
            for c in range(n_chunk):
                par = c % 2
                if c >= 1:
                    tn.wait_ge(ve.h, veA_v(c - 1))
                    tn.wait_ge(ve.h, vcev_v(c - 1, 1))
                for j in range(HP):
                    jp = j % 2
                    tn.wait_ge(sc.h, sq_v(c, j))
                    tn.matmul(ps_st[0:1, 512:1024], ones_f[:, 0:1],
                              xq[:, jp * 512:(jp + 1) * 512],
                              start=(j == 0), stop=(j == HP - 1)).then_inc(mm.h, 1)
                    mm.v += 1
                i = None
                for j in range(HP):
                    i = tn.matmul(ps_st[0:1, 0:512], ones_f[:, 0:1],
                                  xf[:, j * 512:(j + 1) * 512],
                                  start=(j == 0), stop=(j == HP - 1))
                i.then_inc(mm.h, 1)
                mm.v += 1
                tn.wait_ge(sc.h, scA_v(c))
                if c >= 1:
                    tn.wait_ge(ve.h, nrm_v(c - 1))
                tn.matmul(ps_b[:, 0:512], st["bmat0"][:, :], srow[:, 0:512],
                          start=True, stop=True)
                tn.matmul(ps_b[:, 512:1024], st["bmat0"][:, :],
                          srow[:, 1536:2048], start=True,
                          stop=True).then_inc(mm.h, 1)
                mm.v += 1
                tn.wait_ge(ve.h, nrm_v(c))
                for t in range(4):
                    for n2 in range(2):
                        if not (c == 0 and t == 0):
                            pc, pt_ = (c, t - 1) if t > 0 else (c - 1, 3)
                            tn.wait_ge(ve.h, rope_v(pc, pt_))
                        i = None
                        for k in range(HP):
                            i = tn.matmul(
                                ps_p[:, n2 * 512:(n2 + 1) * 512],
                                xnb[:, k * 512 + t * 128: k * 512 + (t + 1) * 128],
                                wq[:, k * H + n2 * 512: k * H + (n2 + 1) * 512],
                                start=(k == 0), stop=(k == HP - 1))
                        i.then_inc(mm.h, 1)
                        mm.v += 1
                    tn.wait_ge(ve.h, rope_v(c, t))
                    for i8 in range(HP):
                        bank = i8 % 2
                        slot = i8 // 2
                        if i8 >= 2:
                            tn.wait_ge(ve.h, qtev_v(c, t, i8 - 2))
                        pt_ap = ps_t[:, bank * 512:(bank + 1) * 512]
                        pt_ap = pt_ap.bitcast(BF16)[:, slot * 128:(slot + 1) * 128]
                        if i8 >= 2:
                            pass
                        tn.transpose(pt_ap,
                                     roped[:, i8 * 128:(i8 + 1) * 128],
                                     eye_bf[:, :]).then_inc(mm.h, 1)
                        mm.v += 1
                tn.wait_ge(ve.h, rope_v(c, 3))
                for m in range(2):
                    if c >= 1:
                        tn.wait_ge(ve.h, kcev_v(c - 1, m))
                    i = None
                    for k in range(HP):
                        i = tn.matmul(ps_p[:, m * 512:m * 512 + 512],
                                      wkc[:, k * 256 + m * 128: k * 256 + (m + 1) * 128],
                                      xnb[:, k * 512:(k + 1) * 512],
                                      start=(k == 0), stop=(k == HP - 1))
                    i.then_inc(mm.h, 1)
                    mm.v += 1
                tn.wait_ge(ve.h, veA_v(c))
                for m in range(2):
                    if c >= 1:
                        tn.wait_ge(ve.h, vcev_v(c - 1, m))
                    i = None
                    for k in range(HP):
                        i = tn.matmul(ps_st[:, m * 512:m * 512 + 512],
                                      wvc[:, k * 256 + m * 128: k * 256 + (m + 1) * 128],
                                      xnb[:, k * 512:(k + 1) * 512],
                                      start=(k == 0), stop=(k == HP - 1))
                    i.then_inc(mm.h, 1)
                    mm.v += 1

        @b.vector
        def _(v):
            vch.i(v.memset(srow[:, :], 0.0))
            for c in range(n_chunk):
                par = c % 2
                v.wait_ge(mm.h, st1_v(c))
                vch.w(v)
                vch.i(v.tensor_scalar(srow[0:1, 0:512], ps_st[0:1, 0:512],
                                      1.0 / H, None, ALU.mult))
                vch.i(v.tensor_scalar(srow[0:1, 512:1024], ps_st[0:1, 512:1024],
                                      1.0 / H, None, ALU.mult))
                vch.w(v)
                vch.i(v.tensor_tensor(srow[0:1, 1024:1536], srow[0:1, 0:512],
                                      srow[0:1, 0:512], ALU.mult))
                vch.w(v)
                vch.i(v.tensor_tensor(srow[0:1, 1024:1536], srow[0:1, 512:1024],
                                      srow[0:1, 1024:1536], ALU.subtract))
                vch.w(v)
                v.tensor_scalar(srow[0:1, 1024:1536], srow[0:1, 1024:1536],
                                1e-5, None, ALU.add).then_inc(ve.h, 1)
                ve.v += 1
                v.wait_ge(mm.h, bc_v(c))
                if c >= 2:
                    stp.wait(v, st_tok(c - 2))
                last = None
                for j in range(HP):
                    vch.i(v.tensor_tensor(xf[:, j * 512:(j + 1) * 512],
                                          xf[:, j * 512:(j + 1) * 512],
                                          ps_b[:, 0:512], ALU.subtract))
                    vch.w(v)
                    last = v.tensor_tensor(
                        xnb[:, j * 512:(j + 1) * 512],
                        xf[:, j * 512:(j + 1) * 512],
                        ps_b[:, 512:1024], ALU.mult)
                last.then_inc(ve.h, 1)
                ve.v += 1
                last = None
                if c >= 1:
                    stp.wait(v, st_tok(c - 1))
                for j in range(HP):
                    vch.w(v)
                    last = v.tensor_tensor(
                        xnf[:, j * 512:(j + 1) * 512],
                        xf[:, j * 512:(j + 1) * 512],
                        ps_b[:, 512:1024], ALU.mult)
                last.then_inc(ve.h, 1)
                ve.v += 1
                for t in range(4):
                    v.wait_ge(mm.h, qmm_v(c, t, 1))
                    _l1_rope(v, vch, ps_p[:, 0:512], ps_p[:, 512:1024],
                             roped, rop[:, t * 1024:t * 1024 + 512],
                             rop[:, t * 1024 + 512:(t + 1) * 1024], rtmp)
                    vch.w(v)
                    v.engine_nop().then_inc(ve.h, 1)
                    ve.v += 1
                    for i8 in range(HP):
                        bank = i8 % 2
                        slot = i8 // 2
                        v.wait_ge(mm.h, qtr_v(c, t, i8))
                        pt_ap = ps_t[:, bank * 512:(bank + 1) * 512]
                        pt_ap = pt_ap.bitcast(BF16)[:, slot * 128:(slot + 1) * 128]
                        v.tensor_copy(
                            qtb[:, par * 4096 + i8 * 512 + t * 128:
                                par * 4096 + i8 * 512 + (t + 1) * 128],
                            pt_ap).then_inc(ve.h, 1)
                        ve.v += 1
                for m in range(2):
                    v.wait_ge(mm.h, kcm_v(c, m))
                    v.tensor_copy(kc_sb[:, m * n_tok + c * 512:
                                        m * n_tok + (c + 1) * 512],
                                  ps_p[:, m * 512:m * 512 + 512]).then_inc(ve.h, 1)
                    ve.v += 1
                for m in range(2):
                    v.wait_ge(mm.h, vcm_v(c, m))
                    v.tensor_copy(vc_sb[:, m * n_tok + c * 512:
                                        m * n_tok + (c + 1) * 512],
                                  ps_st[:, m * 512:m * 512 + 512]).then_inc(ve.h, 1)
                    ve.v += 1

        @b.gpsimd
        def _(gp):
            for c in range(n_chunk):
                par = c % 2
                t0 = c * 512
                gp.wait_ge(ve.h, xnf_v(c))
                stp.issue(gp, st["xnf_hbm"][:, t0:t0 + 512]
                          .rearrange("(c p) t -> p c t", p=128),
                          xnf[:, :].rearrange("p (c t) -> p c t", t=512), c)
                gp.wait_ge(ve.h, qtev_v(c, 3, HP - 1))
                stp.issue(gp, st["qt_hbm"][:, t0:t0 + 512]
                          .rearrange("(c p) t -> p c t", p=128),
                          qtb[:, par * 4096:(par + 1) * 4096]
                          .rearrange("p (c t) -> p c t", t=512), c)
            stp.wait_all(gp)

    assert mm.v == b_mm + n_chunk * MMC, (mm.v, b_mm, MMC)
    assert sc.v == b_sc + n_chunk * SCC
    assert ve.v == b_ve + n_chunk * VEC, (ve.v, b_ve, VEC)


def _l1_p1(st, kc_sb, vc_sb):
    nc = st["nc"]
    w = st["w"]
    ld, wl, stp, mm, sc, ve = (st["ld"], st["wl"], st["stp"], st["mm"],
                               st["sc"], st["ve"])
    vch = ChainCtx(st["vch"])
    eye_bf = st["eye_bf"]
    n_tok = st["n_tok"]
    n_chunk = n_tok // 512
    b_mm, b_ve = mm.v, ve.v
    b_ld = ld.bases()
    b_wl = wl.bases()
    b_st = stp.bases()

    MMC = 48
    VEC = 44

    def kmm_v(c, t, n2):
        return b_mm + c * MMC + t * 12 + n2 + 1

    def ktr_v(c, t, i):
        return b_mm + c * MMC + t * 12 + 2 + i + 1

    def vmm_v(c, t, n2):
        return b_mm + c * MMC + t * 12 + 10 + n2 + 1

    def rope_v(c, t):
        return b_ve + c * VEC + t * 11 + 1

    def ktev_v(c, t, i):
        return b_ve + c * VEC + t * 11 + 1 + i + 1

    def vev_v(c, t, n2):
        return b_ve + c * VEC + t * 11 + 9 + n2 + 1

    def ld_tok(c):
        return (c % 2, b_ld[c % 2] + (c // 2 + 1) * 16 * 8)

    def st_tok(c):
        return (c % 2, b_st[c % 2] + (c // 2 + 1) * 32 * 16)

    with (
        nc.sbuf_tensor(uname("p1_wk"), [128, 2 * H], BF16) as wk,
        nc.sbuf_tensor(uname("p1_wv"), [128, 2 * H], BF16) as wv,
        nc.sbuf_tensor(uname("p1_rope"), [128, 4 * 1024], F32) as rop,
        nc.sbuf_tensor(uname("p1_rop2"), [128, 1024], F32) as rtmp,
        nc.sbuf_tensor(uname("p1_rpd"), [128, 1024], BF16) as roped,
        nc.sbuf_tensor(uname("p1_kt"), [128, 2 * HP * 512], BF16) as ktb,
        nc.sbuf_tensor(uname("p1_vt"), [128, 2 * 4 * 1024], BF16) as vtb,
        nc.psum_tensor(uname("p1_ps_p"), [128, 1024], F32) as ps_p,
        nc.psum_tensor(uname("p1_ps_v"), [128, 1024], F32) as ps_v,
        nc.psum_tensor(uname("p1_ps_t"), [128, 1024], F32) as ps_t,
        nc.Block() as b,
    ):
        jobs = CacheJobs(st)

        @b.gpsimd
        def _(gp):
            wbf = st["wbf"]
            for nm, buf in (("L1_ffn_Wk", wk), ("L1_ffn_Wv", wv)):
                _wload(st, gp, wl,
                       buf[:, :].rearrange("p (c t) -> p c t", t=H),
                       (nm,),
                       w[nm][:, :].rearrange("(c p) t -> p c t", p=128),
                       wbf[nm][:, :].rearrange("(c p) t -> p c t", p=128),
                       0, jobs, ready=(mm.h, kmm_v(0, 0, 0)))

        @b.sync
        def _(sy):
            for c in range(n_chunk):
                t0 = c * 512
                if c == 1:
                    _emit_cache_stores(st, sy, jobs)
                for t in range(4):
                    if c >= 1:
                        sy.wait_ge(ve.h, rope_v(c - 1, t))
                    ld.issue(sy, rop[:, t * 1024:t * 1024 + 512],
                             st["rope_cos"][t0 + t * 128:t0 + (t + 1) * 128, :], c)
                    ld.issue(sy, rop[:, t * 1024 + 512:(t + 1) * 1024],
                             st["rope_sin"][t0 + t * 128:t0 + (t + 1) * 128, :], c)

        @b.tensor
        def _(tn):
            wl.wait(tn, (0, b_wl[0] + 32))
            for c in range(n_chunk):
                for t in range(4):
                    for n2 in range(2):
                        if not (c == 0 and t == 0):
                            pc, pt_ = (c, t - 1) if t > 0 else (c - 1, 3)
                            tn.wait_ge(ve.h, rope_v(pc, pt_))
                        i = None
                        for k2 in range(2):
                            i = tn.matmul(
                                ps_p[:, n2 * 512:(n2 + 1) * 512],
                                kc_sb[:, k2 * n_tok + c * 512 + t * 128:
                                      k2 * n_tok + c * 512 + (t + 1) * 128],
                                wk[:, k2 * H + n2 * 512: k2 * H + (n2 + 1) * 512],
                                start=(k2 == 0), stop=(k2 == 1))
                        i.then_inc(mm.h, 1)
                        mm.v += 1
                    tn.wait_ge(ve.h, rope_v(c, t))
                    for i8 in range(HP):
                        bank = i8 % 2
                        slot = i8 // 2
                        if i8 >= 2:
                            tn.wait_ge(ve.h, ktev_v(c, t, i8 - 2))
                        pt_ap = ps_t[:, bank * 512:(bank + 1) * 512]
                        pt_ap = pt_ap.bitcast(BF16)[:, slot * 128:(slot + 1) * 128]
                        tn.transpose(pt_ap,
                                     roped[:, i8 * 128:(i8 + 1) * 128],
                                     eye_bf[:, :]).then_inc(mm.h, 1)
                        mm.v += 1
                    for n2 in range(2):
                        if not (c == 0 and t == 0):
                            pc, pt_ = (c, t - 1) if t > 0 else (c - 1, 3)
                            tn.wait_ge(ve.h, vev_v(pc, pt_, n2))
                        i = None
                        for k2 in range(2):
                            i = tn.matmul(
                                ps_v[:, n2 * 512:(n2 + 1) * 512],
                                vc_sb[:, k2 * n_tok + c * 512 + t * 128:
                                      k2 * n_tok + c * 512 + (t + 1) * 128],
                                wv[:, k2 * H + n2 * 512: k2 * H + (n2 + 1) * 512],
                                start=(k2 == 0), stop=(k2 == 1))
                        i.then_inc(mm.h, 1)
                        mm.v += 1

        @b.vector
        def _(v):
            for c in range(n_chunk):
                par = c % 2
                for t in range(4):
                    v.wait_ge(mm.h, kmm_v(c, t, 1))
                    ld.wait(v, ld_tok(c))
                    if c >= 2:
                        stp.wait(v, st_tok(c - 2))
                    _l1_rope(v, vch, ps_p[:, 0:512], ps_p[:, 512:1024],
                             roped, rop[:, t * 1024:t * 1024 + 512],
                             rop[:, t * 1024 + 512:(t + 1) * 1024], rtmp)
                    vch.w(v)
                    v.engine_nop().then_inc(ve.h, 1)
                    ve.v += 1
                    for i8 in range(HP):
                        bank = i8 % 2
                        slot = i8 // 2
                        v.wait_ge(mm.h, ktr_v(c, t, i8))
                        pt_ap = ps_t[:, bank * 512:(bank + 1) * 512]
                        pt_ap = pt_ap.bitcast(BF16)[:, slot * 128:(slot + 1) * 128]
                        v.tensor_copy(
                            ktb[:, par * 4096 + i8 * 512 + t * 128:
                                par * 4096 + i8 * 512 + (t + 1) * 128],
                            pt_ap).then_inc(ve.h, 1)
                        ve.v += 1
                    for n2 in range(2):
                        v.wait_ge(mm.h, vmm_v(c, t, n2))
                        v.tensor_copy(vtb[:, par * 4096 + t * 1024 + n2 * 512:
                                          par * 4096 + t * 1024 + (n2 + 1) * 512],
                                      ps_v[:, n2 * 512:(n2 + 1) * 512]) \
                            .then_inc(ve.h, 1)
                        ve.v += 1

        @b.gpsimd
        def _(gp):
            kt_bd, v_bd = st["kt_bd"], st["v_bd"]
            for c in range(n_chunk):
                par = c % 2
                if c == 0:
                    st["cs"].wait_all(gp)
                gp.wait_ge(ve.h, ktev_v(c, 3, HP - 1))
                for hh in range(8):
                    stp.issue(gp, kt_bd[hh, 8 * c:8 * c + 8, 0:64, 0:64]
                              .rearrange("c p t -> p c t"),
                              ktb[0:64, par * 4096 + hh * 512:
                                  par * 4096 + (hh + 1) * 512]
                              .rearrange("p (c t) -> p c t", t=64), c)
                    stp.issue(gp, kt_bd[hh, 8 * c:8 * c + 8, 64:128, 64:128]
                              .rearrange("c p t -> p c t"),
                              ktb[64:128, par * 4096 + hh * 512:
                                  par * 4096 + (hh + 1) * 512]
                              .rearrange("p (c t) -> p c t", t=64), c)
                gp.wait_ge(ve.h, vev_v(c, 3, 1))
                for half in range(2):
                    for t in range(4):
                        src = vtb[half * 64:(half + 1) * 64,
                                  par * 4096 + t * 1024:
                                  par * 4096 + (t + 1) * 1024] \
                            .rearrange("p (hh d) -> p hh d", d=64)
                        kb = 8 * c + 2 * t + half
                        stp.issue(gp, v_bd[0:8, kb, 0:64, 0:64]
                                  .rearrange("h p t -> p h t"),
                                  src[:, 0::2, :], c)
                        stp.issue(gp, v_bd[0:8, kb, 64:128, 64:128]
                                  .rearrange("h p t -> p h t"),
                                  src[:, 1::2, :], c)
            stp.wait_all(gp)

    assert mm.v == b_mm + n_chunk * MMC
    assert ve.v == b_ve + n_chunk * VEC


def _l1_p2(st, we_sb, nw_sb):
    nc = st["nc"]
    w = st["w"]
    ld, wl, stp, mm, sc, ve = (st["ld"], st["wl"], st["stp"], st["mm"],
                               st["sc"], st["ve"])
    ones_f, eye_f = st["ones_f"], st["eye_f"]
    n_tok = st["n_tok"]
    n_kt = n_tok // 128
    n_grp = n_tok // 512

    KB = n_tok // 64
    with (
        nc.sbuf_tensor(uname("p2_kt"), [128, 2 * KB * 128], BF16) as ktw,
        nc.sbuf_tensor(uname("p2_v"), [128, 2 * KB * 128], BF16) as vw,
        nc.sbuf_tensor(uname("p2_obd"), [128, 128], F32) as obd,
        nc.sbuf_tensor(uname("p2_osum"), [128, 2], BF16) as osum,
        nc.sbuf_tensor(uname("p2_wo"), [128, HP * H], BF16) as wo,
        nc.sbuf_tensor(uname("p2_qt"), [128, 2 * HP * 512], BF16) as qtg,
        nc.sbuf_tensor(uname("p2_ptc"), [128, 2048], BF16) as ptc,
        nc.sbuf_tensor(uname("p2_x1"), [128, HP * 512], F32) as x1sb,
        nc.sbuf_tensor(uname("p2_xnf"), [128, 2 * 512], F32) as xnfc,
        nc.sbuf_tensor(uname("p2_rr"), [128, 512], F32) as rrow,
        nc.sbuf_tensor(uname("p2_rb"), [128, 512], F32) as rbsb,
        nc.sbuf_tensor(uname("p2_sq"), [128, 1024], F32) as sqtmp,
        nc.sbuf_tensor(uname("p2_wg"), [128, HP * 10], F32) as wgc_sb,
        nc.sbuf_tensor(uname("p2_gb"), [128, 16], F32) as gb_sb,
        nc.sbuf_tensor(uname("p2_gs"), [128, 192], F32) as gsc,
        nc.sbuf_tensor(uname("p2_br"), [128, 1056], F32) as brow,
    ):
        with (
            nc.psum_tensor(uname("p2_pp"), [128, 1024], F32) as pp,
            nc.Block() as b,
        ):
            b_mm0, b_ve0 = mm.v, ve.v
            b_ld0 = ld.bases()
            jobs = CacheJobs(st)

            tok_wo = [None]

            @b.gpsimd
            def _(gp):
                tok_wo[0], _ = _wload(
                    st, gp, wl,
                    wo[:, :].rearrange("p (c t) -> p c t", t=H),
                    ("L1_ffn_Wo",),
                    w["L1_ffn_Wo"][:, :]
                    .rearrange("(c p) t -> p c t", p=128),
                    st["wbf"]["L1_ffn_Wo"][:, :]
                    .rearrange("(c p) t -> p c t", p=128), 0, jobs)

            @b.sync
            def _(sy):
                for c in range(HP):
                    ld.issue(sy, wgc_sb[:, c * 10:c * 10 + 2],
                             w["L1_ffn_Wg_gate"][c * 128:(c + 1) * 128, :], 0)
                    ld.issue(sy, wgc_sb[:, c * 10 + 2:c * 10 + 10],
                             w["L1_ffn_We_gate"][c * 128:(c + 1) * 128, :], 0)
                ld.issue(sy, brow[0:1, 0:1024], w["L1_ffn_norm1_w"][None, :], 0)
                ld.issue(sy, brow[32:33, 0:2], w["L1_ffn_group_bias"][None, :], 0)
                ld.issue(sy, brow[64:65, 0:8],
                         w["L1_ffn_expert_bias"][None, :], 0)
                ld.issue(sy, obd[:, 0:128], st["obd_c"][:, :], 0)
                tok = ld.issue(sy, brow[32:33, 2:1026],
                               w["L1_ffn_norm2_w"][None, :], 0)
                ld.wait(sy, tok)
                _emit_cache_stores(st, sy, jobs)

            @b.tensor
            def _(tn):
                ld.wait(tn, (0, b_ld0[0] + 21 * 16))
                wl.wait(tn, tok_wo[0])
                i = None
                for c in range(HP):
                    i = tn.matmul(pp[:, c:c + 1],
                                  brow[0:1, c * 128:(c + 1) * 128],
                                  ones_f[0:1, 0:1], start=True, stop=True)
                    i = tn.matmul(pp[:, 8 + c:9 + c],
                                  brow[32:33, 2 + c * 128:2 + (c + 1) * 128],
                                  ones_f[32:33, 0:1], start=True, stop=True)
                tn.matmul(pp[:, 512:514], ones_f[32:33, :], brow[32:33, 0:2],
                          start=True, stop=True)
                i = tn.matmul(pp[:, 514:522], ones_f[64:65, :], brow[64:65, 0:8],
                              start=True, stop=True)
                i.then_inc(mm.h, 1)
                mm.v += 1

            @b.vector
            def _(v):
                vchp = ChainCtx(st["vch"])
                vchp.i(v.memset(osum[:, :], 0.0))
                vchp.w(v)
                v.memset(osum[0:64, 0:1], 1.0)
                v.memset(osum[64:128, 1:2], 1.0)
                v.memset(rrow[:, :], 0.0)
                v.wait_ge(mm.h, b_mm0 + 1)
                v.tensor_copy(nw_sb[:, 0:16], pp[:, 0:16])
                v.tensor_copy(gb_sb[:, 0:2], pp[:, 512:514])
                v.tensor_copy(gb_sb[:, 2:10], pp[:, 514:522]).then_inc(ve.h, 1)
                ve.v += 1

        st["p2q"] = {}
        for g in range(n_grp):
            _l1_p2_group(st, g, n_grp, ktw, vw, obd, osum, wo, qtg, ptc, x1sb,
                         xnfc, rrow, rbsb, sqtmp, wgc_sb, gb_sb, gsc,
                         we_sb, nw_sb, n_tok)


def _l1_p2_group(st, g, n_grp, ktw, vw, obd, osum, wo, qtg, ptc, x1sb,
                 xnfc, rrow, rbsb, sqtmp, wgc_sb, gb_sb, gsc, we_sb, nw_sb,
                 n_tok):
    nc = st["nc"]
    ld, wl, stp, mm, sc, ve = (st["ld"], st["wl"], st["stp"], st["mm"],
                               st["sc"], st["ve"])
    wl2 = st["wl2"]
    cs = st["cs"]
    vch = ChainCtx(st["vch"])
    sch = ChainCtx(st["sch"])
    ones_f, eye_f = st["ones_f"], st["eye_f"]
    KB = n_tok // 64
    K2 = KB // 2
    NP = 8
    b_mm, b_sc, b_ve = mm.v, sc.v, ve.v
    b_wl2 = wl2.bases()
    sl = (g % 2) * 4096

    PM = 3 * KB + 1

    def m_sco(p, kb):
        idx = kb if kb < 4 else 3 * kb - 8
        return b_mm + p * PM + idx + 1

    def m_ctx(p, kb):
        idx = 3 * kb + 5 if kb <= KB - 5 else 3 * KB - 8 + (kb - KB + 4) * 2
        return b_mm + p * PM + idx + 1

    def m_sum(p, kb):
        idx = 3 * kb + 6 if kb <= KB - 5 else 3 * KB - 7 + (kb - KB + 4) * 2
        return b_mm + p * PM + idx + 1

    def m_rbm(p):
        return b_mm + (p + 1) * PM

    def m_wo(m):
        return b_mm + NP * PM + m + 1

    def m_s2(m):
        return b_mm + NP * PM + 8 + m + 1

    m_bc = b_mm + NP * PM + 17

    def m_gmm(sub):
        return m_bc + sub + 1

    def s_ep(gsp):
        return b_sc + gsp + 1

    def s_sq(m):
        return b_sc + NP * K2 + m + 1

    s_rstd = b_sc + NP * K2 + 9

    def s_gex(sub):
        return s_rstd + sub + 1

    def v_rcp(p):
        return b_ve + p * 2 + 1

    def v_sat(p):
        return b_ve + p * 2 + 2

    def v_x1p(m):
        return b_ve + NP * 2 + m + 1

    v_sta = b_ve + NP * 2 + 9
    v_x1f = v_sta + 1

    def v_gv(sub):
        return v_x1f + sub + 1

    def v_gf(sub):
        return v_x1f + 4 + sub + 1

    with (
        nc.psum_tensor(uname("pg_ps"), [128, 4096], F32) as ps,
        nc.Block() as b,
    ):
        tok_xn = []
        tok_kv = []

        @b.sync
        def _(sy):
            if g == 0:
                st["p2q"][0] = cs.issue(
                    sy, qtg[:, 0:4096].rearrange("p (c t) -> p c t", t=512),
                    st["qt_hbm"][:, 0:512]
                    .rearrange("(c p) t -> p c t", p=128), 0)
                if n_grp > 1:
                    st["p2q"][1] = cs.issue(
                        sy, qtg[:, 4096:8192]
                        .rearrange("p (c t) -> p c t", t=512),
                        st["qt_hbm"][:, 512:1024]
                        .rearrange("(c p) t -> p c t", p=128), 1)
            elif g + 1 < n_grp:
                nsl = ((g + 1) % 2) * 4096
                st["p2q"][g + 1] = cs.issue(
                    sy, qtg[:, nsl:nsl + 4096]
                    .rearrange("p (c t) -> p c t", t=512),
                    st["qt_hbm"][:, (g + 1) * 512:(g + 2) * 512]
                    .rearrange("(c p) t -> p c t", p=128), g + 1)
            for m in range(HP):
                if m >= 2:
                    sy.wait_ge(ve.h, v_x1p(m - 2))
                tok_xn.append(ld.issue(
                    sy, xnfc[:, (m % 2) * 512:(m % 2) * 512 + 512],
                    st["xnf_hbm"][m * 128:(m + 1) * 128, g * 512:(g + 1) * 512],
                    2 * g + m))

        @b.gpsimd
        def _(gp):
            for p in range(NP):
                par = p % 2
                if p >= 2:
                    gp.wait_ge(mm.h, m_rbm(p - 2))
                wl2.issue(gp, ktw[:, par * KB * 128:(par + 1) * KB * 128]
                          .rearrange("p (c t) -> p c t", t=128),
                          st["kt_bd"][p].rearrange("c p t -> p c t"), p)
                tok_kv.append(wl2.issue(
                    gp, vw[:, par * KB * 128:(par + 1) * KB * 128]
                    .rearrange("p (c t) -> p c t", t=128),
                    st["v_bd"][p].rearrange("c p t -> p c t"), p))
            gp.wait_ge(ve.h, v_x1f)
            stp.issue(gp, st["x1f_hbm"][:, g * 512:(g + 1) * 512]
                      .rearrange("(c p) t -> p c t", p=128),
                      x1sb[:, :].rearrange("p (c t) -> p c t", t=512), 2 * g)
            stp.issue(gp, st["x1b_hbm"][:, g * 512:(g + 1) * 512]
                      .rearrange("(c p) t -> p c t", p=128),
                      x1sb[:, :].rearrange("p (c t) -> p c t", t=512), 2 * g + 1)
            stp.wait_all(gp)

        @b.scalar
        def _(s):
            for p in range(NP):
                for j in range(K2):
                    gsp = p * K2 + j
                    s.wait_ge(mm.h, m_sco(p, 2 * j + 1))
                    if gsp >= 2:
                        pc = (gsp - 2) * 2 + 1
                        s.wait_ge(mm.h, m_sum(pc // KB, pc % KB))
                    s.activation(ptc[:, (gsp % 2) * 1024:(gsp % 2) * 1024 + 1024],
                                 ps[:, (j % 2) * 1024:(j % 2) * 1024 + 1024],
                                 AF.Exp, scale=0.125).then_inc(sc.h, 1)
                    sc.v += 1
            for m in range(HP):
                jp = m % 2
                s.wait_ge(ve.h, v_x1p(m))
                if m >= 2:
                    s.wait_ge(mm.h, m_s2(m - 2))
                s.activation(sqtmp[:, jp * 512:(jp + 1) * 512],
                             x1sb[:, m * 512:(m + 1) * 512],
                             AF.Square).then_inc(sc.h, 1)
                sc.v += 1
            s.wait_ge(ve.h, v_sta)
            sch.i(s.activation(rrow[32:33, 0:512], rrow[0:1, 0:512], AF.Ln))
            sch.w(s)
            s.activation(rrow[32:33, 0:512], rrow[32:33, 0:512],
                         AF.Exp, scale=-0.5).then_inc(sc.h, 1)
            sc.v += 1
            for sub in range(4):
                s.wait_ge(ve.h, v_gv(sub))
                mx = gsc[:, 64 + sub * 8:64 + sub * 8 + 1]
                sch.i(s.activation(gsc[:, sub * 8:sub * 8 + 8],
                                   gsc[:, 32 + sub * 8:32 + sub * 8 + 8],
                                   AF.Exp, bias=mx))
                s.activation(gsc[:, 64 + sub * 8 + 1:64 + sub * 8 + 2],
                             gsc[:, 64 + sub * 8 + 2:64 + sub * 8 + 3],
                             AF.Exp).then_inc(sc.h, 1)
                sc.v += 1

        @b.tensor
        def _(tn):
            ld.wait(tn, tok_xn[0])
            cs.wait(tn, st["p2q"][g])
            for p in range(NP):
                par = p % 2
                kbase = par * KB * 128
                wl2.wait(tn, tok_kv[p])

                def _sco(kb, p=p, kbase=kbase):
                    gsp = p * K2 + kb // 2
                    if gsp >= 2:
                        tn.wait_ge(sc.h, s_ep(gsp - 2))
                    tn.matmul(ps[:, (kb % 4) * 512:(kb % 4) * 512 + 512],
                              ktw[:, kbase + kb * 128:kbase + (kb + 1) * 128],
                              qtg[:, sl + p * 512:sl + (p + 1) * 512],
                              start=True, stop=True).then_inc(mm.h, 1)
                    mm.v += 1

                def _cs(kb, p=p, kbase=kbase):
                    gsp = p * K2 + kb // 2
                    po = (gsp % 2) * 1024 + (kb % 2) * 512
                    if kb == 0 and p >= 1:
                        tn.wait_ge(ve.h, v_sat(p - 1))
                    tn.wait_ge(sc.h, s_ep(gsp))
                    tn.matmul(ps[:, 2048:2560],
                              vw[:, kbase + kb * 128:kbase + (kb + 1) * 128],
                              ptc[:, po:po + 512],
                              start=(kb == 0), stop=(kb == KB - 1)) \
                        .then_inc(mm.h, 1)
                    mm.v += 1
                    if kb == 0 and p >= 1:
                        tn.wait_ge(ve.h, v_rcp(p - 1))
                    tn.matmul(ps[64:66, 2560:3072], osum[:, 0:2],
                              ptc[:, po:po + 512],
                              start=(kb == 0), stop=(kb == KB - 1),
                              tile_position=(0, 64)) \
                        .then_inc(mm.h, 1)
                    mm.v += 1

                for kb in range(4):
                    _sco(kb)
                for kb in range(4, KB):
                    _sco(kb)
                    _cs(kb - 4)
                for kb in range(KB - 4, KB):
                    _cs(kb)
                tn.wait_ge(ve.h, v_rcp(p))
                tn.matmul(ps[:, 3072:3584], obd[:, :], rrow[:, 0:512],
                          start=True, stop=True).then_inc(mm.h, 1)
                mm.v += 1
            tn.wait_ge(ve.h, v_sat(NP - 1))
            for m in range(HP):
                off = (m % 2) * 512
                if m >= 2:
                    tn.wait_ge(ve.h, v_x1p(m - 2))
                i = None
                for k in range(HP):
                    i = tn.matmul(ps[:, off:off + 512],
                                  wo[:, k * H + m * 128: k * H + (m + 1) * 128],
                                  qtg[:, sl + k * 512:sl + (k + 1) * 512],
                                  start=(k == 0), stop=(k == HP - 1))
                i.then_inc(mm.h, 1)
                mm.v += 1
            for m in range(HP):
                jp = m % 2
                tn.wait_ge(sc.h, s_sq(m))
                tn.matmul(ps[0:1, 1024:1536], ones_f[:, 0:1],
                          sqtmp[:, jp * 512:(jp + 1) * 512],
                          start=(m == 0), stop=(m == HP - 1)).then_inc(mm.h, 1)
                mm.v += 1
            tn.wait_ge(sc.h, s_rstd)
            tn.matmul(ps[:, 1536:2048], st["bmat32"][:, :], rrow[:, 0:512],
                      start=True, stop=True).then_inc(mm.h, 1)
            mm.v += 1
            tn.wait_ge(ve.h, v_x1f)
            for sub in range(4):
                if sub >= 1:
                    tn.wait_ge(ve.h, v_gv(sub - 1))
                i = None
                for k in range(HP):
                    i = tn.matmul(ps[:, 3072 + sub * 16:3072 + sub * 16 + 10],
                                  x1sb[:, k * 512 + sub * 128:
                                       k * 512 + (sub + 1) * 128],
                                  wgc_sb[:, k * 10:(k + 1) * 10],
                                  start=(k == 0), stop=(k == HP - 1))
                i.then_inc(mm.h, 1)
                mm.v += 1

        @b.vector
        def _(v):
            for p in range(NP):
                v.wait_ge(mm.h, m_sum(p, KB - 1))
                v.reciprocal(rrow[64:66, 0:512],
                             ps[64:66, 2560:3072]).then_inc(ve.h, 1)
                ve.v += 1
                v.wait_ge(mm.h, m_rbm(p))
                vch.i(v.tensor_copy(rbsb[:, 0:512], ps[:, 3072:3584]))
                vch.w(v)
                v.tensor_tensor(qtg[:, sl + p * 512:sl + (p + 1) * 512],
                                ps[:, 2048:2560], rbsb[:, 0:512],
                                ALU.mult).then_inc(ve.h, 1)
                ve.v += 1
            for m in range(HP):
                off = (m % 2) * 512
                v.wait_ge(mm.h, m_wo(m))
                ld.wait(v, tok_xn[m])
                v.tensor_tensor(x1sb[:, m * 512:(m + 1) * 512],
                                ps[:, off:off + 512],
                                xnfc[:, (m % 2) * 512:(m % 2) * 512 + 512],
                                ALU.add).then_inc(ve.h, 1)
                ve.v += 1
            v.wait_ge(mm.h, m_s2(HP - 1))
            v.tensor_scalar(rrow[0:1, 0:512], ps[0:1, 1024:1536],
                            1.0 / H, 1e-6, ALU.mult, ALU.add).then_inc(ve.h, 1)
            ve.v += 1
            v.wait_ge(mm.h, m_bc)
            last = None
            for m in range(HP):
                vch.w(v)
                vch.i(v.tensor_tensor(x1sb[:, m * 512:(m + 1) * 512],
                                      x1sb[:, m * 512:(m + 1) * 512],
                                      ps[:, 1536:2048], ALU.mult))
                vch.w(v)
                last = v.tensor_scalar(x1sb[:, m * 512:(m + 1) * 512],
                                       x1sb[:, m * 512:(m + 1) * 512],
                                       nw_sb[:, m:m + 1], None, ALU.mult)
                vch.i(last)
            vch.w(v)
            v.engine_nop().then_inc(ve.h, 1)
            ve.v += 1
            for sub in range(4):
                gl = gsc[:, 96 + sub * 4:96 + sub * 4 + 2]
                el = gsc[:, 32 + sub * 8:32 + sub * 8 + 8]
                mx = gsc[:, 64 + sub * 8:64 + sub * 8 + 1]
                zz = gsc[:, 64 + sub * 8 + 1:64 + sub * 8 + 2]
                dgl = gsc[:, 64 + sub * 8 + 2:64 + sub * 8 + 3]
                v.wait_ge(mm.h, m_gmm(sub))
                vch.i(v.tensor_tensor(gl, ps[:, 3072 + sub * 16:
                                             3072 + sub * 16 + 2],
                                      gb_sb[:, 0:2], ALU.add))
                vch.i(v.tensor_tensor(el, ps[:, 3072 + sub * 16 + 2:
                                             3072 + sub * 16 + 10],
                                      gb_sb[:, 2:10], ALU.add))
                vch.w(v)
                vch.i(v.reduce_max(mx, el, axis=AX.X, negate=True))
                last = v.tensor_tensor(dgl, gl[:, 1:2], gl[:, 0:1], ALU.subtract)
                vch.i(last)
                vch.w(v)
                v.engine_nop().then_inc(ve.h, 1)
                ve.v += 1
            for sub in range(4):
                exps = gsc[:, sub * 8:sub * 8 + 8]
                gl = gsc[:, 96 + sub * 4:96 + sub * 4 + 2]
                zz = gsc[:, 64 + sub * 8 + 1:64 + sub * 8 + 2]
                s0 = gsc[:, 96 + sub * 4 + 2:96 + sub * 4 + 3]
                s1 = gsc[:, 96 + sub * 4 + 3:96 + sub * 4 + 4]
                mcol = gsc[:, 112 + sub * 4:112 + sub * 4 + 1]
                gp0 = gsc[:, 112 + sub * 4 + 1:112 + sub * 4 + 2]
                q = gsc[:, 112 + sub * 4 + 2:112 + sub * 4 + 3]
                t0c = gsc[:, 112 + sub * 4 + 3:112 + sub * 4 + 4]
                v.wait_ge(sc.h, s_gex(sub))
                vch.w(v)
                vch.i(v.reduce_sum(s0, exps[:, 0:4], axis=AX.X))
                vch.i(v.reduce_sum(s1, exps[:, 4:8], axis=AX.X))
                vch.i(v.tensor_tensor(mcol, gl[:, 0:1], gl[:, 1:2], ALU.is_ge))
                vch.i(v.tensor_scalar(gp0, zz, 1.0, None, ALU.add))
                vch.w(v)
                vch.i(v.reciprocal(gp0, gp0))
                vch.w(v)
                vch.i(v.tensor_scalar(t0c, gp0, 2.0, -1.0, ALU.mult, ALU.add))
                vch.w(v)
                vch.i(v.tensor_scalar(t0c, mcol, t0c, None, ALU.mult))
                vch.w(v)
                vch.i(v.tensor_scalar(gp0, gp0, -1.0, 1.0, ALU.mult, ALU.add))
                vch.w(v)
                vch.i(v.tensor_tensor(gp0, gp0, t0c, ALU.add))
                vch.w(v)
                vch.i(v.tensor_tensor(t0c, s0, s1, ALU.subtract))
                vch.w(v)
                vch.i(v.tensor_scalar(t0c, mcol, t0c, None, ALU.mult))
                vch.w(v)
                vch.i(v.tensor_tensor(t0c, s1, t0c, ALU.add))
                vch.w(v)
                vch.i(v.reciprocal(q, t0c))
                vch.w(v)
                vch.i(v.tensor_tensor(q, gp0, q, ALU.mult))
                vch.w(v)
                vch.i(v.tensor_scalar(s0, mcol, q, None, ALU.mult))
                vch.w(v)
                vch.i(v.tensor_tensor(s1, q, s0, ALU.subtract))
                vch.w(v)
                vch.i(v.tensor_scalar(exps[:, 0:4], exps[:, 0:4], s0, None,
                                      ALU.mult))
                vch.i(v.tensor_scalar(exps[:, 4:8], exps[:, 4:8], s1, None,
                                      ALU.mult))
                vch.w(v)
                m1 = gsc[:, 112 + sub * 4:112 + sub * 4 + 1]
                vch.i(v.reduce_max(m1, exps, axis=AX.X))
                vch.w(v)
                eq1 = gsc[:, 128 + sub * 8:128 + sub * 8 + 8]
                vch.i(v.tensor_scalar(eq1, exps, m1, None, ALU.is_equal))
                vch.w(v)
                p2 = gsc[:, 32 + sub * 8:32 + sub * 8 + 8]
                vch.i(v.scalar_tensor_tensor(p2, eq1, NEG_BIG, exps,
                                             ALU.mult, ALU.add))
                vch.w(v)
                vch.i(v.reduce_max(m1, p2, axis=AX.X))
                vch.w(v)
                vch.i(v.tensor_scalar(p2, p2, m1, None, ALU.is_equal))
                vch.w(v)
                vch.i(v.tensor_tensor(eq1, eq1, p2, ALU.add))
                vch.w(v)
                vch.i(v.tensor_tensor(eq1, exps, eq1, ALU.mult))
                vch.w(v)
                ctile = g * 4 + sub
                last = v.tensor_scalar(we_sb[:, ctile * 8:(ctile + 1) * 8],
                                       eq1, 0.5, None, ALU.mult)
                vch.i(last)
                vch.w(v)
                v.engine_nop().then_inc(ve.h, 1)
                ve.v += 1

    assert mm.v == m_gmm(3), (mm.v, m_gmm(3))
    assert sc.v == s_gex(3)
    assert ve.v == v_gf(3)


def _l1_p3(st, we_sb):
    nc = st["nc"]
    n_tok = st["n_tok"]
    n_half = n_tok // 2
    for half in range(2):
        with (
            nc.sbuf_tensor(uname("p3_st2"), [128, 16 * n_half], BF16) as sT2,
            nc.sbuf_tensor(uname("p3_tmp"), [128, 2 * HP * 512], F32) as tmp,
        ):
            _l1_u_block(st, half, n_half, sT2, tmp, kind="shared")
            _l1_u_block(st, half, n_half, sT2, tmp, kind="h2")
            _l1_routed_block(st, half, n_half, sT2, tmp, we_sb)


def _l1_u_block(st, half, n_half, sT2, tmp, kind):
    nc = st["nc"]
    w = st["w"]
    ld, wl, stp, mm, sc, ve = (st["ld"], st["wl"], st["stp"], st["mm"],
                               st["sc"], st["ve"])
    wl2 = st["wl2"]
    vch = ChainCtx(st["vch"])
    n_q = n_half // 512
    wname = "L1_ffn_W_shared_in" if kind == "shared" else "L1_ffn_W1_shared"
    b_mm, b_sc, b_ve = mm.v, sc.v, ve.v
    b_ld = ld.bases()
    b_wl = wl.bases()
    b_st = stp.bases()

    n_out = 32 if kind == "shared" else 0
    MMQ = 32 + n_out

    def a_v(q, hc, m):
        return b_mm + q * MMQ + hc * 8 + m + 1

    def b_v(q, hc, m):
        return b_mm + q * MMQ + hc * 8 + 4 + m + 1

    def o_v(q, hc2, m):
        return b_mm + q * MMQ + 32 + hc2 * 8 + m + 1

    def sil_v(q, hc, m):
        return b_sc + q * 16 + hc * 4 + m + 1

    VEQ = 16 + (8 if kind == "shared" else 0)

    def prod_v(q, hc, m):
        return b_ve + q * VEQ + hc * 4 + m + 1

    def oev_v(q, m):
        return b_ve + q * VEQ + 16 + m + 1

    def ld_tok(q):
        return (q % 2, b_ld[q % 2] + (q // 2 + 1) * 16)

    with (
        nc.sbuf_tensor(uname("u_x1b"), [128, 2 * HP * 512], BF16) as x1b,
        nc.sbuf_tensor(uname("u_wa"), [128, 2 * HP * 512], BF16) as wab,
        nc.sbuf_tensor(uname("u_wb"), [128, 2 * HP * 512], BF16) as wbb,
        nc.sbuf_tensor(uname("u_wso"), [128, 2 * 4 * 1024], BF16) as wso,
        nc.sbuf_tensor(uname("u_sil"), [128, 4 * 512], BF16) as silb,
        nc.sbuf_tensor(uname("u_sT"), [128, 16 * 512], BF16) as sTl,
        nc.psum_tensor(uname("u_ps"), [128, 4096], F32) as ps,
        nc.Block() as b,
    ):
        jobs = CacheJobs(st)
        slot_ptoks = {}
        oslot_ptoks = {}
        toks_w = []
        toks_o = []

        @b.gpsimd
        def _(gp):
            wbf = st["wbf"]
            for q in range(n_q):
                for hc in range(4):
                    fg = q * 4 + hc
                    if fg >= 2:
                        gp.wait_ge(mm.h, b_v(*divmod(fg - 2, 4), 3))
                        for pt in slot_ptoks.get(fg - 2, ()):
                            gp.wait_ge(st["cs"].c[pt[0]].h, pt[1])
                    par = fg % 2
                    rdy = (mm.h, a_v(q, hc, 0))
                    _, p1 = _wload(
                        st, gp, wl,
                        wab[:, par * 4096:(par + 1) * 4096]
                        .rearrange("p (c t) -> p c t", t=512),
                        (wname, hc, "a"),
                        w[wname][:, hc * 512:(hc + 1) * 512]
                        .rearrange("(c p) t -> p c t", p=128),
                        wbf[wname][:, hc * 512:(hc + 1) * 512]
                        .rearrange("(c p) t -> p c t", p=128), fg, jobs,
                        ready=rdy)
                    tk, p2 = _wload(
                        st, gp, wl,
                        wbb[:, par * 4096:(par + 1) * 4096]
                        .rearrange("p (c t) -> p c t", t=512),
                        (wname, hc, "b"),
                        w[wname][:, 2048 + hc * 512:2048 + (hc + 1) * 512]
                        .rearrange("(c p) t -> p c t", p=128),
                        wbf[wname][:, 2048 + hc * 512:2048 + (hc + 1) * 512]
                        .rearrange("(c p) t -> p c t", p=128), fg, jobs,
                        ready=rdy)
                    toks_w.append(tk)
                    slot_ptoks[fg] = [p for p in (p1, p2) if p is not None]
                if kind == "shared":
                    for hc2 in range(4):
                        fo = q * 4 + hc2
                        if fo >= 2:
                            gp.wait_ge(mm.h, o_v(*divmod(fo - 2, 4), 7))
                            for pt in oslot_ptoks.get(fo - 2, ()):
                                gp.wait_ge(st["cs"].c[pt[0]].h, pt[1])
                        tk, p3 = _wload(
                            st, gp, wl2,
                            wso[:, (fo % 2) * 4096:(fo % 2 + 1) * 4096]
                            .rearrange("p (c t) -> p c t", t=1024),
                            ("wso", hc2),
                            w["L1_ffn_W_shared_out"]
                            [hc2 * 512:(hc2 + 1) * 512, :]
                            .rearrange("(c p) t -> p c t", p=128),
                            st["wbf"]["L1_ffn_W_shared_out"]
                            [hc2 * 512:(hc2 + 1) * 512, :]
                            .rearrange("(c p) t -> p c t", p=128), fo, jobs,
                            ready=(mm.h, o_v(q, hc2, 0)))
                        toks_o.append(tk)
                        oslot_ptoks[fo] = [p3] if p3 is not None else []
                    gp.wait_ge(ve.h, oev_v(q, HP - 1))
                    stp.issue(gp, st["acc_hbm"][:, half * n_half + q * 512:
                                                half * n_half + (q + 1) * 512]
                              .rearrange("(c p) t -> p c t", p=128),
                              tmp[:, 0:4096]
                              .rearrange("p (c t) -> p c t", t=512), q)
            stp.wait_all(gp)

        @b.sync
        def _(sy):
            emitted = False
            for q in range(n_q):
                if q >= 2:
                    sy.wait_ge(mm.h, b_mm + (q - 1) * MMQ)
                ld.issue(sy, x1b[:, (q % 2) * 4096:(q % 2) * 4096 + 4096]
                         .rearrange("p (c t) -> p c t", t=512),
                         st["x1b_hbm"][:, half * n_half + q * 512:
                                       half * n_half + (q + 1) * 512]
                         .rearrange("(c p) t -> p c t", p=128), q)
                if q == 1:
                    _emit_cache_stores(st, sy, jobs)
                    emitted = True
            if not emitted:
                _emit_cache_stores(st, sy, jobs)

        @b.tensor
        def _(tn):
            for q in range(n_q):
                qpar = q % 2
                ld.wait(tn, ld_tok(q))
                for hc in range(4):
                    fg = q * 4 + hc
                    par = fg % 2
                    wl.wait(tn, toks_w[fg])
                    for m in range(4):
                        if hc >= 1:
                            tn.wait_ge(sc.h, sil_v(q, hc - 1, m))
                        elif q >= 1:
                            if kind == "shared":
                                tn.wait_ge(ve.h, oev_v(q - 1, m))
                            else:
                                tn.wait_ge(sc.h, sil_v(q - 1, 3, m))
                        i = None
                        for k in range(HP):
                            i = tn.matmul(
                                ps[:, m * 512:(m + 1) * 512],
                                wab[:, par * 4096 + k * 512 + m * 128:
                                    par * 4096 + k * 512 + (m + 1) * 128],
                                x1b[:, qpar * 4096 + k * 512:
                                    qpar * 4096 + (k + 1) * 512],
                                start=(k == 0), stop=(k == HP - 1))
                        i.then_inc(mm.h, 1)
                        mm.v += 1
                    for m in range(4):
                        if hc >= 1:
                            tn.wait_ge(ve.h, prod_v(q, hc - 1, m))
                        elif q >= 1:
                            if kind == "shared":
                                tn.wait_ge(ve.h, oev_v(q - 1, 4 + m))
                            else:
                                tn.wait_ge(ve.h, prod_v(q - 1, 3, m))
                        i = None
                        for k in range(HP):
                            i = tn.matmul(
                                ps[:, 2048 + m * 512:2048 + (m + 1) * 512],
                                wbb[:, par * 4096 + k * 512 + m * 128:
                                    par * 4096 + k * 512 + (m + 1) * 128],
                                x1b[:, qpar * 4096 + k * 512:
                                    qpar * 4096 + (k + 1) * 512],
                                start=(k == 0), stop=(k == HP - 1))
                        i.then_inc(mm.h, 1)
                        mm.v += 1
                if kind == "shared":
                    tn.wait_ge(ve.h, prod_v(q, 3, 3))
                    for hc2 in range(4):
                        fo = q * 4 + hc2
                        wl2.wait(tn, toks_o[fo])
                        for m in range(HP):
                            i = None
                            for k4 in range(4):
                                i = tn.matmul(
                                    ps[:, m * 512:(m + 1) * 512],
                                    wso[:, (fo % 2) * 4096 + k4 * 1024 + m * 128:
                                        (fo % 2) * 4096 + k4 * 1024 + (m + 1) * 128],
                                    sTl[:, (hc2 * 4 + k4) * 512:
                                        (hc2 * 4 + k4 + 1) * 512],
                                    start=(hc2 == 0 and k4 == 0),
                                    stop=(hc2 == 3 and k4 == 3))
                            i.then_inc(mm.h, 1)
                            mm.v += 1

        @b.scalar
        def _(s):
            for q in range(n_q):
                for hc in range(4):
                    fg = q * 4 + hc
                    for m in range(4):
                        s.wait_ge(mm.h, a_v(q, hc, m))
                        if fg >= 1:
                            pfg = fg - 1
                            s.wait_ge(ve.h, prod_v(pfg // 4, pfg % 4, m))
                        s.activation(silb[:, m * 512:(m + 1) * 512],
                                     ps[:, m * 512:(m + 1) * 512],
                                     AF.Silu).then_inc(sc.h, 1)
                        sc.v += 1

        @b.vector
        def _(v):
            for q in range(n_q):
                for hc in range(4):
                    for m in range(4):
                        v.wait_ge(mm.h, b_v(q, hc, m))
                        v.wait_ge(sc.h, sil_v(q, hc, m))
                        dst = (sTl[:, (hc * 4 + m) * 512:(hc * 4 + m + 1) * 512]
                               if kind == "shared" else
                               sT2[:, (hc * 4 + m) * n_half + q * 512:
                                   (hc * 4 + m) * n_half + (q + 1) * 512])
                        v.tensor_tensor(dst, silb[:, m * 512:(m + 1) * 512],
                                        ps[:, 2048 + m * 512:
                                           2048 + (m + 1) * 512],
                                        ALU.mult).then_inc(ve.h, 1)
                        ve.v += 1
                if kind == "shared":
                    for m in range(HP):
                        v.wait_ge(mm.h, o_v(q, 3, m))
                        if q >= 2:
                            stp.wait(v, (q % 2, b_st[q % 2] + ((q - 2) // 2 + 1) * 16))
                        v.tensor_copy(tmp[:, m * 512:(m + 1) * 512],
                                      ps[:, m * 512:(m + 1) * 512]) \
                            .then_inc(ve.h, 1)
                        ve.v += 1

    assert mm.v == b_mm + n_q * MMQ, (mm.v, b_mm)
    assert sc.v == b_sc + n_q * 16
    assert ve.v == b_ve + n_q * VEQ


def _l1_routed_block(st, half, n_half, sT2, tmp, we_sb):
    nc = st["nc"]
    w = st["w"]
    ld, wl, stp, mm, sc, ve = (st["ld"], st["wl"], st["stp"], st["mm"],
                               st["sc"], st["ve"])
    vch = ChainCtx(st["vch"])
    ones_f, eye_f = st["ones_f"], st["eye_f"]
    n_q = n_half // 512
    b_mm, b_ve = mm.v, ve.v
    b_st = stp.bases()

    def fe(pp, q, ei):
        return (pp * n_q + q) * 2 + ei

    def m_wr(pp, q, ei):
        return b_mm + fe(pp, q, ei) * 10 + 1

    def m_cb(pp, q, ei):
        return b_mm + fe(pp, q, ei) * 10 + 2

    def m_out(pp, q, ei, m):
        return b_mm + fe(pp, q, ei) * 10 + 2 + m + 1

    VEPQ = 21

    def v_wr(pp, q, ei):
        return b_ve + (pp * n_q + q) * VEPQ + ei * 10 + 1

    def v_cb(pp, q, ei):
        return b_ve + (pp * n_q + q) * VEPQ + ei * 10 + 2

    def v_m(pp, q, ei, m):
        return b_ve + (pp * n_q + q) * VEPQ + ei * 10 + 2 + m + 1

    def v_fin(pp, q):
        return b_ve + (pp * n_q + q + 1) * VEPQ

    def st_tok(pp, q):
        idx = pp * n_q + q
        return (idx % 2, b_st[idx % 2] + (idx // 2 + 1) * 16)

    def _prev(f):
        pq, ei = divmod(f, 2)
        pp, q = divmod(pq, n_q)
        return pp, q, ei

    with (
        nc.sbuf_tensor(uname("r_w2"), [128, 2 * 16 * 1024], BF16) as w2b,
        nc.sbuf_tensor(uname("r_wr"), [128, 512], F32) as wrow,
        nc.sbuf_tensor(uname("r_cb"), [128, 512], F32) as cbsb,
        nc.sbuf_tensor(uname("r_scr"), [128, HP * 512], F32) as scr,
        nc.psum_tensor(uname("r_ps"), [128, 4096], F32) as ps,
        nc.Block() as b,
    ):
        toks_w2 = []
        jobs = CacheJobs(st)
        slot_ptoks = {}

        @b.gpsimd
        def _(gp):
            for pp in range(4):
                ptl = []
                for ei in range(2):
                    e = 2 * pp + ei
                    if pp >= 1:
                        gp.wait_ge(mm.h, m_out(pp - 1, n_q - 1, ei, 7))
                        for pt in slot_ptoks.get((pp - 1, ei), ()):
                            gp.wait_ge(st["cs"].c[pt[0]].h, pt[1])
                    tk, p1 = _wload(
                        st, gp, wl,
                        w2b[:, ei * 16384:(ei + 1) * 16384]
                        .rearrange("p (c t) -> p c t", t=1024),
                        ("w2e", e),
                        w["L1_ffn_W2_experts"][e, :, :]
                        .rearrange("(c p) t -> p c t", p=128),
                        st["wbf"]["L1_ffn_W2_experts"][e, :, :]
                        .rearrange("(c p) t -> p c t", p=128), pp, jobs,
                        ready=(mm.h, m_out(pp, 0, ei, 0)))
                    toks_w2.append(tk)
                    slot_ptoks[(pp, ei)] = [p1] if p1 is not None else []
                for q in range(n_q):
                    gp.wait_ge(ve.h, v_fin(pp, q))
                    stp.issue(gp, st["acc_hbm"][:, half * n_half + q * 512:
                                                half * n_half + (q + 1) * 512]
                              .rearrange("(c p) t -> p c t", p=128),
                              tmp[:, ((pp * n_q + q) % 2) * 4096:
                                  ((pp * n_q + q) % 2) * 4096 + 4096]
                              .rearrange("p (c t) -> p c t", t=512),
                              pp * n_q + q, accum=True)
            stp.wait_all(gp)

        if jobs.items:
            @b.sync
            def _(sy):
                _emit_cache_stores(st, sy, jobs)

        @b.tensor
        def _(tn):
            for pp in range(4):
                wl.wait(tn, toks_w2[2 * pp + 1])
                for q in range(n_q):
                    for ei in range(2):
                        f = fe(pp, q, ei)
                        if f >= 1:
                            tn.wait_ge(ve.h, v_m(*_prev(f - 1), 0))
                        i = None
                        for t4 in range(4):
                            gt = (half * n_half + q * 512) // 128 + t4
                            ecol = 2 * pp + ei
                            i = tn.matmul(ps[0:1, t4 * 128:(t4 + 1) * 128],
                                          we_sb[:, gt * 8 + ecol:
                                                gt * 8 + ecol + 1],
                                          eye_f[:, :], start=True, stop=True)
                        i.then_inc(mm.h, 1)
                        mm.v += 1
                        tn.wait_ge(ve.h, v_wr(pp, q, ei))
                        if f >= 1:
                            tn.wait_ge(ve.h, v_m(*_prev(f - 1), 1))
                        tn.matmul(ps[:, 512:1024], st["bmat0"][:, :],
                                  wrow[:, 0:512], start=True,
                                  stop=True).then_inc(mm.h, 1)
                        mm.v += 1
                        tn.wait_ge(ve.h, v_cb(pp, q, ei))
                        for m in range(HP):
                            if f >= 1 and m >= 2:
                                tn.wait_ge(ve.h, v_m(*_prev(f - 1), m))
                            i = None
                            for k in range(16):
                                i = tn.matmul(
                                    ps[:, m * 512:(m + 1) * 512],
                                    w2b[:, ei * 16384 + k * 1024 + m * 128:
                                        ei * 16384 + k * 1024 + (m + 1) * 128],
                                    sT2[:, k * n_half + q * 512:
                                        k * n_half + (q + 1) * 512],
                                    start=(k == 0), stop=(k == 15))
                            i.then_inc(mm.h, 1)
                            mm.v += 1

        @b.vector
        def _(v):
            vch.i(v.memset(wrow[:, :], 0.0))
            for pp in range(4):
                for q in range(n_q):
                    tsl = ((pp * n_q + q) % 2) * 4096
                    for ei in range(2):
                        v.wait_ge(mm.h, m_wr(pp, q, ei))
                        vch.w(v)
                        v.tensor_copy(wrow[0:1, 0:512],
                                      ps[0:1, 0:512]).then_inc(ve.h, 1)
                        ve.v += 1
                        v.wait_ge(mm.h, m_cb(pp, q, ei))
                        vch.i(v.tensor_copy(cbsb[:, :], ps[:, 512:1024]))
                        vch.w(v)
                        v.engine_nop().then_inc(ve.h, 1)
                        ve.v += 1
                        if ei == 0:
                            if pp >= 1:
                                stp.wait(v, st_tok(pp - 1, q))
                            for m in range(HP):
                                v.wait_ge(mm.h, m_out(pp, q, 0, m))
                                vch.w(v)
                                v.tensor_tensor(
                                    tmp[:, tsl + m * 512:tsl + (m + 1) * 512],
                                    ps[:, m * 512:(m + 1) * 512],
                                    cbsb[:, :], ALU.mult).then_inc(ve.h, 1)
                                ve.v += 1
                        else:
                            for m in range(HP):
                                v.wait_ge(mm.h, m_out(pp, q, 1, m))
                                vch.w(v)
                                vch.i(v.tensor_tensor(
                                    scr[:, m * 512:(m + 1) * 512],
                                    ps[:, m * 512:(m + 1) * 512],
                                    cbsb[:, :], ALU.mult))
                                vch.w(v)
                                v.engine_nop().then_inc(ve.h, 1)
                                ve.v += 1
                            v.wait_ge(ve.h, v_m(pp, q, 0, HP - 1))
                            for m in range(HP):
                                vch.w(v)
                                vch.i(v.tensor_tensor(
                                    tmp[:, tsl + m * 512:tsl + (m + 1) * 512],
                                    tmp[:, tsl + m * 512:tsl + (m + 1) * 512],
                                    scr[:, m * 512:(m + 1) * 512],
                                    ALU.add))
                            vch.w(v)
                            v.engine_nop().then_inc(ve.h, 1)
                            ve.v += 1

    assert mm.v == b_mm + 8 * n_q * 10
    assert ve.v == b_ve + 4 * n_q * VEPQ, (ve.v, b_ve)


def _l1_p4(st, nw_sb):
    nc = st["nc"]
    w = st["w"]
    ld, stp, mm, sc, ve = st["ld"], st["stp"], st["mm"], st["sc"], st["ve"]
    vch = ChainCtx(st["vch"])
    sch = ChainCtx(st["sch"])
    ones_f = st["ones_f"]
    n_tok = st["n_tok"]
    n_chunk = n_tok // 512
    b_mm, b_sc, b_ve = mm.v, sc.v, ve.v
    b_ld = ld.bases()
    b_st = stp.bases()

    def s2_v(c, j):
        return b_mm + c * 9 + j + 1

    def bc_v(c):
        return b_mm + c * 9 + 9

    def sq_v(c, j):
        return b_sc + c * 9 + j + 1

    def rstd_v(c):
        return b_sc + c * 9 + 9

    def sum_v(c):
        return b_ve + c * 3 + 1

    def sta_v(c):
        return b_ve + c * 3 + 2

    def fin_v(c):
        return b_ve + c * 3 + 3

    def ld_tok(c):
        return (c % 2, b_ld[c % 2] + (c // 2 + 1) * 48)

    def st_tok(c):
        return (c % 2, b_st[c % 2] + (c // 2 + 1) * 16)

    with (
        nc.sbuf_tensor(uname("p4_x1"), [128, HP * 512], F32) as x1c,
        nc.sbuf_tensor(uname("p4_ac"), [128, HP * 512], F32) as acc,
        nc.sbuf_tensor(uname("p4_xr"), [128, HP * 512], F32) as xrc,
        nc.sbuf_tensor(uname("p4_sq"), [128, 1024], F32) as sqtmp,
        nc.sbuf_tensor(uname("p4_sr"), [128, 1024], F32) as srow,
        nc.psum_tensor(uname("p4_ps"), [128, 1024], F32) as ps,
        nc.Block() as b,
    ):
        @b.sync
        def _(sy):
            for c in range(n_chunk):
                t0 = c * 512
                if c >= 1:
                    sy.wait_ge(ve.h, fin_v(c - 1))
                    stp.wait(sy, st_tok(c - 1))
                ld.issue(sy, x1c[:, :].rearrange("p (c t) -> p c t", t=512),
                         st["x1f_hbm"][:, t0:t0 + 512]
                         .rearrange("(c p) t -> p c t", p=128), c)
                ld.issue(sy, acc[:, :].rearrange("p (c t) -> p c t", t=512),
                         st["acc_hbm"][:, t0:t0 + 512]
                         .rearrange("(c p) t -> p c t", p=128), c)
                ld.issue(sy, xrc[:, :].rearrange("p (c t) -> p c t", t=512),
                         st["xres"][:, t0:t0 + 512]
                         .rearrange("(c p) t -> p c t", p=128), c)

        @b.scalar
        def _(s):
            for c in range(n_chunk):
                s.wait_ge(ve.h, sum_v(c))
                for j in range(HP):
                    jp = j % 2
                    if j >= 2:
                        s.wait_ge(mm.h, s2_v(c, j - 2))
                    s.activation(sqtmp[:, jp * 512:(jp + 1) * 512],
                                 x1c[:, j * 512:(j + 1) * 512],
                                 AF.Square).then_inc(sc.h, 1)
                    sc.v += 1
                s.wait_ge(ve.h, sta_v(c))
                sch.i(s.activation(srow[0:1, 512:1024], srow[0:1, 0:512],
                                   AF.Ln))
                sch.w(s)
                s.activation(srow[0:1, 512:1024], srow[0:1, 512:1024],
                             AF.Exp, scale=-0.5).then_inc(sc.h, 1)
                sc.v += 1

        @b.tensor
        def _(tn):
            for c in range(n_chunk):
                if c >= 1:
                    tn.wait_ge(ve.h, sta_v(c - 1))
                for j in range(HP):
                    jp = j % 2
                    tn.wait_ge(sc.h, sq_v(c, j))
                    tn.matmul(ps[0:1, 0:512], ones_f[:, 0:1],
                              sqtmp[:, jp * 512:(jp + 1) * 512],
                              start=(j == 0), stop=(j == HP - 1)).then_inc(mm.h, 1)
                    mm.v += 1
                tn.wait_ge(sc.h, rstd_v(c))
                if c >= 1:
                    tn.wait_ge(ve.h, fin_v(c - 1))
                tn.matmul(ps[:, 512:1024], st["bmat0"][:, :],
                          srow[:, 512:1024],
                          start=True, stop=True).then_inc(mm.h, 1)
                mm.v += 1

        @b.vector
        def _(v):
            vch.i(v.memset(srow[:, :], 0.0))
            for c in range(n_chunk):
                ld.wait(v, ld_tok(c))
                last = None
                for j in range(HP):
                    last = v.tensor_tensor(x1c[:, j * 512:(j + 1) * 512],
                                           x1c[:, j * 512:(j + 1) * 512],
                                           acc[:, j * 512:(j + 1) * 512],
                                           ALU.add)
                    vch.i(last)
                vch.w(v)
                v.engine_nop().then_inc(ve.h, 1)
                ve.v += 1
                v.wait_ge(mm.h, s2_v(c, HP - 1))
                vch.w(v)
                v.tensor_scalar(srow[0:1, 0:512], ps[0:1, 0:512],
                                1.0 / H, 1e-6, ALU.mult, ALU.add).then_inc(ve.h, 1)
                ve.v += 1
                v.wait_ge(mm.h, bc_v(c))
                if c >= 2:
                    stp.wait(v, st_tok(c - 2))
                last = None
                for j in range(HP):
                    vch.w(v)
                    vch.i(v.tensor_tensor(x1c[:, j * 512:(j + 1) * 512],
                                          x1c[:, j * 512:(j + 1) * 512],
                                          ps[:, 512:1024], ALU.mult))
                    vch.w(v)
                    vch.i(v.tensor_scalar(x1c[:, j * 512:(j + 1) * 512],
                                          x1c[:, j * 512:(j + 1) * 512],
                                          nw_sb[:, 8 + j:9 + j], None, ALU.mult))
                    vch.w(v)
                    last = v.scalar_tensor_tensor(
                        xrc[:, j * 512:(j + 1) * 512],
                        x1c[:, j * 512:(j + 1) * 512], 0.5,
                        xrc[:, j * 512:(j + 1) * 512], ALU.mult, ALU.add)
                    vch.i(last)
                vch.w(v)
                v.engine_nop().then_inc(ve.h, 1)
                ve.v += 1

        @b.gpsimd
        def _(gp):
            for c in range(n_chunk):
                t0 = c * 512
                gp.wait_ge(ve.h, fin_v(c))
                stp.issue(gp, st["xres"][:, t0:t0 + 512]
                          .rearrange("(c p) t -> p c t", p=128),
                          xrc[:, :].rearrange("p (c t) -> p c t", t=512), c)
            stp.wait_all(gp)

    assert mm.v == b_mm + n_chunk * 9
    assert sc.v == b_sc + n_chunk * 9
    assert ve.v == b_ve + n_chunk * 3




def _host_consts():
    import ml_dtypes
    eye = np.eye(128, dtype=np.float32)
    dh = 64
    inv_freq = (1.0 / 10000.0 ** (np.arange(0, dh, 2, dtype=np.float64) / dh))
    ang = np.arange(T, dtype=np.float64)[:, None] * inv_freq[None, :]
    sin_rep = np.tile(np.sin(ang), (1, 16)).astype(np.float32)
    cos_rep = np.tile(np.cos(ang), (1, 16)).astype(np.float32)
    obd = np.zeros((128, 128), dtype=np.float32)
    obd[64, 0:64] = 1.0
    obd[65, 64:128] = 1.0
    return {
        "eye_bf": eye.astype(ml_dtypes.bfloat16).view(np.uint16),
        "eye_f": eye,
        "rope_sin": sin_rep,
        "rope_cos": cos_rep,
        "obd_c": obd,
    }


ALL_PARTS = ("l0ffn_a", "l0attn", "l0ffn_b", "l1ffn_a", "l1attn", "l1ffn_b")


def run_parts(parts, inputs, n_cores=8):
    consts = _host_consts()
    x_full = np.ascontiguousarray(np.asarray(inputs["L0_attn_x"], dtype=np.float32))
    in_maps = []
    for c in range(n_cores):
        m = {"x": x_full[c]}
        for k, v in inputs.items():
            if k == "L0_attn_x":
                continue
            m[k] = np.ascontiguousarray(np.asarray(v, dtype=np.float32))
        m.update(consts)
        in_maps.append(m)
    nc = build_nc(parts)
    res = run_bass_kernel_spmd(nc, in_maps, core_ids=list(range(n_cores)))
    return np.stack([np.asarray(r["out"]) for r in res.results], axis=0)


def kernel(**inputs):
    return run_parts(ALL_PARTS, inputs, n_cores=8)


# version 08
# speedup vs baseline: 1.1204x; 1.1204x over previous
import contextlib
import numpy as np
import concourse.bass as bass
import concourse.mybir as mybir
from concourse.bass_utils import run_bass_kernel_spmd

F32 = mybir.dt.float32
BF16 = mybir.dt.bfloat16
AF = mybir.ActivationFunctionType
ALU = mybir.AluOpType
AX = mybir.AxisListType

H = 1024
T = 4096
HP = H // 128
NEG_BIG = -1.0e30

_UID = [0]


def uname(s):
    _UID[0] += 1
    return f"{s}{_UID[0]}"


class Ctr:
    def __init__(self, h):
        self.h = h
        self.v = 0


class Pipe:

    def __init__(self, ctrs):
        self.c = ctrs
        self.k = len(ctrs)

    def issue(self, eng, dst, src, group, accum=False):
        c = self.c[group % self.k]
        if accum:
            eng.dma_start(dst, src, accum_op=ALU.add).then_inc(c.h, 16)
        else:
            eng.dma_start(dst, src).then_inc(c.h, 16)
        c.v += 16
        return (group % self.k, c.v)

    def wait(self, eng, tok):
        p, v = tok
        eng.wait_ge(self.c[p].h, v)

    def wait_all(self, eng):
        for c in self.c:
            eng.wait_ge(c.h, c.v)

    def bases(self):
        return [c.v for c in self.c]


class CacheJobs:

    def __init__(self, st):
        self.base = st["cs"].bases()
        self.items = []


def _wload(st, gp, pipe, dst_ap, key, f32_ap, bf_ap, group, jobs,
           ready=None):
    cs = st["cs"]
    if key in st["cached"]:
        for (pp, pr, vv) in st["cstok"].get(key, ()):
            gp.wait_ge(pp.c[pr].h, vv)
        return pipe.issue(gp, dst_ap, bf_ap, group), None
    st["cached"].add(key)
    tok = pipe.issue(gp, dst_ap, f32_ap, group)
    j = len(jobs.items)
    ptok = (j % 2, jobs.base[j % 2] + (j // 2 + 1) * 16)
    jobs.items.append((tok, pipe, dst_ap, bf_ap, ptok, ready))
    st["cstok"][key] = [(cs, ptok[0], ptok[1])]
    return tok, ptok


def _emit_cache_stores(st, sy, jobs):
    cs = st["cs"]
    for j, (tok, pipe, sb_ap, bf_ap, ptok, ready) in enumerate(jobs.items):
        if j >= 2:
            sy.wait_ge(cs.c[j % 2].h, jobs.items[j - 2][4][1])
        if ready is not None:
            sy.wait_ge(ready[0], ready[1])
        else:
            pipe.wait(sy, tok)
        t = cs.issue(sy, bf_ap, sb_ap, j)
        assert t == ptok, (t, ptok)
    if jobs.items:
        cs.wait_all(sy)
    jobs.items = []


def _build_precast_jobs(st):
    w, wbf = st["w"], st["wbf"]
    jobs = []
    for e in range(8):
        jobs.append((wbf["L1_ffn_W2_experts"][e], w["L1_ffn_W2_experts"][e],
                     [("w2e", e)]))
    for ci, ab in ((0, "a"), (1, "b")):
        cols = slice(ci * 2048, (ci + 1) * 2048)
        for nm in ("L1_ffn_W_shared_in", "L1_ffn_W1_shared"):
            jobs.append((wbf[nm][:, cols], w[nm][:, cols],
                         [(nm, hc, ab) for hc in range(4)]))
    jobs.append((wbf["L1_ffn_W_shared_out"][:, :],
                 w["L1_ffn_W_shared_out"][:, :],
                 [("wso", h) for h in range(4)]))
    for nm in ("L1_ffn_Wq", "L1_ffn_Wk_c", "L1_ffn_Wv_c", "L1_ffn_Wk",
               "L1_ffn_Wv", "L1_ffn_Wo"):
        jobs.append((wbf[nm][:, :], w[nm][:, :], [(nm,)]))
    return jobs


def _chain(ctr, eng, instr):
    instr.then_inc(ctr.h, 1)
    ctr.v += 1


class ChainCtx:

    def __init__(self, ctr):
        self.ctr = ctr

    def w(self, eng):
        if self.ctr.v:
            eng.wait_ge(self.ctr.h, self.ctr.v)

    def i(self, instr):
        instr.then_inc(self.ctr.h, 1)
        self.ctr.v += 1




def build_nc(parts, n_tok=T):
    _UID[0] = 0
    nc = bass.Bass("TRN2", target_bir_lowering=False)

    x_in = nc.dram_tensor("x", [n_tok, H], F32, kind="ExternalInput")
    w = {}
    for name, shape in [
        ("L0_attn_Wq", [H, H]), ("L0_attn_bq", [H]),
        ("L0_attn_Wk", [H, H]), ("L0_attn_bk", [H]),
        ("L0_attn_Wv", [H, H]), ("L0_attn_bv", [H]),
        ("L0_attn_Wo", [H, H]), ("L0_attn_bo", [H]),
        ("L0_ffn_gate_w", [H, 8]),
        ("L0_ffn_w1", [8, H, 4096]),
        ("L0_ffn_w2", [8, 4096, H]),
        ("L1_attn_Wq", [H, H]), ("L1_attn_bq", [H]),
        ("L1_attn_Wk", [H, H]), ("L1_attn_bk", [H]),
        ("L1_attn_Wv", [H, H]), ("L1_attn_bv", [H]),
        ("L1_attn_Wo", [H, H]), ("L1_attn_bo", [H]),
        ("L1_ffn_Wq", [H, H]),
        ("L1_ffn_Wk_c", [H, 256]), ("L1_ffn_Wv_c", [H, 256]),
        ("L1_ffn_Wk", [256, H]), ("L1_ffn_Wv", [256, H]),
        ("L1_ffn_Wo", [H, H]),
        ("L1_ffn_W_shared_in", [H, 4096]),
        ("L1_ffn_W_shared_out", [2048, H]),
        ("L1_ffn_W1_shared", [H, 4096]),
        ("L1_ffn_W2_experts", [8, 2048, H]),
        ("L1_ffn_Wg_gate", [H, 2]), ("L1_ffn_We_gate", [H, 8]),
        ("L1_ffn_group_bias", [2]), ("L1_ffn_expert_bias", [8]),
        ("L1_ffn_norm1_w", [H]), ("L1_ffn_norm2_w", [H]),
    ]:
        w[name] = nc.dram_tensor(name, shape, F32, kind="ExternalInput")
    wbf = {}
    for name in ("L0_ffn_w1", "L0_ffn_w2", "L1_ffn_Wq", "L1_ffn_Wk_c",
                 "L1_ffn_Wv_c", "L1_ffn_Wk", "L1_ffn_Wv", "L1_ffn_Wo",
                 "L1_ffn_W_shared_in", "L1_ffn_W_shared_out",
                 "L1_ffn_W1_shared", "L1_ffn_W2_experts"):
        wbf[name] = nc.dram_tensor("c_" + name, w[name].shape, BF16)
    eye_bf_in = nc.dram_tensor("eye_bf", [128, 128], BF16, kind="ExternalInput")
    obd_in = nc.dram_tensor("obd_c", [128, 128], F32, kind="ExternalInput")
    eye_f_in = nc.dram_tensor("eye_f", [128, 128], F32, kind="ExternalInput")
    rope_sin_in = nc.dram_tensor("rope_sin", [T, 512], F32, kind="ExternalInput")
    rope_cos_in = nc.dram_tensor("rope_cos", [T, 512], F32, kind="ExternalInput")

    out = nc.dram_tensor("out", [n_tok, H], F32, kind="ExternalOutput")

    xres = nc.dram_tensor("xres", [H, n_tok], F32)
    qt_hbm = nc.dram_tensor("qt_hbm", [H, T], BF16)
    kt_hbm = nc.dram_tensor("kt_hbm", [H, T], BF16)
    v_hbm = nc.dram_tensor("v_hbm", [T, 1088], BF16)
    kt_bd = nc.dram_tensor("kt_bd", [8, n_tok // 64, 128, 128], BF16)
    v_bd = nc.dram_tensor("v_bd", [8, n_tok // 64, 128, 128], BF16)
    x1f_hbm = nc.dram_tensor("x1f_hbm", [H, T], F32)
    x1b_hbm = nc.dram_tensor("x1b_hbm", [H, T], BF16)
    xnf_hbm = nc.dram_tensor("xnf_hbm", [H, T], F32)
    acc_hbm = nc.dram_tensor("acc_hbm", [H, T], F32)

    st = {"nc": nc, "w": w, "xres": xres, "out": out, "x_in": x_in,
          "qt_hbm": qt_hbm, "kt_hbm": kt_hbm, "v_hbm": v_hbm,
          "kt_bd": kt_bd, "v_bd": v_bd,
          "x1f_hbm": x1f_hbm, "x1b_hbm": x1b_hbm, "xnf_hbm": xnf_hbm,
          "acc_hbm": acc_hbm,
          "rope_sin": rope_sin_in, "rope_cos": rope_cos_in,
          "obd_c": obd_in,
          "n_tok": n_tok}

    with (
        contextlib.ExitStack() as _stack,
    ):
        def _cm(x):
            return _stack.enter_context(x)
        eye_bf = _cm(nc.sbuf_tensor("eye_bf_sb", [128, 128], BF16))
        eye_f = _cm(nc.sbuf_tensor("eye_f_sb", [128, 128], F32))
        ones_f = _cm(nc.sbuf_tensor("ones_f_sb", [128, 128], F32))
        ones_bf = _cm(nc.sbuf_tensor("ones_bf_sb", [128, 128], BF16))
        zbuf = _cm(nc.sbuf_tensor("zero_bf_sb", [128, 2048], BF16))
        bmat0 = _cm(nc.sbuf_tensor("bmat0_sb", [128, 128], F32))
        bmat32 = _cm(nc.sbuf_tensor("bmat32_sb", [128, 128], F32))
        s_ld0 = _cm(nc.semaphore("s_ld0"))
        s_ld1 = _cm(nc.semaphore("s_ld1"))
        s_wl0 = _cm(nc.semaphore("s_wl0"))
        s_wl1 = _cm(nc.semaphore("s_wl1"))
        s_wl2 = _cm(nc.semaphore("s_wl2"))
        s_wl3 = _cm(nc.semaphore("s_wl3"))
        s_st0 = _cm(nc.semaphore("s_st0"))
        s_st1 = _cm(nc.semaphore("s_st1"))
        s_cs0 = _cm(nc.semaphore("s_cs0"))
        s_cs1 = _cm(nc.semaphore("s_cs1"))
        s_mm = _cm(nc.semaphore("s_mm"))
        s_sc = _cm(nc.semaphore("s_sc"))
        s_ve = _cm(nc.semaphore("s_ve"))
        s_vch = _cm(nc.semaphore("s_vch"))
        s_sch = _cm(nc.semaphore("s_sch"))
        st.update(eye_bf=eye_bf, eye_f=eye_f, ones_f=ones_f, ones_bf=ones_bf,
                  bmat0=bmat0, bmat32=bmat32,
                  ld=Pipe([Ctr(s_ld0), Ctr(s_ld1)]),
                  wl=Pipe([Ctr(s_wl0), Ctr(s_wl1)]),
                  wl2=Pipe([Ctr(s_wl2), Ctr(s_wl3)]),
                  stp=Pipe([Ctr(s_st0), Ctr(s_st1)]),
                  cs=Pipe([Ctr(s_cs0), Ctr(s_cs1)]),
                  mm=Ctr(s_mm), sc=Ctr(s_sc), ve=Ctr(s_ve),
                  vch=Ctr(s_vch), sch=Ctr(s_sch),
                  wbf=wbf, cached=set(), cstok={})

        with nc.Block() as b:
            @b.vector
            def _(v):
                ve = st["ve"]
                vch = ChainCtx(st["vch"])
                v.memset(ones_f[:, :], 1.0)
                v.memset(ones_bf[:, :], 1.0)
                vch.i(v.memset(bmat0[:, :], 0.0))
                vch.i(v.memset(bmat32[:, :], 0.0))
                vch.w(v)
                v.memset(bmat0[0:1, :], 1.0)
                v.memset(bmat32[32:33, :], 1.0)
                v.memset(zbuf[:, :], 0.0).then_inc(ve.h, 1)
                ve.v += 1

            @b.sync
            def _(sy):
                ld = st["ld"]
                cs = st["cs"]
                ve = st["ve"]
                ld.issue(sy, eye_bf[:, :], eye_bf_in[:, :], 0)
                tok = ld.issue(sy, eye_f[:, :], eye_f_in[:, :], 0)
                ld.wait(sy, tok)
                sy.wait_ge(ve.h, ve.v)
                kb_n = n_tok // 64
                step = 262144 // (128 * 128)
                j = 0
                for tns in (kt_bd, v_bd):
                    for pair in range(8):
                        for k0 in range(0, kb_n, step):
                            kk = min(step, kb_n - k0)
                            cs.issue(sy, tns[pair, k0:k0 + kk, :, :]
                                     .rearrange("c p t -> p c t"),
                                     zbuf[:, 0:kk * 128]
                                     .rearrange("p (c t) -> p c t", t=128), j)
                            j += 1
                st["zf_final"] = list(cs.bases())

        emit_transpose_hbm(st, src=x_in, dst=xres, direction="a2b")

        for p in parts:
            if p in ("l0ffn_a", "l0ffn_b"):
                emit_l0ffn(st)
            elif p == "l0attn":
                emit_attn(st, layer=0)
            elif p == "l1attn":
                emit_attn(st, layer=1)
            elif p in ("l1ffn_a", "l1ffn_b"):
                emit_l1ffn(st)
            else:
                raise ValueError(p)

        emit_transpose_hbm(st, src=xres, dst=out, direction="b2a")
    return nc




def emit_transpose_hbm(st, src, dst, direction):
    nc = st["nc"]
    eye_f = st["eye_f"]
    ld, stp, mm, ve = st["ld"], st["stp"], st["mm"], st["ve"]
    n_tiles = st["n_tok"] // 128
    b_mm, b_ve = mm.v, ve.v
    b_ld = ld.bases()
    b_st = stp.bases()
    n_dma_per = 1 if direction == "a2b" else HP
    n_st_per = HP if direction == "a2b" else 1

    def mm_val(idx):
        return b_mm + idx + 1

    def ve_val(idx):
        return b_ve + idx + 1

    def ld_tok(i):
        return (i % 2, b_ld[i % 2] + (i // 2 + 1) * 16 * n_dma_per)

    def st_tok(i):
        return (i % 2, b_st[i % 2] + (i // 2 + 1) * 16 * n_st_per)

    with (
        nc.sbuf_tensor(uname("tp_xt"), [128, 2 * H], F32) as xt,
        nc.sbuf_tensor(uname("tp_xc"), [128, 2 * H], F32) as xc,
        nc.psum_tensor(uname("tp_ps"), [128, 4096], F32) as ps,
        nc.Block() as b,
    ):
        @b.sync
        def _(sy):
            for i in range(n_tiles):
                par = i % 2
                if i >= 2:
                    sy.wait_ge(mm.h, mm_val((i - 2) * HP + HP - 1))
                if direction == "a2b":
                    ld.issue(sy, xt[:, par * H:(par + 1) * H],
                             src[i * 128:(i + 1) * 128, :], i)
                else:
                    for c in range(HP):
                        ld.issue(sy,
                                 xt[:, par * H + c * 128:par * H + (c + 1) * 128],
                                 src[c * 128:(c + 1) * 128, i * 128:(i + 1) * 128],
                                 i)

        @b.tensor
        def _(tn):
            for i in range(n_tiles):
                par = i % 2
                ld.wait(tn, ld_tok(i))
                if i >= 2:
                    stp.wait(tn, st_tok(i - 2))
                for j in range(HP):
                    idx = i * HP + j
                    bank = idx % 8
                    if idx >= 8:
                        tn.wait_ge(ve.h, ve_val(idx - 8))
                    tn.transpose(ps[:, bank * 512:bank * 512 + 128],
                                 xt[:, par * H + j * 128: par * H + (j + 1) * 128],
                                 eye_f[:, :]).then_inc(mm.h, 1)
                    mm.v += 1

        @b.vector
        def _(v):
            for i in range(n_tiles):
                par = i % 2
                for j in range(HP):
                    idx = i * HP + j
                    bank = idx % 8
                    v.wait_ge(mm.h, mm_val(idx))
                    v.tensor_copy(
                        xc[:, par * H + j * 128: par * H + (j + 1) * 128],
                        ps[:, bank * 512: bank * 512 + 128]).then_inc(ve.h, 1)
                    ve.v += 1

        @b.gpsimd
        def _(gp):
            for i in range(n_tiles):
                par = i % 2
                gp.wait_ge(ve.h, ve_val(i * HP + HP - 1))
                if direction == "a2b":
                    for c in range(HP):
                        stp.issue(gp,
                                  dst[c * 128:(c + 1) * 128, i * 128:(i + 1) * 128],
                                  xc[:, par * H + c * 128:par * H + (c + 1) * 128],
                                  i)
                else:
                    stp.issue(gp, dst[i * 128:(i + 1) * 128, :],
                              xc[:, par * H:(par + 1) * H], i)
            stp.wait_all(gp)

    assert mm.v == b_mm + n_tiles * HP
    assert ve.v == b_ve + n_tiles * HP




def emit_l0ffn(st):
    nc = st["nc"]
    n_tok = st["n_tok"]
    n_half = n_tok // 2
    with (
        nc.sbuf_tensor(uname("ff_xn"), [128, HP * n_half], BF16) as xn,
        nc.sbuf_tensor(uname("ff_acc"), [128, (n_half // 128) * H], F32) as acc,
        nc.sbuf_tensor(uname("ff_h"), [128, 2 * 4 * n_half], BF16) as hbuf,
        nc.sbuf_tensor(uname("ff_w1"), [128, 2 * HP * 512], BF16) as w1b,
        nc.sbuf_tensor(uname("ff_w2"), [128, 2 * 4 * 1024], BF16) as w2b,
        nc.sbuf_tensor(uname("ff_gw"), [128, HP * 8], F32) as gw_sb,
        nc.sbuf_tensor(uname("ff_c"), [128, (n_tok // 128) * 8], F32) as c_sb,
        nc.sbuf_tensor(uname("ff_tmp"), [128, 2 * 1024], F32) as tmpev,
    ):
        with nc.Block() as b:
            @b.sync
            def _(sy):
                ld = st["ld"]
                tok = None
                for c in range(HP):
                    tok = ld.issue(sy, gw_sb[:, c * 8:(c + 1) * 8],
                                   st["w"]["L0_ffn_gate_w"][c * 128:(c + 1) * 128, :],
                                   0)
                ld.wait(sy, tok)

        for half in range(2):
            _l0_ln_gate_phase(st, half, n_half, xn, gw_sb, c_sb)

            with nc.Block() as b:
                @b.vector
                def _(v):
                    v.memset(acc[:, :], 0.0)

            for e in range(8):
                _l0_expert_block(st, e, half, n_half, xn, hbuf, w1b, w2b,
                                 acc, c_sb, tmpev)

            _acc_to_res_phase(st, half, n_half, acc)


def _l0_ln_gate_phase(st, half, n_half, xn, gw_sb, c_sb):
    nc = st["nc"]
    xres = st["xres"]
    ld, mm, sc, ve = st["ld"], st["mm"], st["sc"], st["ve"]
    vch = ChainCtx(st["vch"])
    sch = ChainCtx(st["sch"])
    ones_f = st["ones_f"]
    n_chunk = n_half // 512
    b_mm, b_sc, b_ve = mm.v, sc.v, ve.v
    b_ld = ld.bases()

    def sq_v(c, j):
        return b_sc + c * 10 + j + 1

    def scA_v(c):
        return b_sc + c * 10 + 9

    def gexp_v(c):
        return b_sc + c * 10 + 10

    def s2_v(c, j):
        return b_mm + c * 11 + j + 1

    def st1_v(c):
        return b_mm + c * 11 + 9

    def bc_v(c):
        return b_mm + c * 11 + 10

    def gmm_v(c):
        return b_mm + c * 11 + 11

    def veA_v(c):
        return b_ve + c * 4 + 1

    def nrm_v(c):
        return b_ve + c * 4 + 2

    def gv_v(c):
        return b_ve + c * 4 + 3

    def gfin_v(c):
        return b_ve + c * 4 + 4

    def ld_tok(c):
        return (c % 2, b_ld[c % 2] + (c // 2 + 1) * 16)

    with (
        nc.sbuf_tensor(uname("lg_xf"), [128, HP * 512], F32) as xf,
        nc.sbuf_tensor(uname("lg_xq"), [128, 1024], F32) as xq,
        nc.sbuf_tensor(uname("lg_srow"), [128, 2048], F32) as srow,
        nc.sbuf_tensor(uname("lg_gs"), [128, 128], F32) as gsc,
        nc.psum_tensor(uname("lg_ps_st"), [128, 1024], F32) as ps_st,
        nc.psum_tensor(uname("lg_ps_b"), [128, 1024], F32) as ps_b,
        nc.psum_tensor(uname("lg_ps_g"), [128, 512], F32) as ps_g,
        nc.Block() as b,
    ):
        @b.sync
        def _(sy):
            for c in range(n_chunk):
                t0 = half * n_half + c * 512
                if c >= 1:
                    sy.wait_ge(mm.h, gmm_v(c - 1))
                    sy.wait_ge(ve.h, nrm_v(c - 1))
                ld.issue(sy, xf[:, :].rearrange("p (c t) -> p c t", t=512),
                         xres[:, t0:t0 + 512]
                         .rearrange("(c p) t -> p c t", p=128), c)

        @b.scalar
        def _(s):
            for c in range(n_chunk):
                ld.wait(s, ld_tok(c))
                if c >= 1:
                    s.wait_ge(mm.h, st1_v(c - 1))
                for j in range(HP):
                    par = j % 2
                    if j >= 2:
                        s.wait_ge(mm.h, s2_v(c, j - 2))
                    s.activation(xq[:, par * 512:(par + 1) * 512],
                                 xf[:, j * 512:(j + 1) * 512],
                                 AF.Square).then_inc(sc.h, 1)
                    sc.v += 1
                s.wait_ge(ve.h, veA_v(c))
                sch.i(s.activation(srow[0:1, 1536:2048], srow[0:1, 1024:1536],
                                   AF.Ln))
                sch.w(s)
                s.activation(srow[0:1, 1536:2048], srow[0:1, 1536:2048],
                             AF.Exp, scale=-0.5).then_inc(sc.h, 1)
                sc.v += 1
                s.wait_ge(ve.h, gv_v(c))
                last = None
                for sub in range(4):
                    d = gsc[:, 96 + sub * 4 + 2:96 + sub * 4 + 3]
                    z = gsc[:, 96 + sub * 4 + 3:96 + sub * 4 + 4]
                    last = s.activation(z, d, AF.Exp)
                last.then_inc(sc.h, 1)
                sc.v += 1

        @b.tensor
        def _(tn):
            for c in range(n_chunk):
                if c >= 1:
                    tn.wait_ge(ve.h, veA_v(c - 1))
                for j in range(HP):
                    par = j % 2
                    tn.wait_ge(sc.h, sq_v(c, j))
                    tn.matmul(ps_st[0:1, 512:1024], ones_f[:, 0:1],
                              xq[:, par * 512:(par + 1) * 512],
                              start=(j == 0), stop=(j == HP - 1)).then_inc(mm.h, 1)
                    mm.v += 1
                i = None
                for j in range(HP):
                    i = tn.matmul(ps_st[0:1, 0:512], ones_f[:, 0:1],
                                  xf[:, j * 512:(j + 1) * 512],
                                  start=(j == 0), stop=(j == HP - 1))
                i.then_inc(mm.h, 1)
                mm.v += 1
                tn.wait_ge(sc.h, scA_v(c))
                if c >= 1:
                    tn.wait_ge(ve.h, nrm_v(c - 1))
                tn.matmul(ps_b[:, 0:512], st["bmat0"][:, :], srow[:, 0:512],
                          start=True, stop=True)
                tn.matmul(ps_b[:, 512:1024], st["bmat0"][:, :],
                          srow[:, 1536:2048], start=True,
                          stop=True).then_inc(mm.h, 1)
                mm.v += 1
                tn.wait_ge(ve.h, nrm_v(c))
                i = None
                for sub in range(4):
                    for j in range(HP):
                        i = tn.matmul(
                            ps_g[:, sub * 8:(sub + 1) * 8],
                            xf[:, j * 512 + sub * 128: j * 512 + (sub + 1) * 128],
                            gw_sb[:, j * 8:(j + 1) * 8],
                            start=(j == 0), stop=(j == HP - 1))
                    i = tn.matmul(ps_g[:, 32 + sub:33 + sub],
                                  srow[0:1, 1536 + sub * 128:
                                       1536 + (sub + 1) * 128],
                                  ones_f[0:1, 0:1], start=True, stop=True)
                i.then_inc(mm.h, 1)
                mm.v += 1

        @b.vector
        def _(v):
            vch.i(v.memset(srow[:, :], 0.0))
            for c in range(n_chunk):
                ctile0 = (half * n_half + c * 512) // 128
                v.wait_ge(mm.h, st1_v(c))
                vch.w(v)
                vch.i(v.tensor_scalar(srow[0:1, 0:512], ps_st[0:1, 0:512],
                                      1.0 / H, None, ALU.mult))
                vch.i(v.tensor_scalar(srow[0:1, 512:1024], ps_st[0:1, 512:1024],
                                      1.0 / H, None, ALU.mult))
                vch.w(v)
                vch.i(v.tensor_tensor(srow[0:1, 1024:1536], srow[0:1, 0:512],
                                      srow[0:1, 0:512], ALU.mult))
                vch.w(v)
                vch.i(v.tensor_tensor(srow[0:1, 1024:1536], srow[0:1, 512:1024],
                                      srow[0:1, 1024:1536], ALU.subtract))
                vch.w(v)
                v.tensor_scalar(srow[0:1, 1024:1536], srow[0:1, 1024:1536],
                                1e-5, None, ALU.add).then_inc(ve.h, 1)
                ve.v += 1
                v.wait_ge(mm.h, bc_v(c))
                last = None
                for j in range(HP):
                    vch.i(v.tensor_tensor(xf[:, j * 512:(j + 1) * 512],
                                          xf[:, j * 512:(j + 1) * 512],
                                          ps_b[:, 0:512], ALU.subtract))
                    vch.w(v)
                    last = v.tensor_tensor(
                        xn[:, j * n_half + c * 512: j * n_half + (c + 1) * 512],
                        xf[:, j * 512:(j + 1) * 512],
                        ps_b[:, 512:1024], ALU.mult)
                last.then_inc(ve.h, 1)
                ve.v += 1
                v.wait_ge(mm.h, gmm_v(c))
                last = None
                for sub in range(4):
                    rcol = gsc[:, 120 + sub:121 + sub]
                    lg = gsc[:, sub * 16:sub * 16 + 8]
                    eq1 = gsc[:, sub * 16 + 8:sub * 16 + 16]
                    l2 = gsc[:, 64 + sub * 8:64 + sub * 8 + 8]
                    m1 = gsc[:, 96 + sub * 4:96 + sub * 4 + 1]
                    m2 = gsc[:, 96 + sub * 4 + 1:96 + sub * 4 + 2]
                    d = gsc[:, 96 + sub * 4 + 2:96 + sub * 4 + 3]
                    vch.i(v.tensor_copy(rcol, ps_g[:, 32 + sub:33 + sub]))
                    vch.w(v)
                    vch.i(v.tensor_scalar(lg, ps_g[:, sub * 8:(sub + 1) * 8],
                                          rcol, None, ALU.mult))
                    vch.w(v)
                    vch.i(v.reduce_max(m1, lg, axis=AX.X))
                    vch.w(v)
                    vch.i(v.tensor_scalar(eq1, lg, m1, None, ALU.is_equal))
                    vch.w(v)
                    vch.i(v.scalar_tensor_tensor(l2, eq1, NEG_BIG, lg,
                                                 ALU.mult, ALU.add))
                    vch.w(v)
                    vch.i(v.reduce_max(m2, l2, axis=AX.X))
                    vch.w(v)
                    vch.i(v.tensor_scalar(l2, l2, m2, None, ALU.is_equal))
                    vch.w(v)
                    last = v.tensor_tensor(d, m2, m1, ALU.subtract)
                last.then_inc(ve.h, 1)
                ve.v += 1
                v.wait_ge(sc.h, gexp_v(c))
                for sub in range(4):
                    eq1 = gsc[:, sub * 16 + 8:sub * 16 + 16]
                    eq2 = gsc[:, 64 + sub * 8:64 + sub * 8 + 8]
                    z = gsc[:, 96 + sub * 4 + 3:96 + sub * 4 + 4]
                    zp = gsc[:, 112 + sub * 2:112 + sub * 2 + 1]
                    r = gsc[:, 112 + sub * 2 + 1:112 + sub * 2 + 2]
                    cdst = c_sb[:, (ctile0 + sub) * 8:(ctile0 + sub + 1) * 8]
                    vch.i(v.tensor_scalar(zp, z, 1.0, None, ALU.add))
                    vch.w(v)
                    vch.i(v.reciprocal(r, zp))
                    vch.w(v)
                    vch.i(v.tensor_tensor(zp, z, r, ALU.mult))
                    vch.w(v)
                    vch.i(v.tensor_scalar(cdst, eq2, zp, None, ALU.mult))
                    vch.w(v)
                    vch.i(v.scalar_tensor_tensor(cdst, eq1, r, cdst,
                                                 ALU.mult, ALU.add))
                    vch.w(v)
                    vch.i(v.tensor_scalar(cdst, cdst, 0.5, None, ALU.mult))
                vch.w(v)
                v.engine_nop().then_inc(ve.h, 1)
                ve.v += 1

    assert sc.v == b_sc + n_chunk * 10
    assert mm.v == b_mm + n_chunk * 11
    assert ve.v == b_ve + n_chunk * 4


def _l0_expert_block(st, e, half, n_half, xn, hbuf, w1b, w2b, acc, c_sb, tmpev):
    nc = st["nc"]
    w = st["w"]
    wl, mm, sc, ve = st["wl"], st["mm"], st["sc"], st["ve"]
    G = 8
    HW = 512
    n_t512 = n_half // 512
    n_t128 = n_half // 128
    na = 4 * n_t512

    b_mm, b_sc, b_ve = mm.v, sc.v, ve.v
    b_wl = wl.bases()

    def wl_tok(g):
        return (g % 2, b_wl[g % 2] + (g // 2 + 1) * 32)

    def a_val(g, gi):
        return b_mm + g * (na + n_t128) + gi + 1

    def bmm_val(g, t):
        return b_mm + g * (na + n_t128) + na + t + 1

    def sil_val(gg):
        return b_sc + (gg // na) * (na + n_t128) + (gg % na) + 1

    def bev_val(bt):
        return b_sc + (bt // n_t128) * (na + n_t128) + na + (bt % n_t128) + 1

    def vad_val(bt):
        return b_ve + bt + 1

    with (
        nc.psum_tensor(uname("ff_ps_a"), [128, 2048], F32) as ps_a,
        nc.psum_tensor(uname("ff_ps_o"), [128, 2048], F32) as ps_o,
        nc.Block() as b,
    ):
        jobs = CacheJobs(st)
        slot_ptoks = {}

        @b.gpsimd
        def _(gp):
            wbf = st["wbf"]
            for g in range(G):
                par = g % 2
                if g >= 2:
                    gp.wait_ge(mm.h, bmm_val(g - 2, n_t128 - 1))
                    for pt in slot_ptoks.get(g - 2, ()):
                        gp.wait_ge(st["cs"].c[pt[0]].h, pt[1])
                rdy = (mm.h, a_val(g, 0))
                _, p1 = _wload(
                    st, gp, wl,
                    w1b[:, par * 4096:(par + 1) * 4096]
                    .rearrange("p (c t) -> p c t", t=HW),
                    ("w1", e, g),
                    w["L0_ffn_w1"][e, :, g * HW:(g + 1) * HW]
                    .rearrange("(c p) t -> p c t", p=128),
                    wbf["L0_ffn_w1"][e, :, g * HW:(g + 1) * HW]
                    .rearrange("(c p) t -> p c t", p=128), g, jobs, ready=rdy)
                _, p2 = _wload(
                    st, gp, wl,
                    w2b[:, par * 4096:(par + 1) * 4096]
                    .rearrange("p (c t) -> p c t", t=1024),
                    ("w2", e, g),
                    w["L0_ffn_w2"][e, g * HW:(g + 1) * HW, :]
                    .rearrange("(c p) t -> p c t", p=128),
                    wbf["L0_ffn_w2"][e, g * HW:(g + 1) * HW, :]
                    .rearrange("(c p) t -> p c t", p=128), g, jobs, ready=rdy)
                slot_ptoks[g] = [p for p in (p1, p2) if p is not None]

        if jobs.items:
            @b.sync
            def _(sy):
                _emit_cache_stores(st, sy, jobs)

        @b.tensor
        def _(tn):
            for g in range(G):
                par = g % 2
                wl.wait(tn, wl_tok(g))
                for m in range(4):
                    for n in range(n_t512):
                        gi = m * n_t512 + n
                        gg = g * na + gi
                        bank = gg % 4
                        if gg >= 4:
                            tn.wait_ge(sc.h, sil_val(gg - 4))
                        i = None
                        for k in range(HP):
                            i = tn.matmul(
                                ps_a[:, bank * 512:(bank + 1) * 512],
                                w1b[:, par * 4096 + k * 512 + m * 128:
                                    par * 4096 + k * 512 + (m + 1) * 128],
                                xn[:, k * n_half + n * 512:
                                   k * n_half + (n + 1) * 512],
                                start=(k == 0), stop=(k == HP - 1))
                        i.then_inc(mm.h, 1)
                        mm.v += 1
                for t in range(n_t128):
                    bt = g * n_t128 + t
                    bp = (t % 2) * 2
                    if bt >= 2:
                        tn.wait_ge(sc.h, bev_val(bt - 2))
                    if t == 0:
                        tn.wait_ge(sc.h, sil_val(g * na + na - 1))
                    i = None
                    for nn in range(2):
                        for k in range(4):
                            i = tn.matmul(
                                ps_o[:, (bp + nn) * 512:(bp + nn + 1) * 512],
                                hbuf[:, par * 4 * n_half + k * n_half + t * 128:
                                     par * 4 * n_half + k * n_half + (t + 1) * 128],
                                w2b[:, par * 4096 + k * 1024 + nn * 512:
                                    par * 4096 + k * 1024 + (nn + 1) * 512],
                                start=(k == 0), stop=(k == 3))
                    i.then_inc(mm.h, 1)
                    mm.v += 1

        @b.scalar
        def _(s):
            for g in range(G):
                par = g % 2
                for m in range(4):
                    for n in range(n_t512):
                        gi = m * n_t512 + n
                        gg = g * na + gi
                        bank = gg % 4
                        s.wait_ge(mm.h, a_val(g, gi))
                        s.activation(
                            hbuf[:, par * 4 * n_half + m * n_half + n * 512:
                                 par * 4 * n_half + m * n_half + (n + 1) * 512],
                            ps_a[:, bank * 512:(bank + 1) * 512],
                            AF.Silu).then_inc(sc.h, 1)
                        sc.v += 1
                for t in range(n_t128):
                    bt = g * n_t128 + t
                    bp = (t % 2) * 2
                    tpar = t % 2
                    s.wait_ge(mm.h, bmm_val(g, t))
                    if bt >= 2:
                        s.wait_ge(ve.h, vad_val(bt - 2))
                    ctile = half * n_t128 + t
                    csl = c_sb[:, ctile * 8 + e:ctile * 8 + e + 1]
                    s.activation(tmpev[:, tpar * 1024:tpar * 1024 + 512],
                                 ps_o[:, bp * 512:(bp + 1) * 512],
                                 AF.Copy, scale=csl)
                    s.activation(
                        tmpev[:, tpar * 1024 + 512:tpar * 1024 + 1024],
                        ps_o[:, (bp + 1) * 512:(bp + 2) * 512],
                        AF.Copy, scale=csl).then_inc(sc.h, 1)
                    sc.v += 1

        @b.vector
        def _(v):
            vch = ChainCtx(st["vch"])
            for g in range(G):
                for t in range(n_t128):
                    bt = g * n_t128 + t
                    tpar = t % 2
                    v.wait_ge(sc.h, bev_val(bt))
                    vch.w(v)
                    vch.i(v.tensor_tensor(
                        acc[:, t * H:t * H + 512],
                        acc[:, t * H:t * H + 512],
                        tmpev[:, tpar * 1024:tpar * 1024 + 512],
                        ALU.add))
                    v.tensor_tensor(
                        acc[:, t * H + 512:(t + 1) * H],
                        acc[:, t * H + 512:(t + 1) * H],
                        tmpev[:, tpar * 1024 + 512:tpar * 1024 + 1024],
                        ALU.add).then_inc(ve.h, 1)
                    ve.v += 1

    assert mm.v == b_mm + G * (na + n_t128), (mm.v, b_mm)
    assert sc.v == b_sc + G * (na + n_t128)
    assert ve.v == b_ve + G * n_t128


def _acc_to_res_phase(st, half, n_half, acc):
    nc = st["nc"]
    xres = st["xres"]
    ld, stp, mm, ve = st["ld"], st["stp"], st["mm"], st["ve"]
    eye_f = st["eye_f"]
    n_t512 = n_half // 512
    b_mm, b_ve = mm.v, ve.v
    b_ld = ld.bases()
    b_st = stp.bases()

    def mm_val(idx):
        return b_mm + idx + 1

    def ve_val(idx):
        return b_ve + idx + 1

    def ld_tok(q):
        return (q % 2, b_ld[q % 2] + (q // 2 + 1) * 16)

    def st_tok(q):
        return (q % 2, b_st[q % 2] + (q // 2 + 1) * 16)

    with (
        nc.sbuf_tensor(uname("ar_xr"), [128, 2 * HP * 512], F32) as xrt,
        nc.psum_tensor(uname("ar_ps"), [128, 4096], F32) as ps_t,
        nc.Block() as b,
    ):
        @b.sync
        def _(sy):
            for q in range(n_t512):
                par = q % 2
                if q >= 2:
                    stp.wait(sy, st_tok(q - 2))
                ld.issue(sy,
                         xrt[:, par * 4096:(par + 1) * 4096]
                         .rearrange("p (c t) -> p c t", t=512),
                         xres[:, half * n_half + q * 512:
                              half * n_half + (q + 1) * 512]
                         .rearrange("(c p) t -> p c t", p=128), q)

        @b.tensor
        def _(tn):
            for q in range(n_t512):
                ld.wait(tn, ld_tok(q))
                for tt in range(4):
                    t = q * 4 + tt
                    for j in range(HP):
                        idx = q * 32 + tt * HP + j
                        bank = idx % 8
                        if idx >= 8:
                            tn.wait_ge(ve.h, ve_val(idx - 8))
                        tn.transpose(
                            ps_t[:, bank * 512:bank * 512 + 128],
                            acc[:, t * H + j * 128: t * H + (j + 1) * 128],
                            eye_f[:, :]).then_inc(mm.h, 1)
                        mm.v += 1

        @b.vector
        def _(v):
            for q in range(n_t512):
                par = q % 2
                for tt in range(4):
                    for j in range(HP):
                        idx = q * 32 + tt * HP + j
                        bank = idx % 8
                        v.wait_ge(mm.h, mm_val(idx))
                        dst = xrt[:, par * 4096 + j * 512 + tt * 128:
                                  par * 4096 + j * 512 + (tt + 1) * 128]
                        v.tensor_tensor(
                            dst, dst, ps_t[:, bank * 512:bank * 512 + 128],
                            ALU.add).then_inc(ve.h, 1)
                        ve.v += 1

        @b.gpsimd
        def _(gp):
            for q in range(n_t512):
                par = q % 2
                gp.wait_ge(ve.h, ve_val(q * 32 + 31))
                stp.issue(gp,
                          xres[:, half * n_half + q * 512:
                               half * n_half + (q + 1) * 512]
                          .rearrange("(c p) t -> p c t", p=128),
                          xrt[:, par * 4096:(par + 1) * 4096]
                          .rearrange("p (c t) -> p c t", t=512), q)
            stp.wait_all(gp)

    assert mm.v == b_mm + n_t512 * 32
    assert ve.v == b_ve + n_t512 * 32


def emit_attn(st, layer):
    nc = st["nc"]
    w = st["w"]
    n_tok = st["n_tok"]
    pre = "L0_attn_" if layer == 0 else "L1_attn_"
    scale = (1.0 / np.sqrt(512.0)) if layer == 0 else (1.0 / 8.0)
    if layer == 0:
        st["precast"] = _build_precast_jobs(st)
        st["precast_issued"] = []
    with (
        nc.sbuf_tensor(uname("at_wo"), [128, HP * H], BF16) as wo,
        nc.sbuf_tensor(uname("at_bo"), [128, 8], F32) as boP,
        nc.sbuf_tensor(uname("at_bq"), [128, 8], F32) as bqP,
    ):
        _attn_weights_phase(st, pre, layer, wo, boP, bqP)
        _attn_proj_phase(st, pre, layer, wo, bqP)
        _attn_core_phase(st, layer, wo, boP, scale)


def _attn_weights_phase(st, pre, layer, wo, boP, bqP):
    nc = st["nc"]
    w = st["w"]
    wl, ld, mm, sc, ve = st["wl"], st["ld"], st["mm"], st["sc"], st["ve"]
    eye_bf = st["eye_bf"]
    ones_f = st["ones_f"]
    import contextlib
    stk = contextlib.ExitStack()
    wq = stk.enter_context(nc.sbuf_tensor(uname("at_wq"), [128, HP * H], BF16))
    wk = stk.enter_context(nc.sbuf_tensor(uname("at_wk"), [128, HP * H], BF16))
    wv = stk.enter_context(nc.sbuf_tensor(uname("at_wv"), [128, HP * H], BF16))
    st["_attn_stack"] = stk
    st["_attn_wqkv"] = (wq, wk, wv)

    b_wl = wl.bases()
    b_ld = ld.bases()
    b_mm, b_sc, b_ve = mm.v, sc.v, ve.v

    with (
        nc.sbuf_tensor(uname("at_bv"), [128, 8], BF16) as bvB,
        nc.sbuf_tensor(uname("at_bo0"), [128, 8], F32) as bo0,
        nc.sbuf_tensor(uname("at_brow"), [128, 1024], F32) as brow,
        nc.psum_tensor(uname("at_pw"), [128, 4096], F32) as pw,
        nc.Block() as b,
    ):
        @b.gpsimd
        def _(gp):
            for gi, (buf, name) in enumerate([(wq, "Wq"), (wk, "Wk"),
                                              (wv, "Wv"), (wo, "Wo")]):
                wl.issue(gp, buf[:, :].rearrange("p (c t) -> p c t", t=H),
                         w[pre + name][:, :].rearrange("(c p) t -> p c t", p=128),
                         0)

        @b.sync
        def _(sy):
            ld.issue(sy, brow[0:1, 0:1024], w[pre + "bq"][None, :], 0)
            ld.issue(sy, brow[32:33, 0:1024], w[pre + "bv"][None, :], 0)
            ld.issue(sy, brow[64:65, 0:1024], w[pre + "bo"][None, :], 0)

        b_ld0 = b_ld[0]

        @b.tensor
        def _(tn):
            ld.wait(tn, (0, b_ld0 + 48))
            i = None
            for bi, bp in enumerate((0, 32, 64)):
                for c in range(HP):
                    i = tn.matmul(pw[:, 3584 + bi * 8 + c:3584 + bi * 8 + c + 1],
                                  brow[bp:bp + 1, c * 128:(c + 1) * 128],
                                  ones_f[bp:bp + 1, 0:1], start=True, stop=True)
            i.then_inc(mm.h, 1)
            mm.v += 1

        @b.vector
        def _(v):
            v.wait_ge(mm.h, b_mm + 1)
            v.tensor_copy(bqP[:, :], pw[:, 3584:3592]).then_inc(ve.h, 1)
            ve.v += 1
            v.tensor_copy(bvB[:, :], pw[:, 3592:3600])
            v.tensor_copy(bo0[:, :], pw[:, 3600:3608]).then_inc(ve.h, 1)
            ve.v += 1

        b_mm += 1
        b_ve += 2

        n_tr = 0
        if layer == 0:
            entries = []
            for bi in range(4):
                for r in range(HP):
                    for j in range(r, HP):
                        if r == j:
                            entries.append((bi, r, r, len(entries)))
                        else:
                            i0 = len(entries)
                            entries.append((bi, r, j, i0 + 1))
                            entries.append((bi, j, r, i0))
            bufs = (wq, wk, wv, wo)

            def _slotap(k):
                slot = k % 8
                pt = pw[:, slot * 512:slot * 512 + 512]
                return pt.bitcast(BF16)[:, 0:128]

            @b.tensor
            def _(tn):
                tn.wait_ge(wl.c[0].h, b_wl[0] + 4 * 16)
                tn.wait_ge(ve.h, b_ve)
                for k, (bi, a, bb, pk) in enumerate(entries):
                    if k >= 8:
                        tn.wait_ge(ve.h, b_ve + k - 8 + 1)
                    tn.transpose(
                        _slotap(k),
                        bufs[bi][:, a * H + bb * 128: a * H + (bb + 1) * 128],
                        eye_bf[:, :]).then_inc(mm.h, 1)
                    mm.v += 1

            @b.vector
            def _(v):
                for k, (bi, a, bb, pk) in enumerate(entries):
                    v.wait_ge(mm.h, b_mm + max(k, pk) + 1)
                    v.tensor_copy(
                        bufs[bi][:, bb * H + a * 128: bb * H + (a + 1) * 128],
                        _slotap(k)).then_inc(ve.h, 1)
                    ve.v += 1

            n_tr = len(entries)

        b_mm2 = b_mm + n_tr

        @b.tensor
        def _(tn):
            if layer != 0:
                tn.wait_ge(wl.c[0].h, b_wl[0] + 4 * 16)
            tn.wait_ge(ve.h, b_ve + n_tr)
            i = None
            for m in range(HP):
                for k in range(HP):
                    i = tn.matmul(pw[:, 2048 + m:2048 + m + 1],
                                  wo[:, k * H + m * 128: k * H + (m + 1) * 128],
                                  bvB[:, k:k + 1], start=(k == 0), stop=(k == HP - 1))
            i.then_inc(mm.h, 1)
            mm.v += 1

        @b.vector
        def _(v):
            v.wait_ge(mm.h, b_mm2 + 1)
            v.tensor_tensor(boP[:, :], pw[:, 2048:2056], bo0[:, :], ALU.add)\
                .then_inc(ve.h, 1)
            ve.v += 1

    assert mm.v == b_mm + n_tr + 1
    assert ve.v == b_ve + n_tr + 1


def _attn_proj_phase(st, pre, layer, wo, bqP):
    nc = st["nc"]
    xres = st["xres"]
    ld, stp, mm, sc, ve = st["ld"], st["stp"], st["mm"], st["sc"], st["ve"]
    vch = ChainCtx(st["vch"])
    sch = ChainCtx(st["sch"])
    ones_f = st["ones_f"]
    wq, wk, wv = st["_attn_wqkv"]
    n_tok = st["n_tok"]
    n_chunk = n_tok // 512
    b_mm, b_sc, b_ve = mm.v, sc.v, ve.v
    b_ld = ld.bases()
    b_st = stp.bases()

    def s2_v(c, j):
        return b_mm + c * 34 + j + 1

    def st1_v(c):
        return b_mm + c * 34 + 9

    def bc_v(c):
        return b_mm + c * 34 + 10

    def qmm_v(c, m):
        return b_mm + c * 34 + 11 + m

    def kmm_v(c, m):
        return b_mm + c * 34 + 19 + m

    def vmm_v(c, g):
        return b_mm + c * 34 + 27 + g

    def sq_v(c, j):
        return b_sc + c * 17 + j + 1

    def scA_v(c):
        return b_sc + c * 17 + 9

    def qev_v(c, m):
        return b_sc + c * 17 + 10 + m

    def veA_v(c):
        return b_ve + c * 18 + 1

    def nrm_v(c):
        return b_ve + c * 18 + 2

    def kev_v(c, m):
        return b_ve + c * 18 + 3 + m

    def vev_v(c, g):
        return b_ve + c * 18 + 11 + g

    def ld_tok(c):
        return (c % 2, b_ld[c % 2] + (c // 2 + 1) * 16)

    def st_tok(c):
        return (c % 2, b_st[c % 2] + (c // 2 + 1) * 48)

    with (
        nc.sbuf_tensor(uname("ap_xf"), [128, HP * 512], F32) as xf,
        nc.sbuf_tensor(uname("ap_xq"), [128, 1024], F32) as xq,
        nc.sbuf_tensor(uname("ap_srow"), [128, 2048], F32) as srow,
        nc.sbuf_tensor(uname("ap_xn"), [128, 2 * HP * 512], BF16) as xnb,
        nc.sbuf_tensor(uname("ap_qt"), [128, 2 * HP * 512], BF16) as qtb,
        nc.sbuf_tensor(uname("ap_kt"), [128, 2 * HP * 512], BF16) as ktb,
        nc.sbuf_tensor(uname("ap_vt"), [128, 2 * 4 * H], BF16) as vtb,
        nc.psum_tensor(uname("ap_ps_st"), [128, 1024], F32) as ps_st,
        nc.psum_tensor(uname("ap_ps_b"), [128, 1024], F32) as ps_b,
        nc.psum_tensor(uname("ap_ps_p"), [128, 2048], F32) as ps_p,
        nc.Block() as b,
    ):
        @b.sync
        def _(sy):
            for c in range(n_chunk):
                t0 = c * 512
                if c >= 1:
                    sy.wait_ge(ve.h, nrm_v(c - 1))
                ld.issue(sy, xf[:, :].rearrange("p (c t) -> p c t", t=512),
                         xres[:, t0:t0 + 512]
                         .rearrange("(c p) t -> p c t", p=128), c)

        @b.scalar
        def _(s):
            for c in range(n_chunk):
                par = c % 2
                ld.wait(s, ld_tok(c))
                if c >= 1:
                    s.wait_ge(mm.h, st1_v(c - 1))
                for j in range(HP):
                    jp = j % 2
                    if j >= 2:
                        s.wait_ge(mm.h, s2_v(c, j - 2))
                    s.activation(xq[:, jp * 512:(jp + 1) * 512],
                                 xf[:, j * 512:(j + 1) * 512],
                                 AF.Square).then_inc(sc.h, 1)
                    sc.v += 1
                s.wait_ge(ve.h, veA_v(c))
                sch.i(s.activation(srow[0:1, 1536:2048],
                                   srow[0:1, 1024:1536], AF.Ln))
                sch.w(s)
                s.activation(srow[0:1, 1536:2048], srow[0:1, 1536:2048],
                             AF.Exp, scale=-0.5).then_inc(sc.h, 1)
                sc.v += 1
                for m in range(HP):
                    s.wait_ge(mm.h, qmm_v(c, m))
                    if c >= 2:
                        stp.wait(s, st_tok(c - 2))
                    s.activation(qtb[:, par * 4096 + m * 512:
                                     par * 4096 + (m + 1) * 512],
                                 ps_p[:, (m % 2) * 512:(m % 2) * 512 + 512],
                                 AF.Identity, bias=bqP[:, m:m + 1]).then_inc(sc.h, 1)
                    sc.v += 1

        @b.tensor
        def _(tn):
            for c in range(n_chunk):
                par = c % 2
                if c >= 1:
                    tn.wait_ge(ve.h, veA_v(c - 1))
                for j in range(HP):
                    jp = j % 2
                    tn.wait_ge(sc.h, sq_v(c, j))
                    tn.matmul(ps_st[0:1, 512:1024], ones_f[:, 0:1],
                              xq[:, jp * 512:(jp + 1) * 512],
                              start=(j == 0), stop=(j == HP - 1)).then_inc(mm.h, 1)
                    mm.v += 1
                i = None
                for j in range(HP):
                    i = tn.matmul(ps_st[0:1, 0:512], ones_f[:, 0:1],
                                  xf[:, j * 512:(j + 1) * 512],
                                  start=(j == 0), stop=(j == HP - 1))
                i.then_inc(mm.h, 1)
                mm.v += 1
                tn.wait_ge(sc.h, scA_v(c))
                if c >= 1:
                    tn.wait_ge(ve.h, nrm_v(c - 1))
                tn.matmul(ps_b[:, 0:512], st["bmat0"][:, :], srow[:, 0:512],
                          start=True, stop=True)
                tn.matmul(ps_b[:, 512:1024], st["bmat0"][:, :],
                          srow[:, 1536:2048], start=True,
                          stop=True).then_inc(mm.h, 1)
                mm.v += 1
                tn.wait_ge(ve.h, nrm_v(c))
                for m in range(HP):
                    bank = m % 2
                    if c >= 1 and m < 2:
                        tn.wait_ge(ve.h, vev_v(c - 1, 6 + m))
                    if m >= 2:
                        tn.wait_ge(sc.h, qev_v(c, m - 2))
                    i = None
                    for k in range(HP):
                        i = tn.matmul(ps_p[:, bank * 512:(bank + 1) * 512],
                                      wq[:, k * H + m * 128: k * H + (m + 1) * 128],
                                      xnb[:, par * 4096 + k * 512:
                                          par * 4096 + (k + 1) * 512],
                                      start=(k == 0), stop=(k == HP - 1))
                    i.then_inc(mm.h, 1)
                    mm.v += 1
                for m in range(HP):
                    bank = 2 + m % 2
                    if m >= 2:
                        tn.wait_ge(ve.h, kev_v(c, m - 2))
                    i = None
                    for k in range(HP):
                        i = tn.matmul(ps_p[:, bank * 512:(bank + 1) * 512],
                                      wk[:, k * H + m * 128: k * H + (m + 1) * 128],
                                      xnb[:, par * 4096 + k * 512:
                                          par * 4096 + (k + 1) * 512],
                                      start=(k == 0), stop=(k == HP - 1))
                    i.then_inc(mm.h, 1)
                    mm.v += 1
                for g in range(8):
                    tsub, n2 = g // 2, g % 2
                    bank = g % 2
                    if g < 2:
                        tn.wait_ge(sc.h, qev_v(c, 6 + g))
                    else:
                        tn.wait_ge(ve.h, vev_v(c, g - 2))
                    i = None
                    for k in range(HP):
                        i = tn.matmul(
                            ps_p[:, bank * 512:(bank + 1) * 512],
                            xnb[:, par * 4096 + k * 512 + tsub * 128:
                                par * 4096 + k * 512 + (tsub + 1) * 128],
                            wv[:, k * H + n2 * 512: k * H + (n2 + 1) * 512],
                            start=(k == 0), stop=(k == HP - 1))
                    i.then_inc(mm.h, 1)
                    mm.v += 1

        @b.vector
        def _(v):
            vch.i(v.memset(srow[:, :], 0.0))
            for c in range(n_chunk):
                par = c % 2
                v.wait_ge(mm.h, st1_v(c))
                vch.w(v)
                vch.i(v.tensor_scalar(srow[0:1, 0:512], ps_st[0:1, 0:512],
                                      1.0 / H, None, ALU.mult))
                vch.i(v.tensor_scalar(srow[0:1, 512:1024], ps_st[0:1, 512:1024],
                                      1.0 / H, None, ALU.mult))
                vch.w(v)
                vch.i(v.tensor_tensor(srow[0:1, 1024:1536], srow[0:1, 0:512],
                                      srow[0:1, 0:512], ALU.mult))
                vch.w(v)
                vch.i(v.tensor_tensor(srow[0:1, 1024:1536], srow[0:1, 512:1024],
                                      srow[0:1, 1024:1536], ALU.subtract))
                vch.w(v)
                v.tensor_scalar(srow[0:1, 1024:1536], srow[0:1, 1024:1536],
                                1e-5, None, ALU.add).then_inc(ve.h, 1)
                ve.v += 1
                v.wait_ge(mm.h, bc_v(c))
                if c >= 2:
                    pass
                last = None
                for j in range(HP):
                    vch.i(v.tensor_tensor(xf[:, j * 512:(j + 1) * 512],
                                          xf[:, j * 512:(j + 1) * 512],
                                          ps_b[:, 0:512], ALU.subtract))
                    vch.w(v)
                    last = v.tensor_tensor(
                        xnb[:, par * 4096 + j * 512: par * 4096 + (j + 1) * 512],
                        xf[:, j * 512:(j + 1) * 512],
                        ps_b[:, 512:1024], ALU.mult)
                last.then_inc(ve.h, 1)
                ve.v += 1
                for m in range(HP):
                    v.wait_ge(mm.h, kmm_v(c, m))
                    if c >= 2:
                        stp.wait(v, st_tok(c - 2))
                    v.tensor_copy(ktb[:, par * 4096 + m * 512:
                                      par * 4096 + (m + 1) * 512],
                                  ps_p[:, (2 + m % 2) * 512:
                                       (2 + m % 2) * 512 + 512]).then_inc(ve.h, 1)
                    ve.v += 1
                for g in range(8):
                    tsub, n2 = g // 2, g % 2
                    v.wait_ge(mm.h, vmm_v(c, g))
                    v.tensor_copy(vtb[:, par * 4096 + tsub * 1024 + n2 * 512:
                                      par * 4096 + tsub * 1024 + (n2 + 1) * 512],
                                  ps_p[:, (g % 2) * 512:
                                       (g % 2) * 512 + 512]).then_inc(ve.h, 1)
                    ve.v += 1

        @b.gpsimd
        def _(gp):
            qt_hbm, kt_hbm, v_hbm = st["qt_hbm"], st["kt_hbm"], st["v_hbm"]
            for c in range(n_chunk):
                par = c % 2
                t0 = c * 512
                gp.wait_ge(sc.h, qev_v(c, 7))
                stp.issue(gp, qt_hbm[:, t0:t0 + 512]
                          .rearrange("(c p) t -> p c t", p=128),
                          qtb[:, par * 4096:(par + 1) * 4096]
                          .rearrange("p (c t) -> p c t", t=512), c)
                gp.wait_ge(ve.h, kev_v(c, 7))
                stp.issue(gp, kt_hbm[:, t0:t0 + 512]
                          .rearrange("(c p) t -> p c t", p=128),
                          ktb[:, par * 4096:(par + 1) * 4096]
                          .rearrange("p (c t) -> p c t", t=512), c)
                gp.wait_ge(ve.h, vev_v(c, 7))
                stp.issue(gp, v_hbm[t0:t0 + 512, 0:1024]
                          .rearrange("(s p) i -> p s i", p=128),
                          vtb[:, par * 4096:(par + 1) * 4096]
                          .rearrange("p (s i) -> p s i", i=1024), c)
            stp.wait_all(gp)

    assert mm.v == b_mm + n_chunk * 34
    assert sc.v == b_sc + n_chunk * 17
    assert ve.v == b_ve + n_chunk * 18
    st["_attn_stack"].close()
    del st["_attn_stack"], st["_attn_wqkv"]


def _attn_core_phase(st, layer, wo, boP, scale):
    nc = st["nc"]
    ld = st["ld"]
    n_tok = st["n_tok"]
    n_kt = n_tok // 128
    n_grp = n_tok // 512

    with (
        nc.sbuf_tensor(uname("ac_kt"), [128, HP * n_tok], BF16) as kt_sb,
        nc.sbuf_tensor(uname("ac_v"), [128, n_kt * H], BF16) as v_sb,
        nc.sbuf_tensor(uname("ac_qt"), [128, HP * 512], BF16) as qtg,
        nc.sbuf_tensor(uname("ac_pb"), [128, n_tok], BF16) as probs,
        nc.sbuf_tensor(uname("ac_pt"), [128, n_kt * 256], BF16) as probsT,
        nc.sbuf_tensor(uname("ac_cx"), [128, HP * 512], BF16) as ctxb,
        nc.sbuf_tensor(uname("ac_xr"), [128, 2 * 512], F32) as xrg,
        nc.sbuf_tensor(uname("ac_xo"), [128, 2 * 512], F32) as xout,
        nc.sbuf_tensor(uname("ac_rr"), [128, 512], F32) as rrow,
        nc.sbuf_tensor(uname("ac_rsb"), [128, 512], F32) as rsb,
        nc.sbuf_tensor(uname("ac_t1"), [128, 512], F32) as t1b,
        nc.sbuf_tensor(uname("ac_sm"), [128, 32], F32) as sm,
    ):
        with nc.Block() as b:
            @b.sync
            def _(sy):
                ld.issue(sy, kt_sb[:, :].rearrange("p (c t) -> p c t", t=n_tok),
                         st["kt_hbm"][:, 0:n_tok]
                         .rearrange("(c p) t -> p c t", p=128), 0)
                tok = ld.issue(sy, v_sb[:, :].rearrange("p (s i) -> p s i", i=H),
                               st["v_hbm"][0:n_tok, 0:1024]
                               .rearrange("(s p) i -> p s i", p=128), 0)
                ld.wait(sy, tok)

            @b.vector
            def _(v):
                v.memset(rrow[:, :], 0.0)

        for g in range(n_grp):
            _attn_group(st, g, layer, wo, boP, scale, kt_sb, v_sb, qtg,
                        probs, probsT, ctxb, xrg, xout, rrow, rsb, t1b, sm,
                        n_tok)
        if st.get("precast") is not None:
            assert not st["precast"], len(st["precast"])
            wl2 = st["wl2"]
            fins = [(wl2, 0, wl2.c[0].v), (wl2, 1, wl2.c[1].v)]
            for k in st["precast_issued"]:
                st["cached"].add(k)
                st["cstok"][k] = fins
            st["precast"] = None


def _attn_group(st, g, layer, wo, boP, scale, kt_sb, v_sb, qtg, probs,
                probsT, ctxb, xrg, xout, rrow, rsb, t1b, sm, n_tok):
    nc = st["nc"]
    xres = st["xres"]
    ld, stp, mm, sc, ve = st["ld"], st["stp"], st["mm"], st["sc"], st["ve"]
    vch = ChainCtx(st["vch"])
    eye_bf, eye_f, ones_f = st["eye_bf"], st["eye_f"], st["ones_f"]
    n_kt = n_tok // 128
    nb = n_tok // 512
    b_mm, b_sc, b_ve = mm.v, sc.v, ve.v
    b_ld = ld.bases()
    b_st = stp.bases()
    qt_hbm = st["qt_hbm"]

    def mm_sco(sub):
        return b_mm + sub * (1 + n_kt) + 8 * (sub // 2) + 1

    def mm_tr(sub, i):
        return mm_sco(sub) + 1 + i

    def mm_ctx(pair, ic):
        return b_mm + (2 + 2 * pair) * (1 + n_kt) + 8 * pair + ic + 1

    mm_rsi = b_mm + 4 * (1 + n_kt) + 16 + 1
    mm_rsb = mm_rsi + 1

    def mm_wo(m):
        return mm_rsb + m + 1

    def sc_exp(sub):
        return b_sc + sub + 1

    def ve_nmx(sub):
        return b_ve + sub * (n_kt + 2) + 8 * (sub // 2) + 1

    def ve_trev(sub, i):
        return ve_nmx(sub) + 1 + i

    def ve_rcp(sub):
        return ve_nmx(sub) + 1 + n_kt

    def ve_cxe(pair, ic):
        return b_ve + (2 + 2 * pair) * (n_kt + 2) + 8 * pair + ic + 1

    ve_rre = b_ve + 4 * (n_kt + 2) + 16 + 1
    ve_rsbe = ve_rre + 1

    def ve_woe(m):
        return ve_rsbe + m + 1

    with (
        nc.psum_tensor(uname("ag_ps"), [128, 4096], F32) as ps,
        nc.Block() as b,
    ):
        tok_xr = []
        tok_q = [None]

        @b.sync
        def _(sy):
            tok_q[0] = ld.issue(sy, qtg[:, :].rearrange("p (c t) -> p c t", t=512),
                                qt_hbm[:, g * 512:(g + 1) * 512]
                                .rearrange("(c p) t -> p c t", p=128), 2 * g)
            for m in range(HP):
                if m >= 2:
                    sy.wait_ge(ve.h, ve_woe(m - 2))
                tok_xr.append(ld.issue(
                    sy, xrg[:, (m % 2) * 512:(m % 2) * 512 + 512],
                    xres[m * 128:(m + 1) * 128, g * 512:(g + 1) * 512],
                    2 * g + m))

        @b.scalar
        def _(s):
            for sub in range(4):
                s.wait_ge(ve.h, ve_nmx(sub))
                s.activation(probs[:, :], ps[:, 0:n_tok], AF.Exp,
                             bias=sm[:, sub:sub + 1], scale=scale,
                             accum_out=sm[:, 4 + sub:5 + sub]).then_inc(sc.h, 1)
                sc.v += 1

        @b.tensor
        def _(tn):
            ld.wait(tn, tok_xr[0])
            for sub in range(4):
                pair = sub // 2
                if sub >= 1:
                    tn.wait_ge(ve.h, ve_trev(sub - 1, n_kt - 1))
                if sub == 2:
                    tn.wait_ge(ve.h, ve_cxe(0, 7))
                i = None
                for n in range(nb):
                    for k in range(HP):
                        i = tn.matmul(
                            ps[:, n * 512:(n + 1) * 512],
                            qtg[:, k * 512 + sub * 128: k * 512 + (sub + 1) * 128],
                            kt_sb[:, k * n_tok + n * 512: k * n_tok + (n + 1) * 512],
                            start=(k == 0), stop=(k == HP - 1))
                i.then_inc(mm.h, 1)
                mm.v += 1
                tn.wait_ge(sc.h, sc_exp(sub))
                for idx in range(n_kt):
                    bank = idx % nb
                    slot = idx // nb
                    if idx >= nb:
                        tn.wait_ge(ve.h, ve_trev(sub, idx - nb))
                    pt = ps[:, bank * 512:(bank + 1) * 512]
                    pt = pt.bitcast(BF16)[:, slot * 128:(slot + 1) * 128]
                    tn.transpose(pt, probs[:, idx * 128:(idx + 1) * 128],
                                 eye_bf[:, :]).then_inc(mm.h, 1)
                    mm.v += 1
                if sub % 2 == 1:
                    tn.wait_ge(ve.h, ve_trev(sub, n_kt - 1))
                    for ic in range(HP):
                        off = (ic % 2) * 512
                        if ic >= 2:
                            tn.wait_ge(ve.h, ve_cxe(pair, ic - 2))
                        i = None
                        for kt in range(n_kt):
                            i = tn.matmul(
                                ps[:, off:off + 256],
                                v_sb[:, kt * H + ic * 128: kt * H + (ic + 1) * 128],
                                probsT[:, kt * 256:(kt + 1) * 256],
                                start=(kt == 0), stop=(kt == n_kt - 1))
                        i.then_inc(mm.h, 1)
                        mm.v += 1
            tn.wait_ge(ve.h, ve_rcp(3))
            i = None
            for sub in range(4):
                i = tn.matmul(ps[0:1, 1024 + sub * 128:1024 + (sub + 1) * 128],
                              sm[:, 8 + sub:9 + sub], eye_f[:, :],
                              start=True, stop=True)
            i.then_inc(mm.h, 1)
            mm.v += 1
            tn.wait_ge(ve.h, ve_rre)
            tn.matmul(ps[:, 1536:2048], st["bmat0"][:, :], rrow[:, 0:512],
                      start=True, stop=True).then_inc(mm.h, 1)
            mm.v += 1
            tn.wait_ge(ve.h, ve_rsbe)
            for m in range(HP):
                off = (m % 2) * 512
                if m >= 2:
                    tn.wait_ge(ve.h, ve_woe(m - 2))
                i = None
                for k in range(HP):
                    i = tn.matmul(ps[:, off:off + 512],
                                  wo[:, k * H + m * 128: k * H + (m + 1) * 128],
                                  ctxb[:, k * 512:(k + 1) * 512],
                                  start=(k == 0), stop=(k == HP - 1))
                i.then_inc(mm.h, 1)
                mm.v += 1

        @b.vector
        def _(v):
            for sub in range(4):
                pair = sub // 2
                v.wait_ge(mm.h, mm_sco(sub))
                vch.i(v.reduce_max(sm[:, 24 + sub:25 + sub], ps[:, 0:n_tok],
                                   axis=AX.X, negate=True))
                vch.w(v)
                v.tensor_scalar(sm[:, sub:sub + 1], sm[:, 24 + sub:25 + sub],
                                scale, None, ALU.mult).then_inc(ve.h, 1)
                ve.v += 1
                for idx in range(n_kt):
                    bank = idx % nb
                    slot = idx // nb
                    v.wait_ge(mm.h, mm_tr(sub, idx))
                    if sub == 2 and idx == 0:
                        v.wait_ge(mm.h, mm_ctx(0, 7))
                    pt = ps[:, bank * 512:(bank + 1) * 512]
                    pt = pt.bitcast(BF16)[:, slot * 128:(slot + 1) * 128]
                    v.tensor_copy(
                        probsT[:, idx * 256 + (sub % 2) * 128:
                               idx * 256 + (sub % 2) * 128 + 128],
                        pt).then_inc(ve.h, 1)
                    ve.v += 1
                v.wait_ge(sc.h, sc_exp(sub))
                v.reciprocal(sm[:, 8 + sub:9 + sub],
                             sm[:, 4 + sub:5 + sub]).then_inc(ve.h, 1)
                ve.v += 1
                if sub % 2 == 1:
                    for ic in range(HP):
                        off = (ic % 2) * 512
                        v.wait_ge(mm.h, mm_ctx(pair, ic))
                        v.tensor_copy(
                            ctxb[:, ic * 512 + pair * 256:
                                 ic * 512 + pair * 256 + 256],
                            ps[:, off:off + 256]).then_inc(ve.h, 1)
                        ve.v += 1
            v.wait_ge(mm.h, mm_rsi)
            v.tensor_copy(rrow[0:1, 0:512],
                          ps[0:1, 1024:1536]).then_inc(ve.h, 1)
            ve.v += 1
            v.wait_ge(mm.h, mm_rsb)
            vch.i(v.tensor_copy(rsb[:, :], ps[:, 1536:2048]))
            vch.w(v)
            v.engine_nop().then_inc(ve.h, 1)
            ve.v += 1
            for m in range(HP):
                off = (m % 2) * 512
                v.wait_ge(mm.h, mm_wo(m))
                if m >= 2:
                    pm = (g * 8 + m - 2) % 2
                    stp.wait(v, (pm, b_st[pm] + ((m - 2) // 2 + 1) * 16))
                vch.w(v)
                vch.i(v.tensor_tensor(t1b[:, :], ps[:, off:off + 512],
                                      rsb[:, :], ALU.mult))
                vch.w(v)
                ld.wait(v, tok_xr[m])
                i2 = v.scalar_tensor_tensor(
                    xout[:, (m % 2) * 512:(m % 2) * 512 + 512], t1b[:, :],
                    boP[:, m:m + 1], xrg[:, (m % 2) * 512:(m % 2) * 512 + 512],
                    ALU.add, ALU.add)
                vch.i(i2)
                vch.w(v)
                v.engine_nop().then_inc(ve.h, 1)
                ve.v += 1

        @b.gpsimd
        def _(gp):
            for m in range(HP):
                gp.wait_ge(ve.h, ve_woe(m))
                stp.issue(gp, xres[m * 128:(m + 1) * 128,
                                   g * 512:(g + 1) * 512],
                          xout[:, (m % 2) * 512:(m % 2) * 512 + 512],
                          g * 8 + m)
            pc = st.get("precast")
            if pc:
                wl2 = st["wl2"]
                take = -(-len(pc) // (nb - g))
                for _ in range(take):
                    if not pc:
                        break
                    dst, src, keys = pc.pop(0)
                    wl2.issue(gp, dst, src, 2 * g)
                    st["precast_issued"].extend(keys)
            stp.wait_all(gp)

    assert mm.v == mm_wo(HP - 1), (mm.v, mm_wo(HP - 1))
    assert sc.v == b_sc + 4
    assert ve.v == ve_woe(HP - 1)




def emit_l1ffn(st):
    nc = st["nc"]
    n_tok = st["n_tok"]
    with (
        nc.sbuf_tensor(uname("lf_we"), [128, (n_tok // 128) * 8], F32) as we_sb,
        nc.sbuf_tensor(uname("lf_nw"), [128, 16], F32) as nw_sb,
    ):
        with (
            nc.sbuf_tensor(uname("lf_kc"), [128, 2 * n_tok], BF16) as kc_sb,
            nc.sbuf_tensor(uname("lf_vc"), [128, 2 * n_tok], BF16) as vc_sb,
        ):
            _l1_p0(st, kc_sb, vc_sb, nw_sb)
            _l1_p1(st, kc_sb, vc_sb)
        _l1_p2(st, we_sb, nw_sb)
        _l1_p3(st, we_sb)
        _l1_p4(st, nw_sb)


def _l1_rope(v, vch, ps_bank0, ps_bank1, roped, cos_t, sin_t, tmp):
    for n2, bank in ((0, ps_bank0), (1, ps_bank1)):
        qe = bank.rearrange("p (h two i) -> p h two i", two=2, i=32)[:, :, 0, :]
        qo = bank.rearrange("p (h two i) -> p h two i", two=2, i=32)[:, :, 1, :]
        cs = cos_t[:, n2 * 256:(n2 + 1) * 256].rearrange("p (h i) -> p h i", i=32)
        sn = sin_t[:, n2 * 256:(n2 + 1) * 256].rearrange("p (h i) -> p h i", i=32)
        t1 = tmp[:, 0:256].rearrange("p (h i) -> p h i", i=32)
        t2 = tmp[:, 256:512].rearrange("p (h i) -> p h i", i=32)
        ev = roped[:, n2 * 512:(n2 + 1) * 512] \
            .rearrange("p (h two i) -> p h two i", two=2, i=32)[:, :, 0, :]
        od = roped[:, n2 * 512:(n2 + 1) * 512] \
            .rearrange("p (h two i) -> p h two i", two=2, i=32)[:, :, 1, :]
        vch.w(v)
        vch.i(v.tensor_tensor(t1, qe, cs, ALU.mult))
        vch.i(v.tensor_tensor(t2, qo, sn, ALU.mult))
        vch.w(v)
        vch.i(v.tensor_tensor(ev, t1, t2, ALU.subtract))
        vch.w(v)
        vch.i(v.tensor_tensor(t1, qe, sn, ALU.mult))
        vch.i(v.tensor_tensor(t2, qo, cs, ALU.mult))
        vch.w(v)
        vch.i(v.tensor_tensor(od, t1, t2, ALU.add))


def _l1_p0(st, kc_sb, vc_sb, nw_sb):
    nc = st["nc"]
    w = st["w"]
    xres = st["xres"]
    ld, wl, stp, mm, sc, ve = (st["ld"], st["wl"], st["stp"], st["mm"],
                               st["sc"], st["ve"])
    vch = ChainCtx(st["vch"])
    sch = ChainCtx(st["sch"])
    ones_f = st["ones_f"]
    eye_bf = st["eye_bf"]
    n_tok = st["n_tok"]
    n_chunk = n_tok // 512
    b_mm, b_sc, b_ve = mm.v, sc.v, ve.v
    b_ld = ld.bases()
    b_wl = wl.bases()
    b_st = stp.bases()

    MMC = 10 + 4 * 10 + 4
    SCC = 9
    VEC = 3 + 4 * 9 + 4

    def s2_v(c, j):
        return b_mm + c * MMC + j + 1

    def st1_v(c):
        return b_mm + c * MMC + 9

    def bc_v(c):
        return b_mm + c * MMC + 10

    def qmm_v(c, t, n2):
        return b_mm + c * MMC + 10 + t * 10 + n2 + 1

    def qtr_v(c, t, i):
        return b_mm + c * MMC + 10 + t * 10 + 2 + i + 1

    def kcm_v(c, m):
        return b_mm + c * MMC + 50 + m + 1

    def vcm_v(c, m):
        return b_mm + c * MMC + 52 + m + 1

    def sq_v(c, j):
        return b_sc + c * SCC + j + 1

    def scA_v(c):
        return b_sc + c * SCC + 9

    def veA_v(c):
        return b_ve + c * VEC + 1

    def nrm_v(c):
        return b_ve + c * VEC + 2

    def xnf_v(c):
        return b_ve + c * VEC + 3

    def rope_v(c, t):
        return b_ve + c * VEC + 3 + t * 9 + 1

    def qtev_v(c, t, i):
        return b_ve + c * VEC + 3 + t * 9 + 1 + i + 1

    def kcev_v(c, m):
        return b_ve + c * VEC + 39 + m + 1

    def vcev_v(c, m):
        return b_ve + c * VEC + 41 + m + 1

    def ld_tok(c):
        return (c % 2, b_ld[c % 2] + (c // 2 + 1) * 16 * 9)

    def st_tok(c):
        return (c % 2, b_st[c % 2] + (c // 2 + 1) * 32)

    with (
        nc.sbuf_tensor(uname("p0_wq"), [128, HP * H], BF16) as wq,
        nc.sbuf_tensor(uname("p0_wkc"), [128, HP * 256], BF16) as wkc,
        nc.sbuf_tensor(uname("p0_wvc"), [128, HP * 256], BF16) as wvc,
        nc.sbuf_tensor(uname("p0_xf"), [128, HP * 512], F32) as xf,
        nc.sbuf_tensor(uname("p0_xq"), [128, 1024], F32) as xq,
        nc.sbuf_tensor(uname("p0_srow"), [128, 2048], F32) as srow,
        nc.sbuf_tensor(uname("p0_xn"), [128, HP * 512], BF16) as xnb,
        nc.sbuf_tensor(uname("p0_xnf"), [128, HP * 512], F32) as xnf,
        nc.sbuf_tensor(uname("p0_rope"), [128, 4 * 1024], F32) as rop,
        nc.sbuf_tensor(uname("p0_rop2"), [128, 1024], F32) as rtmp,
        nc.sbuf_tensor(uname("p0_rpd"), [128, 1024], BF16) as roped,
        nc.sbuf_tensor(uname("p0_qt"), [128, 2 * HP * 512], BF16) as qtb,
        nc.psum_tensor(uname("p0_ps_st"), [128, 1024], F32) as ps_st,
        nc.psum_tensor(uname("p0_ps_b"), [128, 1024], F32) as ps_b,
        nc.psum_tensor(uname("p0_ps_p"), [128, 1024], F32) as ps_p,
        nc.psum_tensor(uname("p0_ps_t"), [128, 1024], F32) as ps_t,
        nc.Block() as b,
    ):
        jobs = CacheJobs(st)

        @b.gpsimd
        def _(gp):
            wbf = st["wbf"]
            for nm, buf, tt in (("L1_ffn_Wq", wq, H), ("L1_ffn_Wk_c", wkc, 256),
                                ("L1_ffn_Wv_c", wvc, 256)):
                _wload(st, gp, wl,
                       buf[:, :].rearrange("p (c t) -> p c t", t=tt),
                       (nm,),
                       w[nm][:, :].rearrange("(c p) t -> p c t", p=128),
                       wbf[nm][:, :].rearrange("(c p) t -> p c t", p=128),
                       0, jobs, ready=(mm.h, s2_v(0, 0)))

        @b.sync
        def _(sy):
            for c in range(n_chunk):
                t0 = c * 512
                if c == 1:
                    _emit_cache_stores(st, sy, jobs)
                if c >= 1:
                    sy.wait_ge(ve.h, nrm_v(c - 1))
                    sy.wait_ge(ve.h, xnf_v(c - 1))
                ld.issue(sy, xf[:, :].rearrange("p (c t) -> p c t", t=512),
                         xres[:, t0:t0 + 512]
                         .rearrange("(c p) t -> p c t", p=128), c)
                for t in range(4):
                    if c >= 1:
                        sy.wait_ge(ve.h, rope_v(c - 1, t))
                    ld.issue(sy, rop[:, t * 1024:t * 1024 + 512],
                             st["rope_cos"][t0 + t * 128:t0 + (t + 1) * 128, :], c)
                    ld.issue(sy, rop[:, t * 1024 + 512:(t + 1) * 1024],
                             st["rope_sin"][t0 + t * 128:t0 + (t + 1) * 128, :], c)

        @b.scalar
        def _(s):
            for c in range(n_chunk):
                ld.wait(s, ld_tok(c))
                if c >= 1:
                    s.wait_ge(mm.h, st1_v(c - 1))
                for j in range(HP):
                    jp = j % 2
                    if j >= 2:
                        s.wait_ge(mm.h, s2_v(c, j - 2))
                    s.activation(xq[:, jp * 512:(jp + 1) * 512],
                                 xf[:, j * 512:(j + 1) * 512],
                                 AF.Square).then_inc(sc.h, 1)
                    sc.v += 1
                s.wait_ge(ve.h, veA_v(c))
                sch.i(s.activation(srow[0:1, 1536:2048],
                                   srow[0:1, 1024:1536], AF.Ln))
                sch.w(s)
                s.activation(srow[0:1, 1536:2048], srow[0:1, 1536:2048],
                             AF.Exp, scale=-0.5).then_inc(sc.h, 1)
                sc.v += 1

        @b.tensor
        def _(tn):
            wl.wait(tn, (0, b_wl[0] + 48))
            for c in range(n_chunk):
                par = c % 2
                if c >= 1:
                    tn.wait_ge(ve.h, veA_v(c - 1))
                    tn.wait_ge(ve.h, vcev_v(c - 1, 1))
                for j in range(HP):
                    jp = j % 2
                    tn.wait_ge(sc.h, sq_v(c, j))
                    tn.matmul(ps_st[0:1, 512:1024], ones_f[:, 0:1],
                              xq[:, jp * 512:(jp + 1) * 512],
                              start=(j == 0), stop=(j == HP - 1)).then_inc(mm.h, 1)
                    mm.v += 1
                i = None
                for j in range(HP):
                    i = tn.matmul(ps_st[0:1, 0:512], ones_f[:, 0:1],
                                  xf[:, j * 512:(j + 1) * 512],
                                  start=(j == 0), stop=(j == HP - 1))
                i.then_inc(mm.h, 1)
                mm.v += 1
                tn.wait_ge(sc.h, scA_v(c))
                if c >= 1:
                    tn.wait_ge(ve.h, nrm_v(c - 1))
                tn.matmul(ps_b[:, 0:512], st["bmat0"][:, :], srow[:, 0:512],
                          start=True, stop=True)
                tn.matmul(ps_b[:, 512:1024], st["bmat0"][:, :],
                          srow[:, 1536:2048], start=True,
                          stop=True).then_inc(mm.h, 1)
                mm.v += 1
                tn.wait_ge(ve.h, nrm_v(c))
                for t in range(4):
                    for n2 in range(2):
                        if not (c == 0 and t == 0):
                            pc, pt_ = (c, t - 1) if t > 0 else (c - 1, 3)
                            tn.wait_ge(ve.h, rope_v(pc, pt_))
                        i = None
                        for k in range(HP):
                            i = tn.matmul(
                                ps_p[:, n2 * 512:(n2 + 1) * 512],
                                xnb[:, k * 512 + t * 128: k * 512 + (t + 1) * 128],
                                wq[:, k * H + n2 * 512: k * H + (n2 + 1) * 512],
                                start=(k == 0), stop=(k == HP - 1))
                        i.then_inc(mm.h, 1)
                        mm.v += 1
                    tn.wait_ge(ve.h, rope_v(c, t))
                    for i8 in range(HP):
                        bank = i8 % 2
                        slot = i8 // 2
                        if i8 >= 2:
                            tn.wait_ge(ve.h, qtev_v(c, t, i8 - 2))
                        pt_ap = ps_t[:, bank * 512:(bank + 1) * 512]
                        pt_ap = pt_ap.bitcast(BF16)[:, slot * 128:(slot + 1) * 128]
                        if i8 >= 2:
                            pass
                        tn.transpose(pt_ap,
                                     roped[:, i8 * 128:(i8 + 1) * 128],
                                     eye_bf[:, :]).then_inc(mm.h, 1)
                        mm.v += 1
                tn.wait_ge(ve.h, rope_v(c, 3))
                for m in range(2):
                    if c >= 1:
                        tn.wait_ge(ve.h, kcev_v(c - 1, m))
                    i = None
                    for k in range(HP):
                        i = tn.matmul(ps_p[:, m * 512:m * 512 + 512],
                                      wkc[:, k * 256 + m * 128: k * 256 + (m + 1) * 128],
                                      xnb[:, k * 512:(k + 1) * 512],
                                      start=(k == 0), stop=(k == HP - 1))
                    i.then_inc(mm.h, 1)
                    mm.v += 1
                tn.wait_ge(ve.h, veA_v(c))
                for m in range(2):
                    if c >= 1:
                        tn.wait_ge(ve.h, vcev_v(c - 1, m))
                    i = None
                    for k in range(HP):
                        i = tn.matmul(ps_st[:, m * 512:m * 512 + 512],
                                      wvc[:, k * 256 + m * 128: k * 256 + (m + 1) * 128],
                                      xnb[:, k * 512:(k + 1) * 512],
                                      start=(k == 0), stop=(k == HP - 1))
                    i.then_inc(mm.h, 1)
                    mm.v += 1

        @b.vector
        def _(v):
            vch.i(v.memset(srow[:, :], 0.0))
            for c in range(n_chunk):
                par = c % 2
                v.wait_ge(mm.h, st1_v(c))
                vch.w(v)
                vch.i(v.tensor_scalar(srow[0:1, 0:512], ps_st[0:1, 0:512],
                                      1.0 / H, None, ALU.mult))
                vch.i(v.tensor_scalar(srow[0:1, 512:1024], ps_st[0:1, 512:1024],
                                      1.0 / H, None, ALU.mult))
                vch.w(v)
                vch.i(v.tensor_tensor(srow[0:1, 1024:1536], srow[0:1, 0:512],
                                      srow[0:1, 0:512], ALU.mult))
                vch.w(v)
                vch.i(v.tensor_tensor(srow[0:1, 1024:1536], srow[0:1, 512:1024],
                                      srow[0:1, 1024:1536], ALU.subtract))
                vch.w(v)
                v.tensor_scalar(srow[0:1, 1024:1536], srow[0:1, 1024:1536],
                                1e-5, None, ALU.add).then_inc(ve.h, 1)
                ve.v += 1
                v.wait_ge(mm.h, bc_v(c))
                if c >= 2:
                    stp.wait(v, st_tok(c - 2))
                last = None
                for j in range(HP):
                    vch.i(v.tensor_tensor(xf[:, j * 512:(j + 1) * 512],
                                          xf[:, j * 512:(j + 1) * 512],
                                          ps_b[:, 0:512], ALU.subtract))
                    vch.w(v)
                    last = v.tensor_tensor(
                        xnb[:, j * 512:(j + 1) * 512],
                        xf[:, j * 512:(j + 1) * 512],
                        ps_b[:, 512:1024], ALU.mult)
                last.then_inc(ve.h, 1)
                ve.v += 1
                last = None
                if c >= 1:
                    stp.wait(v, st_tok(c - 1))
                for j in range(HP):
                    vch.w(v)
                    last = v.tensor_tensor(
                        xnf[:, j * 512:(j + 1) * 512],
                        xf[:, j * 512:(j + 1) * 512],
                        ps_b[:, 512:1024], ALU.mult)
                last.then_inc(ve.h, 1)
                ve.v += 1
                for t in range(4):
                    v.wait_ge(mm.h, qmm_v(c, t, 1))
                    _l1_rope(v, vch, ps_p[:, 0:512], ps_p[:, 512:1024],
                             roped, rop[:, t * 1024:t * 1024 + 512],
                             rop[:, t * 1024 + 512:(t + 1) * 1024], rtmp)
                    vch.w(v)
                    v.engine_nop().then_inc(ve.h, 1)
                    ve.v += 1
                    for i8 in range(HP):
                        bank = i8 % 2
                        slot = i8 // 2
                        v.wait_ge(mm.h, qtr_v(c, t, i8))
                        pt_ap = ps_t[:, bank * 512:(bank + 1) * 512]
                        pt_ap = pt_ap.bitcast(BF16)[:, slot * 128:(slot + 1) * 128]
                        v.tensor_copy(
                            qtb[:, par * 4096 + i8 * 512 + t * 128:
                                par * 4096 + i8 * 512 + (t + 1) * 128],
                            pt_ap).then_inc(ve.h, 1)
                        ve.v += 1
                for m in range(2):
                    v.wait_ge(mm.h, kcm_v(c, m))
                    v.tensor_copy(kc_sb[:, m * n_tok + c * 512:
                                        m * n_tok + (c + 1) * 512],
                                  ps_p[:, m * 512:m * 512 + 512]).then_inc(ve.h, 1)
                    ve.v += 1
                for m in range(2):
                    v.wait_ge(mm.h, vcm_v(c, m))
                    v.tensor_copy(vc_sb[:, m * n_tok + c * 512:
                                        m * n_tok + (c + 1) * 512],
                                  ps_st[:, m * 512:m * 512 + 512]).then_inc(ve.h, 1)
                    ve.v += 1

        @b.gpsimd
        def _(gp):
            for c in range(n_chunk):
                par = c % 2
                t0 = c * 512
                gp.wait_ge(ve.h, xnf_v(c))
                stp.issue(gp, st["xnf_hbm"][:, t0:t0 + 512]
                          .rearrange("(c p) t -> p c t", p=128),
                          xnf[:, :].rearrange("p (c t) -> p c t", t=512), c)
                gp.wait_ge(ve.h, qtev_v(c, 3, HP - 1))
                stp.issue(gp, st["qt_hbm"][:, t0:t0 + 512]
                          .rearrange("(c p) t -> p c t", p=128),
                          qtb[:, par * 4096:(par + 1) * 4096]
                          .rearrange("p (c t) -> p c t", t=512), c)
            stp.wait_all(gp)

    assert mm.v == b_mm + n_chunk * MMC, (mm.v, b_mm, MMC)
    assert sc.v == b_sc + n_chunk * SCC
    assert ve.v == b_ve + n_chunk * VEC, (ve.v, b_ve, VEC)


def _l1_p1(st, kc_sb, vc_sb):
    nc = st["nc"]
    w = st["w"]
    ld, wl, stp, mm, sc, ve = (st["ld"], st["wl"], st["stp"], st["mm"],
                               st["sc"], st["ve"])
    vch = ChainCtx(st["vch"])
    eye_bf = st["eye_bf"]
    n_tok = st["n_tok"]
    n_chunk = n_tok // 512
    b_mm, b_ve = mm.v, ve.v
    b_ld = ld.bases()
    b_wl = wl.bases()
    b_st = stp.bases()

    MMC = 48
    VEC = 44

    def kmm_v(c, t, n2):
        return b_mm + c * MMC + t * 12 + n2 + 1

    def ktr_v(c, t, i):
        return b_mm + c * MMC + t * 12 + 2 + i + 1

    def vmm_v(c, t, n2):
        return b_mm + c * MMC + t * 12 + 10 + n2 + 1

    def rope_v(c, t):
        return b_ve + c * VEC + t * 11 + 1

    def ktev_v(c, t, i):
        return b_ve + c * VEC + t * 11 + 1 + i + 1

    def vev_v(c, t, n2):
        return b_ve + c * VEC + t * 11 + 9 + n2 + 1

    def ld_tok(c):
        return (c % 2, b_ld[c % 2] + (c // 2 + 1) * 16 * 8)

    def st_tok(c):
        return (c % 2, b_st[c % 2] + (c // 2 + 1) * 32 * 16)

    with (
        nc.sbuf_tensor(uname("p1_wk"), [128, 2 * H], BF16) as wk,
        nc.sbuf_tensor(uname("p1_wv"), [128, 2 * H], BF16) as wv,
        nc.sbuf_tensor(uname("p1_rope"), [128, 4 * 1024], F32) as rop,
        nc.sbuf_tensor(uname("p1_rop2"), [128, 1024], F32) as rtmp,
        nc.sbuf_tensor(uname("p1_rpd"), [128, 1024], BF16) as roped,
        nc.sbuf_tensor(uname("p1_kt"), [128, 2 * HP * 512], BF16) as ktb,
        nc.sbuf_tensor(uname("p1_vt"), [128, 2 * 4 * 1024], BF16) as vtb,
        nc.psum_tensor(uname("p1_ps_p"), [128, 1024], F32) as ps_p,
        nc.psum_tensor(uname("p1_ps_v"), [128, 1024], F32) as ps_v,
        nc.psum_tensor(uname("p1_ps_t"), [128, 1024], F32) as ps_t,
        nc.Block() as b,
    ):
        jobs = CacheJobs(st)

        @b.gpsimd
        def _(gp):
            wbf = st["wbf"]
            for nm, buf in (("L1_ffn_Wk", wk), ("L1_ffn_Wv", wv)):
                _wload(st, gp, wl,
                       buf[:, :].rearrange("p (c t) -> p c t", t=H),
                       (nm,),
                       w[nm][:, :].rearrange("(c p) t -> p c t", p=128),
                       wbf[nm][:, :].rearrange("(c p) t -> p c t", p=128),
                       0, jobs, ready=(mm.h, kmm_v(0, 0, 0)))

        @b.sync
        def _(sy):
            for c in range(n_chunk):
                t0 = c * 512
                if c == 1:
                    _emit_cache_stores(st, sy, jobs)
                for t in range(4):
                    if c >= 1:
                        sy.wait_ge(ve.h, rope_v(c - 1, t))
                    ld.issue(sy, rop[:, t * 1024:t * 1024 + 512],
                             st["rope_cos"][t0 + t * 128:t0 + (t + 1) * 128, :], c)
                    ld.issue(sy, rop[:, t * 1024 + 512:(t + 1) * 1024],
                             st["rope_sin"][t0 + t * 128:t0 + (t + 1) * 128, :], c)

        @b.tensor
        def _(tn):
            wl.wait(tn, (0, b_wl[0] + 32))
            for c in range(n_chunk):
                for t in range(4):
                    for n2 in range(2):
                        if not (c == 0 and t == 0):
                            pc, pt_ = (c, t - 1) if t > 0 else (c - 1, 3)
                            tn.wait_ge(ve.h, rope_v(pc, pt_))
                        i = None
                        for k2 in range(2):
                            i = tn.matmul(
                                ps_p[:, n2 * 512:(n2 + 1) * 512],
                                kc_sb[:, k2 * n_tok + c * 512 + t * 128:
                                      k2 * n_tok + c * 512 + (t + 1) * 128],
                                wk[:, k2 * H + n2 * 512: k2 * H + (n2 + 1) * 512],
                                start=(k2 == 0), stop=(k2 == 1))
                        i.then_inc(mm.h, 1)
                        mm.v += 1
                    tn.wait_ge(ve.h, rope_v(c, t))
                    for i8 in range(HP):
                        bank = i8 % 2
                        slot = i8 // 2
                        if i8 >= 2:
                            tn.wait_ge(ve.h, ktev_v(c, t, i8 - 2))
                        pt_ap = ps_t[:, bank * 512:(bank + 1) * 512]
                        pt_ap = pt_ap.bitcast(BF16)[:, slot * 128:(slot + 1) * 128]
                        tn.transpose(pt_ap,
                                     roped[:, i8 * 128:(i8 + 1) * 128],
                                     eye_bf[:, :]).then_inc(mm.h, 1)
                        mm.v += 1
                    for n2 in range(2):
                        if not (c == 0 and t == 0):
                            pc, pt_ = (c, t - 1) if t > 0 else (c - 1, 3)
                            tn.wait_ge(ve.h, vev_v(pc, pt_, n2))
                        i = None
                        for k2 in range(2):
                            i = tn.matmul(
                                ps_v[:, n2 * 512:(n2 + 1) * 512],
                                vc_sb[:, k2 * n_tok + c * 512 + t * 128:
                                      k2 * n_tok + c * 512 + (t + 1) * 128],
                                wv[:, k2 * H + n2 * 512: k2 * H + (n2 + 1) * 512],
                                start=(k2 == 0), stop=(k2 == 1))
                        i.then_inc(mm.h, 1)
                        mm.v += 1

        @b.vector
        def _(v):
            for c in range(n_chunk):
                par = c % 2
                for t in range(4):
                    v.wait_ge(mm.h, kmm_v(c, t, 1))
                    ld.wait(v, ld_tok(c))
                    if c >= 2:
                        stp.wait(v, st_tok(c - 2))
                    _l1_rope(v, vch, ps_p[:, 0:512], ps_p[:, 512:1024],
                             roped, rop[:, t * 1024:t * 1024 + 512],
                             rop[:, t * 1024 + 512:(t + 1) * 1024], rtmp)
                    vch.w(v)
                    v.engine_nop().then_inc(ve.h, 1)
                    ve.v += 1
                    for i8 in range(HP):
                        bank = i8 % 2
                        slot = i8 // 2
                        v.wait_ge(mm.h, ktr_v(c, t, i8))
                        pt_ap = ps_t[:, bank * 512:(bank + 1) * 512]
                        pt_ap = pt_ap.bitcast(BF16)[:, slot * 128:(slot + 1) * 128]
                        v.tensor_copy(
                            ktb[:, par * 4096 + i8 * 512 + t * 128:
                                par * 4096 + i8 * 512 + (t + 1) * 128],
                            pt_ap).then_inc(ve.h, 1)
                        ve.v += 1
                    for n2 in range(2):
                        v.wait_ge(mm.h, vmm_v(c, t, n2))
                        v.tensor_copy(vtb[:, par * 4096 + t * 1024 + n2 * 512:
                                          par * 4096 + t * 1024 + (n2 + 1) * 512],
                                      ps_v[:, n2 * 512:(n2 + 1) * 512]) \
                            .then_inc(ve.h, 1)
                        ve.v += 1

        @b.gpsimd
        def _(gp):
            kt_bd, v_bd = st["kt_bd"], st["v_bd"]
            for c in range(n_chunk):
                par = c % 2
                if c == 0:
                    st["cs"].wait_all(gp)
                gp.wait_ge(ve.h, ktev_v(c, 3, HP - 1))
                for hh in range(8):
                    stp.issue(gp, kt_bd[hh, 8 * c:8 * c + 8, 0:64, 0:64]
                              .rearrange("c p t -> p c t"),
                              ktb[0:64, par * 4096 + hh * 512:
                                  par * 4096 + (hh + 1) * 512]
                              .rearrange("p (c t) -> p c t", t=64), c)
                    stp.issue(gp, kt_bd[hh, 8 * c:8 * c + 8, 64:128, 64:128]
                              .rearrange("c p t -> p c t"),
                              ktb[64:128, par * 4096 + hh * 512:
                                  par * 4096 + (hh + 1) * 512]
                              .rearrange("p (c t) -> p c t", t=64), c)
                gp.wait_ge(ve.h, vev_v(c, 3, 1))
                for half in range(2):
                    for t in range(4):
                        src = vtb[half * 64:(half + 1) * 64,
                                  par * 4096 + t * 1024:
                                  par * 4096 + (t + 1) * 1024] \
                            .rearrange("p (hh d) -> p hh d", d=64)
                        kb = 8 * c + 2 * t + half
                        stp.issue(gp, v_bd[0:8, kb, 0:64, 0:64]
                                  .rearrange("h p t -> p h t"),
                                  src[:, 0::2, :], c)
                        stp.issue(gp, v_bd[0:8, kb, 64:128, 64:128]
                                  .rearrange("h p t -> p h t"),
                                  src[:, 1::2, :], c)
            stp.wait_all(gp)

    assert mm.v == b_mm + n_chunk * MMC
    assert ve.v == b_ve + n_chunk * VEC


def _l1_p2(st, we_sb, nw_sb):
    nc = st["nc"]
    w = st["w"]
    ld, wl, stp, mm, sc, ve = (st["ld"], st["wl"], st["stp"], st["mm"],
                               st["sc"], st["ve"])
    ones_f, eye_f = st["ones_f"], st["eye_f"]
    n_tok = st["n_tok"]
    n_kt = n_tok // 128
    n_grp = n_tok // 512

    KB = n_tok // 64
    with (
        nc.sbuf_tensor(uname("p2_kt"), [128, 2 * KB * 128], BF16) as ktw,
        nc.sbuf_tensor(uname("p2_v"), [128, 2 * KB * 128], BF16) as vw,
        nc.sbuf_tensor(uname("p2_obd"), [128, 128], F32) as obd,
        nc.sbuf_tensor(uname("p2_osum"), [128, 2], BF16) as osum,
        nc.sbuf_tensor(uname("p2_wo"), [128, HP * H], BF16) as wo,
        nc.sbuf_tensor(uname("p2_qt"), [128, 2 * HP * 512], BF16) as qtg,
        nc.sbuf_tensor(uname("p2_ptc"), [128, 2048], BF16) as ptc,
        nc.sbuf_tensor(uname("p2_x1"), [128, HP * 512], F32) as x1sb,
        nc.sbuf_tensor(uname("p2_xnf"), [128, 2 * 512], F32) as xnfc,
        nc.sbuf_tensor(uname("p2_rr"), [128, 512], F32) as rrow,
        nc.sbuf_tensor(uname("p2_rb"), [128, 512], F32) as rbsb,
        nc.sbuf_tensor(uname("p2_sq"), [128, 1024], F32) as sqtmp,
        nc.sbuf_tensor(uname("p2_wg"), [128, HP * 10], F32) as wgc_sb,
        nc.sbuf_tensor(uname("p2_gb"), [128, 16], F32) as gb_sb,
        nc.sbuf_tensor(uname("p2_gs"), [128, 192], F32) as gsc,
        nc.sbuf_tensor(uname("p2_br"), [128, 1056], F32) as brow,
    ):
        with (
            nc.psum_tensor(uname("p2_pp"), [128, 1024], F32) as pp,
            nc.Block() as b,
        ):
            b_mm0, b_ve0 = mm.v, ve.v
            b_ld0 = ld.bases()
            jobs = CacheJobs(st)

            tok_wo = [None]

            @b.gpsimd
            def _(gp):
                tok_wo[0], _ = _wload(
                    st, gp, wl,
                    wo[:, :].rearrange("p (c t) -> p c t", t=H),
                    ("L1_ffn_Wo",),
                    w["L1_ffn_Wo"][:, :]
                    .rearrange("(c p) t -> p c t", p=128),
                    st["wbf"]["L1_ffn_Wo"][:, :]
                    .rearrange("(c p) t -> p c t", p=128), 0, jobs)

            @b.sync
            def _(sy):
                for c in range(HP):
                    ld.issue(sy, wgc_sb[:, c * 10:c * 10 + 2],
                             w["L1_ffn_Wg_gate"][c * 128:(c + 1) * 128, :], 0)
                    ld.issue(sy, wgc_sb[:, c * 10 + 2:c * 10 + 10],
                             w["L1_ffn_We_gate"][c * 128:(c + 1) * 128, :], 0)
                ld.issue(sy, brow[0:1, 0:1024], w["L1_ffn_norm1_w"][None, :], 0)
                ld.issue(sy, brow[32:33, 0:2], w["L1_ffn_group_bias"][None, :], 0)
                ld.issue(sy, brow[64:65, 0:8],
                         w["L1_ffn_expert_bias"][None, :], 0)
                ld.issue(sy, obd[:, 0:128], st["obd_c"][:, :], 0)
                tok = ld.issue(sy, brow[32:33, 2:1026],
                               w["L1_ffn_norm2_w"][None, :], 0)
                ld.wait(sy, tok)
                _emit_cache_stores(st, sy, jobs)

            @b.tensor
            def _(tn):
                ld.wait(tn, (0, b_ld0[0] + 21 * 16))
                wl.wait(tn, tok_wo[0])
                i = None
                for c in range(HP):
                    i = tn.matmul(pp[:, c:c + 1],
                                  brow[0:1, c * 128:(c + 1) * 128],
                                  ones_f[0:1, 0:1], start=True, stop=True)
                    i = tn.matmul(pp[:, 8 + c:9 + c],
                                  brow[32:33, 2 + c * 128:2 + (c + 1) * 128],
                                  ones_f[32:33, 0:1], start=True, stop=True)
                tn.matmul(pp[:, 512:514], ones_f[32:33, :], brow[32:33, 0:2],
                          start=True, stop=True)
                i = tn.matmul(pp[:, 514:522], ones_f[64:65, :], brow[64:65, 0:8],
                              start=True, stop=True)
                i.then_inc(mm.h, 1)
                mm.v += 1

            @b.vector
            def _(v):
                vchp = ChainCtx(st["vch"])
                vchp.i(v.memset(osum[:, :], 0.0))
                vchp.w(v)
                v.memset(osum[0:64, 0:1], 1.0)
                v.memset(osum[64:128, 1:2], 1.0)
                v.memset(rrow[:, :], 0.0)
                v.memset(rbsb[:, :], 0.0)
                v.wait_ge(mm.h, b_mm0 + 1)
                v.tensor_copy(nw_sb[:, 0:16], pp[:, 0:16])
                v.tensor_copy(gb_sb[:, 0:2], pp[:, 512:514])
                v.tensor_copy(gb_sb[:, 2:10], pp[:, 514:522]).then_inc(ve.h, 1)
                ve.v += 1

        st["p2q"] = {}
        for g in range(n_grp):
            _l1_p2_group(st, g, n_grp, ktw, vw, obd, osum, wo, qtg, ptc, x1sb,
                         xnfc, rrow, rbsb, sqtmp, wgc_sb, gb_sb, gsc,
                         we_sb, nw_sb, n_tok)


def _l1_p2_group(st, g, n_grp, ktw, vw, obd, osum, wo, qtg, ptc, x1sb,
                 xnfc, rrow, rbsb, sqtmp, wgc_sb, gb_sb, gsc, we_sb, nw_sb,
                 n_tok):
    nc = st["nc"]
    ld, wl, stp, mm, sc, ve = (st["ld"], st["wl"], st["stp"], st["mm"],
                               st["sc"], st["ve"])
    wl2 = st["wl2"]
    cs = st["cs"]
    vch = ChainCtx(st["vch"])
    sch = ChainCtx(st["sch"])
    ones_f, eye_f = st["ones_f"], st["eye_f"]
    KB = n_tok // 64
    K2 = KB // 2
    NP = 8
    b_mm, b_sc, b_ve = mm.v, sc.v, ve.v
    b_wl2 = wl2.bases()
    sl = (g % 2) * 4096

    PM = 3 * KB + 1

    def m_sco(p, kb):
        idx = kb if kb < 4 else 3 * kb - 8
        return b_mm + p * PM + idx + 1

    def m_ctx(p, kb):
        idx = 3 * kb + 5 if kb <= KB - 5 else 3 * KB - 8 + (kb - KB + 4) * 2
        return b_mm + p * PM + idx + 1

    def m_sum(p, kb):
        idx = 3 * kb + 6 if kb <= KB - 5 else 3 * KB - 7 + (kb - KB + 4) * 2
        return b_mm + p * PM + idx + 1

    def m_rbm(p):
        return b_mm + (p + 1) * PM

    def m_wo(m):
        return b_mm + NP * PM + m + 1

    def m_s2(m):
        return b_mm + NP * PM + 8 + m + 1

    m_bc = b_mm + NP * PM + 17

    def m_gmm(sub):
        return m_bc + sub + 1

    def s_ep(gsp):
        return b_sc + gsp + 1

    def s_sq(m):
        return b_sc + NP * K2 + m + 1

    s_rstd = b_sc + NP * K2 + 9

    def s_gex(sub):
        return s_rstd + sub + 1

    def v_rcp(p):
        return b_ve + p * 2 + 1

    def v_sat(p):
        return b_ve + p * 2 + 2

    def v_x1p(m):
        return b_ve + NP * 2 + m + 1

    v_sta = b_ve + NP * 2 + 9
    v_x1f = v_sta + 1

    def v_gv(sub):
        return v_x1f + sub + 1

    def v_gf(sub):
        return v_x1f + 4 + sub + 1

    with (
        nc.psum_tensor(uname("pg_ps"), [128, 4096], F32) as ps,
        nc.Block() as b,
    ):
        tok_xn = []
        tok_kv = []

        @b.sync
        def _(sy):
            if g == 0:
                st["p2q"][0] = cs.issue(
                    sy, qtg[:, 0:4096].rearrange("p (c t) -> p c t", t=512),
                    st["qt_hbm"][:, 0:512]
                    .rearrange("(c p) t -> p c t", p=128), 0)
                if n_grp > 1:
                    st["p2q"][1] = cs.issue(
                        sy, qtg[:, 4096:8192]
                        .rearrange("p (c t) -> p c t", t=512),
                        st["qt_hbm"][:, 512:1024]
                        .rearrange("(c p) t -> p c t", p=128), 1)
            elif g + 1 < n_grp:
                nsl = ((g + 1) % 2) * 4096
                st["p2q"][g + 1] = cs.issue(
                    sy, qtg[:, nsl:nsl + 4096]
                    .rearrange("p (c t) -> p c t", t=512),
                    st["qt_hbm"][:, (g + 1) * 512:(g + 2) * 512]
                    .rearrange("(c p) t -> p c t", p=128), g + 1)
            for m in range(HP):
                if m >= 2:
                    sy.wait_ge(ve.h, v_x1p(m - 2))
                tok_xn.append(ld.issue(
                    sy, xnfc[:, (m % 2) * 512:(m % 2) * 512 + 512],
                    st["xnf_hbm"][m * 128:(m + 1) * 128, g * 512:(g + 1) * 512],
                    2 * g + m))

        @b.gpsimd
        def _(gp):
            for p in range(NP):
                par = p % 2
                if p >= 2:
                    gp.wait_ge(mm.h, m_rbm(p - 2))
                wl2.issue(gp, ktw[:, par * KB * 128:(par + 1) * KB * 128]
                          .rearrange("p (c t) -> p c t", t=128),
                          st["kt_bd"][p].rearrange("c p t -> p c t"), p)
                tok_kv.append(wl2.issue(
                    gp, vw[:, par * KB * 128:(par + 1) * KB * 128]
                    .rearrange("p (c t) -> p c t", t=128),
                    st["v_bd"][p].rearrange("c p t -> p c t"), p))
            gp.wait_ge(ve.h, v_x1f)
            stp.issue(gp, st["x1f_hbm"][:, g * 512:(g + 1) * 512]
                      .rearrange("(c p) t -> p c t", p=128),
                      x1sb[:, :].rearrange("p (c t) -> p c t", t=512), 2 * g)
            stp.issue(gp, st["x1b_hbm"][:, g * 512:(g + 1) * 512]
                      .rearrange("(c p) t -> p c t", p=128),
                      x1sb[:, :].rearrange("p (c t) -> p c t", t=512), 2 * g + 1)
            stp.wait_all(gp)

        @b.scalar
        def _(s):
            for p in range(NP):
                for j in range(K2):
                    gsp = p * K2 + j
                    s.wait_ge(mm.h, m_sco(p, 2 * j + 1))
                    if gsp >= 2:
                        pc = (gsp - 2) * 2 + 1
                        s.wait_ge(mm.h, m_sum(pc // KB, pc % KB))
                    s.activation(ptc[:, (gsp % 2) * 1024:(gsp % 2) * 1024 + 1024],
                                 ps[:, (j % 2) * 1024:(j % 2) * 1024 + 1024],
                                 AF.Exp, scale=0.125).then_inc(sc.h, 1)
                    sc.v += 1
            for m in range(HP):
                jp = m % 2
                s.wait_ge(ve.h, v_x1p(m))
                if m >= 2:
                    s.wait_ge(mm.h, m_s2(m - 2))
                s.activation(sqtmp[:, jp * 512:(jp + 1) * 512],
                             x1sb[:, m * 512:(m + 1) * 512],
                             AF.Square).then_inc(sc.h, 1)
                sc.v += 1
            s.wait_ge(ve.h, v_sta)
            sch.i(s.activation(rrow[32:33, 0:512], rrow[0:1, 0:512], AF.Ln))
            sch.w(s)
            s.activation(rrow[32:33, 0:512], rrow[32:33, 0:512],
                         AF.Exp, scale=-0.5).then_inc(sc.h, 1)
            sc.v += 1
            for sub in range(4):
                s.wait_ge(ve.h, v_gv(sub))
                mx = gsc[:, 64 + sub * 8:64 + sub * 8 + 1]
                sch.i(s.activation(gsc[:, sub * 8:sub * 8 + 8],
                                   gsc[:, 32 + sub * 8:32 + sub * 8 + 8],
                                   AF.Exp, bias=mx))
                s.activation(gsc[:, 64 + sub * 8 + 1:64 + sub * 8 + 2],
                             gsc[:, 64 + sub * 8 + 2:64 + sub * 8 + 3],
                             AF.Exp).then_inc(sc.h, 1)
                sc.v += 1

        @b.tensor
        def _(tn):
            ld.wait(tn, tok_xn[0])
            cs.wait(tn, st["p2q"][g])
            for p in range(NP):
                par = p % 2
                kbase = par * KB * 128
                wl2.wait(tn, tok_kv[p])

                def _sco(kb, p=p, kbase=kbase):
                    gsp = p * K2 + kb // 2
                    if gsp >= 2:
                        tn.wait_ge(sc.h, s_ep(gsp - 2))
                    tn.matmul(ps[:, (kb % 4) * 512:(kb % 4) * 512 + 512],
                              ktw[:, kbase + kb * 128:kbase + (kb + 1) * 128],
                              qtg[:, sl + p * 512:sl + (p + 1) * 512],
                              start=True, stop=True).then_inc(mm.h, 1)
                    mm.v += 1

                def _cs(kb, p=p, kbase=kbase):
                    gsp = p * K2 + kb // 2
                    po = (gsp % 2) * 1024 + (kb % 2) * 512
                    if kb == 0 and p >= 1:
                        tn.wait_ge(ve.h, v_sat(p - 1))
                    tn.wait_ge(sc.h, s_ep(gsp))
                    tn.matmul(ps[:, 2048:2560],
                              vw[:, kbase + kb * 128:kbase + (kb + 1) * 128],
                              ptc[:, po:po + 512],
                              start=(kb == 0), stop=(kb == KB - 1)) \
                        .then_inc(mm.h, 1)
                    mm.v += 1
                    if kb == 0 and p >= 1:
                        tn.wait_ge(ve.h, v_rcp(p - 1))
                    tn.matmul(ps[0:2, 2560:3072], osum[:, 0:2],
                              ptc[:, po:po + 512],
                              start=(kb == 0), stop=(kb == KB - 1)) \
                        .then_inc(mm.h, 1)
                    mm.v += 1

                for kb in range(4):
                    _sco(kb)
                for kb in range(4, KB):
                    _sco(kb)
                    _cs(kb - 4)
                for kb in range(KB - 4, KB):
                    _cs(kb)
                tn.wait_ge(ve.h, v_rcp(p))
                tn.matmul(ps[:, 3072:3584], obd[:, :], rbsb[:, 0:512],
                          start=True, stop=True).then_inc(mm.h, 1)
                mm.v += 1
            tn.wait_ge(ve.h, v_sat(NP - 1))
            for m in range(HP):
                off = (m % 2) * 512
                if m >= 2:
                    tn.wait_ge(ve.h, v_x1p(m - 2))
                i = None
                for k in range(HP):
                    i = tn.matmul(ps[:, off:off + 512],
                                  wo[:, k * H + m * 128: k * H + (m + 1) * 128],
                                  qtg[:, sl + k * 512:sl + (k + 1) * 512],
                                  start=(k == 0), stop=(k == HP - 1))
                i.then_inc(mm.h, 1)
                mm.v += 1
            for m in range(HP):
                jp = m % 2
                tn.wait_ge(sc.h, s_sq(m))
                tn.matmul(ps[0:1, 1024:1536], ones_f[:, 0:1],
                          sqtmp[:, jp * 512:(jp + 1) * 512],
                          start=(m == 0), stop=(m == HP - 1)).then_inc(mm.h, 1)
                mm.v += 1
            tn.wait_ge(sc.h, s_rstd)
            tn.matmul(ps[:, 1536:2048], st["bmat32"][:, :], rrow[:, 0:512],
                      start=True, stop=True).then_inc(mm.h, 1)
            mm.v += 1
            tn.wait_ge(ve.h, v_x1f)
            for sub in range(4):
                if sub >= 1:
                    tn.wait_ge(ve.h, v_gv(sub - 1))
                i = None
                for k in range(HP):
                    i = tn.matmul(ps[:, 3072 + sub * 16:3072 + sub * 16 + 10],
                                  x1sb[:, k * 512 + sub * 128:
                                       k * 512 + (sub + 1) * 128],
                                  wgc_sb[:, k * 10:(k + 1) * 10],
                                  start=(k == 0), stop=(k == HP - 1))
                i.then_inc(mm.h, 1)
                mm.v += 1

        @b.vector
        def _(v):
            for p in range(NP):
                v.wait_ge(mm.h, m_sum(p, KB - 1))
                if p >= 1:
                    v.wait_ge(ve.h, v_sat(p - 1))
                v.reciprocal(rbsb[0:2, 0:512],
                             ps[0:2, 2560:3072]).then_inc(ve.h, 1)
                ve.v += 1
                v.wait_ge(mm.h, m_rbm(p))
                vch.i(v.tensor_copy(rbsb[:, 0:512], ps[:, 3072:3584]))
                vch.w(v)
                v.tensor_tensor(qtg[:, sl + p * 512:sl + (p + 1) * 512],
                                ps[:, 2048:2560], rbsb[:, 0:512],
                                ALU.mult).then_inc(ve.h, 1)
                ve.v += 1
            for m in range(HP):
                off = (m % 2) * 512
                v.wait_ge(mm.h, m_wo(m))
                ld.wait(v, tok_xn[m])
                v.tensor_tensor(x1sb[:, m * 512:(m + 1) * 512],
                                ps[:, off:off + 512],
                                xnfc[:, (m % 2) * 512:(m % 2) * 512 + 512],
                                ALU.add).then_inc(ve.h, 1)
                ve.v += 1
            v.wait_ge(mm.h, m_s2(HP - 1))
            v.tensor_scalar(rrow[0:1, 0:512], ps[0:1, 1024:1536],
                            1.0 / H, 1e-6, ALU.mult, ALU.add).then_inc(ve.h, 1)
            ve.v += 1
            v.wait_ge(mm.h, m_bc)
            last = None
            for m in range(HP):
                vch.w(v)
                vch.i(v.tensor_tensor(x1sb[:, m * 512:(m + 1) * 512],
                                      x1sb[:, m * 512:(m + 1) * 512],
                                      ps[:, 1536:2048], ALU.mult))
                vch.w(v)
                last = v.tensor_scalar(x1sb[:, m * 512:(m + 1) * 512],
                                       x1sb[:, m * 512:(m + 1) * 512],
                                       nw_sb[:, m:m + 1], None, ALU.mult)
                vch.i(last)
            vch.w(v)
            v.engine_nop().then_inc(ve.h, 1)
            ve.v += 1
            for sub in range(4):
                gl = gsc[:, 96 + sub * 4:96 + sub * 4 + 2]
                el = gsc[:, 32 + sub * 8:32 + sub * 8 + 8]
                mx = gsc[:, 64 + sub * 8:64 + sub * 8 + 1]
                zz = gsc[:, 64 + sub * 8 + 1:64 + sub * 8 + 2]
                dgl = gsc[:, 64 + sub * 8 + 2:64 + sub * 8 + 3]
                v.wait_ge(mm.h, m_gmm(sub))
                vch.i(v.tensor_tensor(gl, ps[:, 3072 + sub * 16:
                                             3072 + sub * 16 + 2],
                                      gb_sb[:, 0:2], ALU.add))
                vch.i(v.tensor_tensor(el, ps[:, 3072 + sub * 16 + 2:
                                             3072 + sub * 16 + 10],
                                      gb_sb[:, 2:10], ALU.add))
                vch.w(v)
                vch.i(v.reduce_max(mx, el, axis=AX.X, negate=True))
                last = v.tensor_tensor(dgl, gl[:, 1:2], gl[:, 0:1], ALU.subtract)
                vch.i(last)
                vch.w(v)
                v.engine_nop().then_inc(ve.h, 1)
                ve.v += 1
            for sub in range(4):
                exps = gsc[:, sub * 8:sub * 8 + 8]
                gl = gsc[:, 96 + sub * 4:96 + sub * 4 + 2]
                zz = gsc[:, 64 + sub * 8 + 1:64 + sub * 8 + 2]
                s0 = gsc[:, 96 + sub * 4 + 2:96 + sub * 4 + 3]
                s1 = gsc[:, 96 + sub * 4 + 3:96 + sub * 4 + 4]
                mcol = gsc[:, 112 + sub * 4:112 + sub * 4 + 1]
                gp0 = gsc[:, 112 + sub * 4 + 1:112 + sub * 4 + 2]
                q = gsc[:, 112 + sub * 4 + 2:112 + sub * 4 + 3]
                t0c = gsc[:, 112 + sub * 4 + 3:112 + sub * 4 + 4]
                v.wait_ge(sc.h, s_gex(sub))
                vch.w(v)
                vch.i(v.reduce_sum(s0, exps[:, 0:4], axis=AX.X))
                vch.i(v.reduce_sum(s1, exps[:, 4:8], axis=AX.X))
                vch.i(v.tensor_tensor(mcol, gl[:, 0:1], gl[:, 1:2], ALU.is_ge))
                vch.i(v.tensor_scalar(gp0, zz, 1.0, None, ALU.add))
                vch.w(v)
                vch.i(v.reciprocal(gp0, gp0))
                vch.w(v)
                vch.i(v.tensor_scalar(t0c, gp0, 2.0, -1.0, ALU.mult, ALU.add))
                vch.w(v)
                vch.i(v.tensor_scalar(t0c, mcol, t0c, None, ALU.mult))
                vch.w(v)
                vch.i(v.tensor_scalar(gp0, gp0, -1.0, 1.0, ALU.mult, ALU.add))
                vch.w(v)
                vch.i(v.tensor_tensor(gp0, gp0, t0c, ALU.add))
                vch.w(v)
                vch.i(v.tensor_tensor(t0c, s0, s1, ALU.subtract))
                vch.w(v)
                vch.i(v.tensor_scalar(t0c, mcol, t0c, None, ALU.mult))
                vch.w(v)
                vch.i(v.tensor_tensor(t0c, s1, t0c, ALU.add))
                vch.w(v)
                vch.i(v.reciprocal(q, t0c))
                vch.w(v)
                vch.i(v.tensor_tensor(q, gp0, q, ALU.mult))
                vch.w(v)
                vch.i(v.tensor_scalar(s0, mcol, q, None, ALU.mult))
                vch.w(v)
                vch.i(v.tensor_tensor(s1, q, s0, ALU.subtract))
                vch.w(v)
                vch.i(v.tensor_scalar(exps[:, 0:4], exps[:, 0:4], s0, None,
                                      ALU.mult))
                vch.i(v.tensor_scalar(exps[:, 4:8], exps[:, 4:8], s1, None,
                                      ALU.mult))
                vch.w(v)
                m1 = gsc[:, 112 + sub * 4:112 + sub * 4 + 1]
                vch.i(v.reduce_max(m1, exps, axis=AX.X))
                vch.w(v)
                eq1 = gsc[:, 128 + sub * 8:128 + sub * 8 + 8]
                vch.i(v.tensor_scalar(eq1, exps, m1, None, ALU.is_equal))
                vch.w(v)
                p2 = gsc[:, 32 + sub * 8:32 + sub * 8 + 8]
                vch.i(v.scalar_tensor_tensor(p2, eq1, NEG_BIG, exps,
                                             ALU.mult, ALU.add))
                vch.w(v)
                vch.i(v.reduce_max(m1, p2, axis=AX.X))
                vch.w(v)
                vch.i(v.tensor_scalar(p2, p2, m1, None, ALU.is_equal))
                vch.w(v)
                vch.i(v.tensor_tensor(eq1, eq1, p2, ALU.add))
                vch.w(v)
                vch.i(v.tensor_tensor(eq1, exps, eq1, ALU.mult))
                vch.w(v)
                ctile = g * 4 + sub
                last = v.tensor_scalar(we_sb[:, ctile * 8:(ctile + 1) * 8],
                                       eq1, 0.5, None, ALU.mult)
                vch.i(last)
                vch.w(v)
                v.engine_nop().then_inc(ve.h, 1)
                ve.v += 1

    assert mm.v == m_gmm(3), (mm.v, m_gmm(3))
    assert sc.v == s_gex(3)
    assert ve.v == v_gf(3)


def _l1_p3(st, we_sb):
    nc = st["nc"]
    n_tok = st["n_tok"]
    n_half = n_tok // 2
    for half in range(2):
        with (
            nc.sbuf_tensor(uname("p3_st2"), [128, 16 * n_half], BF16) as sT2,
            nc.sbuf_tensor(uname("p3_tmp"), [128, 2 * HP * 512], F32) as tmp,
        ):
            _l1_u_block(st, half, n_half, sT2, tmp, kind="shared")
            _l1_u_block(st, half, n_half, sT2, tmp, kind="h2")
            _l1_routed_block(st, half, n_half, sT2, tmp, we_sb)


def _l1_u_block(st, half, n_half, sT2, tmp, kind):
    nc = st["nc"]
    w = st["w"]
    ld, wl, stp, mm, sc, ve = (st["ld"], st["wl"], st["stp"], st["mm"],
                               st["sc"], st["ve"])
    wl2 = st["wl2"]
    vch = ChainCtx(st["vch"])
    n_q = n_half // 512
    wname = "L1_ffn_W_shared_in" if kind == "shared" else "L1_ffn_W1_shared"
    b_mm, b_sc, b_ve = mm.v, sc.v, ve.v
    b_ld = ld.bases()
    b_wl = wl.bases()
    b_st = stp.bases()

    n_out = 32 if kind == "shared" else 0
    MMQ = 32 + n_out

    def a_v(q, hc, m):
        return b_mm + q * MMQ + hc * 8 + m + 1

    def b_v(q, hc, m):
        return b_mm + q * MMQ + hc * 8 + 4 + m + 1

    def o_v(q, hc2, m):
        return b_mm + q * MMQ + 32 + hc2 * 8 + m + 1

    def sil_v(q, hc, m):
        return b_sc + q * 16 + hc * 4 + m + 1

    VEQ = 16 + (8 if kind == "shared" else 0)

    def prod_v(q, hc, m):
        return b_ve + q * VEQ + hc * 4 + m + 1

    def oev_v(q, m):
        return b_ve + q * VEQ + 16 + m + 1

    def ld_tok(q):
        return (q % 2, b_ld[q % 2] + (q // 2 + 1) * 16)

    with (
        nc.sbuf_tensor(uname("u_x1b"), [128, 2 * HP * 512], BF16) as x1b,
        nc.sbuf_tensor(uname("u_wa"), [128, 2 * HP * 512], BF16) as wab,
        nc.sbuf_tensor(uname("u_wb"), [128, 2 * HP * 512], BF16) as wbb,
        nc.sbuf_tensor(uname("u_wso"), [128, 2 * 4 * 1024], BF16) as wso,
        nc.sbuf_tensor(uname("u_sil"), [128, 4 * 512], BF16) as silb,
        nc.sbuf_tensor(uname("u_sT"), [128, 16 * 512], BF16) as sTl,
        nc.psum_tensor(uname("u_ps"), [128, 4096], F32) as ps,
        nc.Block() as b,
    ):
        jobs = CacheJobs(st)
        slot_ptoks = {}
        oslot_ptoks = {}
        toks_w = []
        toks_o = []

        @b.gpsimd
        def _(gp):
            wbf = st["wbf"]
            for q in range(n_q):
                for hc in range(4):
                    fg = q * 4 + hc
                    if fg >= 2:
                        gp.wait_ge(mm.h, b_v(*divmod(fg - 2, 4), 3))
                        for pt in slot_ptoks.get(fg - 2, ()):
                            gp.wait_ge(st["cs"].c[pt[0]].h, pt[1])
                    par = fg % 2
                    rdy = (mm.h, a_v(q, hc, 0))
                    _, p1 = _wload(
                        st, gp, wl,
                        wab[:, par * 4096:(par + 1) * 4096]
                        .rearrange("p (c t) -> p c t", t=512),
                        (wname, hc, "a"),
                        w[wname][:, hc * 512:(hc + 1) * 512]
                        .rearrange("(c p) t -> p c t", p=128),
                        wbf[wname][:, hc * 512:(hc + 1) * 512]
                        .rearrange("(c p) t -> p c t", p=128), fg, jobs,
                        ready=rdy)
                    tk, p2 = _wload(
                        st, gp, wl,
                        wbb[:, par * 4096:(par + 1) * 4096]
                        .rearrange("p (c t) -> p c t", t=512),
                        (wname, hc, "b"),
                        w[wname][:, 2048 + hc * 512:2048 + (hc + 1) * 512]
                        .rearrange("(c p) t -> p c t", p=128),
                        wbf[wname][:, 2048 + hc * 512:2048 + (hc + 1) * 512]
                        .rearrange("(c p) t -> p c t", p=128), fg, jobs,
                        ready=rdy)
                    toks_w.append(tk)
                    slot_ptoks[fg] = [p for p in (p1, p2) if p is not None]
                if kind == "shared":
                    for hc2 in range(4):
                        fo = q * 4 + hc2
                        if fo >= 2:
                            gp.wait_ge(mm.h, o_v(*divmod(fo - 2, 4), 7))
                            for pt in oslot_ptoks.get(fo - 2, ()):
                                gp.wait_ge(st["cs"].c[pt[0]].h, pt[1])
                        tk, p3 = _wload(
                            st, gp, wl2,
                            wso[:, (fo % 2) * 4096:(fo % 2 + 1) * 4096]
                            .rearrange("p (c t) -> p c t", t=1024),
                            ("wso", hc2),
                            w["L1_ffn_W_shared_out"]
                            [hc2 * 512:(hc2 + 1) * 512, :]
                            .rearrange("(c p) t -> p c t", p=128),
                            st["wbf"]["L1_ffn_W_shared_out"]
                            [hc2 * 512:(hc2 + 1) * 512, :]
                            .rearrange("(c p) t -> p c t", p=128), fo, jobs,
                            ready=(mm.h, o_v(q, hc2, 0)))
                        toks_o.append(tk)
                        oslot_ptoks[fo] = [p3] if p3 is not None else []
                    gp.wait_ge(ve.h, oev_v(q, HP - 1))
                    stp.issue(gp, st["acc_hbm"][:, half * n_half + q * 512:
                                                half * n_half + (q + 1) * 512]
                              .rearrange("(c p) t -> p c t", p=128),
                              tmp[:, 0:4096]
                              .rearrange("p (c t) -> p c t", t=512), q)
            stp.wait_all(gp)

        @b.sync
        def _(sy):
            emitted = False
            for q in range(n_q):
                if q >= 2:
                    sy.wait_ge(mm.h, b_mm + (q - 1) * MMQ)
                ld.issue(sy, x1b[:, (q % 2) * 4096:(q % 2) * 4096 + 4096]
                         .rearrange("p (c t) -> p c t", t=512),
                         st["x1b_hbm"][:, half * n_half + q * 512:
                                       half * n_half + (q + 1) * 512]
                         .rearrange("(c p) t -> p c t", p=128), q)
                if q == 1:
                    _emit_cache_stores(st, sy, jobs)
                    emitted = True
            if not emitted:
                _emit_cache_stores(st, sy, jobs)

        @b.tensor
        def _(tn):
            for q in range(n_q):
                qpar = q % 2
                ld.wait(tn, ld_tok(q))
                for hc in range(4):
                    fg = q * 4 + hc
                    par = fg % 2
                    wl.wait(tn, toks_w[fg])
                    for m in range(4):
                        if hc >= 1:
                            tn.wait_ge(sc.h, sil_v(q, hc - 1, m))
                        elif q >= 1:
                            if kind == "shared":
                                tn.wait_ge(ve.h, oev_v(q - 1, m))
                            else:
                                tn.wait_ge(sc.h, sil_v(q - 1, 3, m))
                        i = None
                        for k in range(HP):
                            i = tn.matmul(
                                ps[:, m * 512:(m + 1) * 512],
                                wab[:, par * 4096 + k * 512 + m * 128:
                                    par * 4096 + k * 512 + (m + 1) * 128],
                                x1b[:, qpar * 4096 + k * 512:
                                    qpar * 4096 + (k + 1) * 512],
                                start=(k == 0), stop=(k == HP - 1))
                        i.then_inc(mm.h, 1)
                        mm.v += 1
                    for m in range(4):
                        if hc >= 1:
                            tn.wait_ge(ve.h, prod_v(q, hc - 1, m))
                        elif q >= 1:
                            if kind == "shared":
                                tn.wait_ge(ve.h, oev_v(q - 1, 4 + m))
                            else:
                                tn.wait_ge(ve.h, prod_v(q - 1, 3, m))
                        i = None
                        for k in range(HP):
                            i = tn.matmul(
                                ps[:, 2048 + m * 512:2048 + (m + 1) * 512],
                                wbb[:, par * 4096 + k * 512 + m * 128:
                                    par * 4096 + k * 512 + (m + 1) * 128],
                                x1b[:, qpar * 4096 + k * 512:
                                    qpar * 4096 + (k + 1) * 512],
                                start=(k == 0), stop=(k == HP - 1))
                        i.then_inc(mm.h, 1)
                        mm.v += 1
                if kind == "shared":
                    tn.wait_ge(ve.h, prod_v(q, 3, 3))
                    for hc2 in range(4):
                        fo = q * 4 + hc2
                        wl2.wait(tn, toks_o[fo])
                        for m in range(HP):
                            i = None
                            for k4 in range(4):
                                i = tn.matmul(
                                    ps[:, m * 512:(m + 1) * 512],
                                    wso[:, (fo % 2) * 4096 + k4 * 1024 + m * 128:
                                        (fo % 2) * 4096 + k4 * 1024 + (m + 1) * 128],
                                    sTl[:, (hc2 * 4 + k4) * 512:
                                        (hc2 * 4 + k4 + 1) * 512],
                                    start=(hc2 == 0 and k4 == 0),
                                    stop=(hc2 == 3 and k4 == 3))
                            i.then_inc(mm.h, 1)
                            mm.v += 1

        @b.scalar
        def _(s):
            for q in range(n_q):
                for hc in range(4):
                    fg = q * 4 + hc
                    for m in range(4):
                        s.wait_ge(mm.h, a_v(q, hc, m))
                        if fg >= 1:
                            pfg = fg - 1
                            s.wait_ge(ve.h, prod_v(pfg // 4, pfg % 4, m))
                        s.activation(silb[:, m * 512:(m + 1) * 512],
                                     ps[:, m * 512:(m + 1) * 512],
                                     AF.Silu).then_inc(sc.h, 1)
                        sc.v += 1

        @b.vector
        def _(v):
            for q in range(n_q):
                for hc in range(4):
                    for m in range(4):
                        v.wait_ge(mm.h, b_v(q, hc, m))
                        v.wait_ge(sc.h, sil_v(q, hc, m))
                        dst = (sTl[:, (hc * 4 + m) * 512:(hc * 4 + m + 1) * 512]
                               if kind == "shared" else
                               sT2[:, (hc * 4 + m) * n_half + q * 512:
                                   (hc * 4 + m) * n_half + (q + 1) * 512])
                        v.tensor_tensor(dst, silb[:, m * 512:(m + 1) * 512],
                                        ps[:, 2048 + m * 512:
                                           2048 + (m + 1) * 512],
                                        ALU.mult).then_inc(ve.h, 1)
                        ve.v += 1
                if kind == "shared":
                    for m in range(HP):
                        v.wait_ge(mm.h, o_v(q, 3, m))
                        if q >= 2:
                            stp.wait(v, (q % 2, b_st[q % 2] + ((q - 2) // 2 + 1) * 16))
                        v.tensor_copy(tmp[:, m * 512:(m + 1) * 512],
                                      ps[:, m * 512:(m + 1) * 512]) \
                            .then_inc(ve.h, 1)
                        ve.v += 1

    assert mm.v == b_mm + n_q * MMQ, (mm.v, b_mm)
    assert sc.v == b_sc + n_q * 16
    assert ve.v == b_ve + n_q * VEQ


def _l1_routed_block(st, half, n_half, sT2, tmp, we_sb):
    nc = st["nc"]
    w = st["w"]
    ld, wl, stp, mm, sc, ve = (st["ld"], st["wl"], st["stp"], st["mm"],
                               st["sc"], st["ve"])
    vch = ChainCtx(st["vch"])
    ones_f, eye_f = st["ones_f"], st["eye_f"]
    n_q = n_half // 512
    b_mm, b_ve = mm.v, ve.v
    b_st = stp.bases()

    def fe(pp, q, ei):
        return (pp * n_q + q) * 2 + ei

    def m_wr(pp, q, ei):
        return b_mm + fe(pp, q, ei) * 10 + 1

    def m_cb(pp, q, ei):
        return b_mm + fe(pp, q, ei) * 10 + 2

    def m_out(pp, q, ei, m):
        return b_mm + fe(pp, q, ei) * 10 + 2 + m + 1

    VEPQ = 21

    def v_wr(pp, q, ei):
        return b_ve + (pp * n_q + q) * VEPQ + ei * 10 + 1

    def v_cb(pp, q, ei):
        return b_ve + (pp * n_q + q) * VEPQ + ei * 10 + 2

    def v_m(pp, q, ei, m):
        return b_ve + (pp * n_q + q) * VEPQ + ei * 10 + 2 + m + 1

    def v_fin(pp, q):
        return b_ve + (pp * n_q + q + 1) * VEPQ

    def st_tok(pp, q):
        idx = pp * n_q + q
        return (idx % 2, b_st[idx % 2] + (idx // 2 + 1) * 16)

    def _prev(f):
        pq, ei = divmod(f, 2)
        pp, q = divmod(pq, n_q)
        return pp, q, ei

    with (
        nc.sbuf_tensor(uname("r_w2"), [128, 2 * 16 * 1024], BF16) as w2b,
        nc.sbuf_tensor(uname("r_wr"), [128, 512], F32) as wrow,
        nc.sbuf_tensor(uname("r_cb"), [128, 512], F32) as cbsb,
        nc.sbuf_tensor(uname("r_scr"), [128, HP * 512], F32) as scr,
        nc.psum_tensor(uname("r_ps"), [128, 4096], F32) as ps,
        nc.Block() as b,
    ):
        toks_w2 = []
        jobs = CacheJobs(st)
        slot_ptoks = {}

        @b.gpsimd
        def _(gp):
            for pp in range(4):
                ptl = []
                for ei in range(2):
                    e = 2 * pp + ei
                    if pp >= 1:
                        gp.wait_ge(mm.h, m_out(pp - 1, n_q - 1, ei, 7))
                        for pt in slot_ptoks.get((pp - 1, ei), ()):
                            gp.wait_ge(st["cs"].c[pt[0]].h, pt[1])
                    tk, p1 = _wload(
                        st, gp, wl,
                        w2b[:, ei * 16384:(ei + 1) * 16384]
                        .rearrange("p (c t) -> p c t", t=1024),
                        ("w2e", e),
                        w["L1_ffn_W2_experts"][e, :, :]
                        .rearrange("(c p) t -> p c t", p=128),
                        st["wbf"]["L1_ffn_W2_experts"][e, :, :]
                        .rearrange("(c p) t -> p c t", p=128), pp, jobs,
                        ready=(mm.h, m_out(pp, 0, ei, 0)))
                    toks_w2.append(tk)
                    slot_ptoks[(pp, ei)] = [p1] if p1 is not None else []
                for q in range(n_q):
                    gp.wait_ge(ve.h, v_fin(pp, q))
                    stp.issue(gp, st["acc_hbm"][:, half * n_half + q * 512:
                                                half * n_half + (q + 1) * 512]
                              .rearrange("(c p) t -> p c t", p=128),
                              tmp[:, ((pp * n_q + q) % 2) * 4096:
                                  ((pp * n_q + q) % 2) * 4096 + 4096]
                              .rearrange("p (c t) -> p c t", t=512),
                              pp * n_q + q, accum=True)
            stp.wait_all(gp)

        if jobs.items:
            @b.sync
            def _(sy):
                _emit_cache_stores(st, sy, jobs)

        @b.tensor
        def _(tn):
            for pp in range(4):
                wl.wait(tn, toks_w2[2 * pp + 1])
                for q in range(n_q):
                    for ei in range(2):
                        f = fe(pp, q, ei)
                        if f >= 1:
                            tn.wait_ge(ve.h, v_m(*_prev(f - 1), 0))
                        i = None
                        for t4 in range(4):
                            gt = (half * n_half + q * 512) // 128 + t4
                            ecol = 2 * pp + ei
                            i = tn.matmul(ps[0:1, t4 * 128:(t4 + 1) * 128],
                                          we_sb[:, gt * 8 + ecol:
                                                gt * 8 + ecol + 1],
                                          eye_f[:, :], start=True, stop=True)
                        i.then_inc(mm.h, 1)
                        mm.v += 1
                        tn.wait_ge(ve.h, v_wr(pp, q, ei))
                        if f >= 1:
                            tn.wait_ge(ve.h, v_m(*_prev(f - 1), 1))
                        tn.matmul(ps[:, 512:1024], st["bmat0"][:, :],
                                  wrow[:, 0:512], start=True,
                                  stop=True).then_inc(mm.h, 1)
                        mm.v += 1
                        tn.wait_ge(ve.h, v_cb(pp, q, ei))
                        for m in range(HP):
                            if f >= 1 and m >= 2:
                                tn.wait_ge(ve.h, v_m(*_prev(f - 1), m))
                            i = None
                            for k in range(16):
                                i = tn.matmul(
                                    ps[:, m * 512:(m + 1) * 512],
                                    w2b[:, ei * 16384 + k * 1024 + m * 128:
                                        ei * 16384 + k * 1024 + (m + 1) * 128],
                                    sT2[:, k * n_half + q * 512:
                                        k * n_half + (q + 1) * 512],
                                    start=(k == 0), stop=(k == 15))
                            i.then_inc(mm.h, 1)
                            mm.v += 1

        @b.vector
        def _(v):
            vch.i(v.memset(wrow[:, :], 0.0))
            for pp in range(4):
                for q in range(n_q):
                    tsl = ((pp * n_q + q) % 2) * 4096
                    for ei in range(2):
                        v.wait_ge(mm.h, m_wr(pp, q, ei))
                        vch.w(v)
                        v.tensor_copy(wrow[0:1, 0:512],
                                      ps[0:1, 0:512]).then_inc(ve.h, 1)
                        ve.v += 1
                        v.wait_ge(mm.h, m_cb(pp, q, ei))
                        vch.i(v.tensor_copy(cbsb[:, :], ps[:, 512:1024]))
                        vch.w(v)
                        v.engine_nop().then_inc(ve.h, 1)
                        ve.v += 1
                        if ei == 0:
                            if pp >= 1:
                                stp.wait(v, st_tok(pp - 1, q))
                            for m in range(HP):
                                v.wait_ge(mm.h, m_out(pp, q, 0, m))
                                vch.w(v)
                                v.tensor_tensor(
                                    tmp[:, tsl + m * 512:tsl + (m + 1) * 512],
                                    ps[:, m * 512:(m + 1) * 512],
                                    cbsb[:, :], ALU.mult).then_inc(ve.h, 1)
                                ve.v += 1
                        else:
                            for m in range(HP):
                                v.wait_ge(mm.h, m_out(pp, q, 1, m))
                                vch.w(v)
                                vch.i(v.tensor_tensor(
                                    scr[:, m * 512:(m + 1) * 512],
                                    ps[:, m * 512:(m + 1) * 512],
                                    cbsb[:, :], ALU.mult))
                                vch.w(v)
                                v.engine_nop().then_inc(ve.h, 1)
                                ve.v += 1
                            v.wait_ge(ve.h, v_m(pp, q, 0, HP - 1))
                            for m in range(HP):
                                vch.w(v)
                                vch.i(v.tensor_tensor(
                                    tmp[:, tsl + m * 512:tsl + (m + 1) * 512],
                                    tmp[:, tsl + m * 512:tsl + (m + 1) * 512],
                                    scr[:, m * 512:(m + 1) * 512],
                                    ALU.add))
                            vch.w(v)
                            v.engine_nop().then_inc(ve.h, 1)
                            ve.v += 1

    assert mm.v == b_mm + 8 * n_q * 10
    assert ve.v == b_ve + 4 * n_q * VEPQ, (ve.v, b_ve)


def _l1_p4(st, nw_sb):
    nc = st["nc"]
    w = st["w"]
    ld, stp, mm, sc, ve = st["ld"], st["stp"], st["mm"], st["sc"], st["ve"]
    vch = ChainCtx(st["vch"])
    sch = ChainCtx(st["sch"])
    ones_f = st["ones_f"]
    n_tok = st["n_tok"]
    n_chunk = n_tok // 512
    b_mm, b_sc, b_ve = mm.v, sc.v, ve.v
    b_ld = ld.bases()
    b_st = stp.bases()

    def s2_v(c, j):
        return b_mm + c * 9 + j + 1

    def bc_v(c):
        return b_mm + c * 9 + 9

    def sq_v(c, j):
        return b_sc + c * 9 + j + 1

    def rstd_v(c):
        return b_sc + c * 9 + 9

    def sum_v(c):
        return b_ve + c * 3 + 1

    def sta_v(c):
        return b_ve + c * 3 + 2

    def fin_v(c):
        return b_ve + c * 3 + 3

    def ld_tok(c):
        return (c % 2, b_ld[c % 2] + (c // 2 + 1) * 48)

    def st_tok(c):
        return (c % 2, b_st[c % 2] + (c // 2 + 1) * 16)

    with (
        nc.sbuf_tensor(uname("p4_x1"), [128, HP * 512], F32) as x1c,
        nc.sbuf_tensor(uname("p4_ac"), [128, HP * 512], F32) as acc,
        nc.sbuf_tensor(uname("p4_xr"), [128, HP * 512], F32) as xrc,
        nc.sbuf_tensor(uname("p4_sq"), [128, 1024], F32) as sqtmp,
        nc.sbuf_tensor(uname("p4_sr"), [128, 1024], F32) as srow,
        nc.psum_tensor(uname("p4_ps"), [128, 1024], F32) as ps,
        nc.Block() as b,
    ):
        @b.sync
        def _(sy):
            for c in range(n_chunk):
                t0 = c * 512
                if c >= 1:
                    sy.wait_ge(ve.h, fin_v(c - 1))
                    stp.wait(sy, st_tok(c - 1))
                ld.issue(sy, x1c[:, :].rearrange("p (c t) -> p c t", t=512),
                         st["x1f_hbm"][:, t0:t0 + 512]
                         .rearrange("(c p) t -> p c t", p=128), c)
                ld.issue(sy, acc[:, :].rearrange("p (c t) -> p c t", t=512),
                         st["acc_hbm"][:, t0:t0 + 512]
                         .rearrange("(c p) t -> p c t", p=128), c)
                ld.issue(sy, xrc[:, :].rearrange("p (c t) -> p c t", t=512),
                         st["xres"][:, t0:t0 + 512]
                         .rearrange("(c p) t -> p c t", p=128), c)

        @b.scalar
        def _(s):
            for c in range(n_chunk):
                s.wait_ge(ve.h, sum_v(c))
                for j in range(HP):
                    jp = j % 2
                    if j >= 2:
                        s.wait_ge(mm.h, s2_v(c, j - 2))
                    s.activation(sqtmp[:, jp * 512:(jp + 1) * 512],
                                 x1c[:, j * 512:(j + 1) * 512],
                                 AF.Square).then_inc(sc.h, 1)
                    sc.v += 1
                s.wait_ge(ve.h, sta_v(c))
                sch.i(s.activation(srow[0:1, 512:1024], srow[0:1, 0:512],
                                   AF.Ln))
                sch.w(s)
                s.activation(srow[0:1, 512:1024], srow[0:1, 512:1024],
                             AF.Exp, scale=-0.5).then_inc(sc.h, 1)
                sc.v += 1

        @b.tensor
        def _(tn):
            for c in range(n_chunk):
                if c >= 1:
                    tn.wait_ge(ve.h, sta_v(c - 1))
                for j in range(HP):
                    jp = j % 2
                    tn.wait_ge(sc.h, sq_v(c, j))
                    tn.matmul(ps[0:1, 0:512], ones_f[:, 0:1],
                              sqtmp[:, jp * 512:(jp + 1) * 512],
                              start=(j == 0), stop=(j == HP - 1)).then_inc(mm.h, 1)
                    mm.v += 1
                tn.wait_ge(sc.h, rstd_v(c))
                if c >= 1:
                    tn.wait_ge(ve.h, fin_v(c - 1))
                tn.matmul(ps[:, 512:1024], st["bmat0"][:, :],
                          srow[:, 512:1024],
                          start=True, stop=True).then_inc(mm.h, 1)
                mm.v += 1

        @b.vector
        def _(v):
            vch.i(v.memset(srow[:, :], 0.0))
            for c in range(n_chunk):
                ld.wait(v, ld_tok(c))
                last = None
                for j in range(HP):
                    last = v.tensor_tensor(x1c[:, j * 512:(j + 1) * 512],
                                           x1c[:, j * 512:(j + 1) * 512],
                                           acc[:, j * 512:(j + 1) * 512],
                                           ALU.add)
                    vch.i(last)
                vch.w(v)
                v.engine_nop().then_inc(ve.h, 1)
                ve.v += 1
                v.wait_ge(mm.h, s2_v(c, HP - 1))
                vch.w(v)
                v.tensor_scalar(srow[0:1, 0:512], ps[0:1, 0:512],
                                1.0 / H, 1e-6, ALU.mult, ALU.add).then_inc(ve.h, 1)
                ve.v += 1
                v.wait_ge(mm.h, bc_v(c))
                if c >= 2:
                    stp.wait(v, st_tok(c - 2))
                last = None
                for j in range(HP):
                    vch.w(v)
                    vch.i(v.tensor_tensor(x1c[:, j * 512:(j + 1) * 512],
                                          x1c[:, j * 512:(j + 1) * 512],
                                          ps[:, 512:1024], ALU.mult))
                    vch.w(v)
                    vch.i(v.tensor_scalar(x1c[:, j * 512:(j + 1) * 512],
                                          x1c[:, j * 512:(j + 1) * 512],
                                          nw_sb[:, 8 + j:9 + j], None, ALU.mult))
                    vch.w(v)
                    last = v.scalar_tensor_tensor(
                        xrc[:, j * 512:(j + 1) * 512],
                        x1c[:, j * 512:(j + 1) * 512], 0.5,
                        xrc[:, j * 512:(j + 1) * 512], ALU.mult, ALU.add)
                    vch.i(last)
                vch.w(v)
                v.engine_nop().then_inc(ve.h, 1)
                ve.v += 1

        @b.gpsimd
        def _(gp):
            for c in range(n_chunk):
                t0 = c * 512
                gp.wait_ge(ve.h, fin_v(c))
                stp.issue(gp, st["xres"][:, t0:t0 + 512]
                          .rearrange("(c p) t -> p c t", p=128),
                          xrc[:, :].rearrange("p (c t) -> p c t", t=512), c)
            stp.wait_all(gp)

    assert mm.v == b_mm + n_chunk * 9
    assert sc.v == b_sc + n_chunk * 9
    assert ve.v == b_ve + n_chunk * 3




def _host_consts():
    import ml_dtypes
    eye = np.eye(128, dtype=np.float32)
    dh = 64
    inv_freq = (1.0 / 10000.0 ** (np.arange(0, dh, 2, dtype=np.float64) / dh))
    ang = np.arange(T, dtype=np.float64)[:, None] * inv_freq[None, :]
    sin_rep = np.tile(np.sin(ang), (1, 16)).astype(np.float32)
    cos_rep = np.tile(np.cos(ang), (1, 16)).astype(np.float32)
    obd = np.zeros((128, 128), dtype=np.float32)
    obd[0, 0:64] = 1.0
    obd[1, 64:128] = 1.0
    return {
        "eye_bf": eye.astype(ml_dtypes.bfloat16).view(np.uint16),
        "eye_f": eye,
        "rope_sin": sin_rep,
        "rope_cos": cos_rep,
        "obd_c": obd,
    }


ALL_PARTS = ("l0ffn_a", "l0attn", "l0ffn_b", "l1ffn_a", "l1attn", "l1ffn_b")


def run_parts(parts, inputs, n_cores=8):
    consts = _host_consts()
    x_full = np.ascontiguousarray(np.asarray(inputs["L0_attn_x"], dtype=np.float32))
    in_maps = []
    for c in range(n_cores):
        m = {"x": x_full[c]}
        for k, v in inputs.items():
            if k == "L0_attn_x":
                continue
            m[k] = np.ascontiguousarray(np.asarray(v, dtype=np.float32))
        m.update(consts)
        in_maps.append(m)
    nc = build_nc(parts)
    res = run_bass_kernel_spmd(nc, in_maps, core_ids=list(range(n_cores)))
    return np.stack([np.asarray(r["out"]) for r in res.results], axis=0)


def kernel(**inputs):
    return run_parts(ALL_PARTS, inputs, n_cores=8)
